# version 14
# baseline (speedup 1.0000x reference)
import numpy as np
from contextlib import ExitStack

import concourse.bass as bass
import concourse.mybir as mybir
from concourse.bass_utils import run_bass_kernel_spmd

F32 = mybir.dt.float32
I32 = mybir.dt.int32
AF = mybir.ActivationFunctionType
ALU = mybir.AluOpType
AX = mybir.AxisListType

NCORES = 8
BB, LSEQ, DM = 2, 4096, 2048
TPC = 1024
NFT = DM // 128
NQT = TPC // 128
NBLK = 8
HKV, HD, NH, SCH, NCH, RDR = 4, 128, 16, 32, 128, 256
INVSQ_HD = 1.0 / np.sqrt(128.0)
INVSQ_RD = 1.0 / 16.0
EXPBIAS0 = -12.0
NEXP, RGD, IDIM, SIDIM = 32, 128, 1408, 2816
NIT = IDIM // 128
NST = SIDIM // 128
NOPE, ROPE, VD, QHD = 128, 64, 128, 192
QLORA, KVLORA = 1536, 512
SCALE1 = float(QHD) ** -0.5
DFF = 5632
NDT = DFF // 128
EPS0, EPS1 = 1e-5, 1e-6
NEG = -1.0e30
DBG_NORM = []


class KB:
    def __init__(self, nc, ctx):
        self.nc = nc
        self.ctx = ctx
        self.T, self.V, self.S, self.G, self.Y = (
            nc.tensor, nc.vector, nc.scalar, nc.gpsimd, nc.sync)
        self.engs = {'T': self.T, 'V': self.V, 'S': self.S,
                     'G': self.G, 'Y': self.Y}
        self.sems = {}
        self.cnt = {}
        self.lastw = {}
        for s in ('T', 'V', 'S', 'G', 'DY', 'DG', 'CC'):
            self.sems[s] = ctx.enter_context(nc.semaphore("sem_" + s))
            self.cnt[s] = 0
        self.psum = [ctx.enter_context(nc.psum_tensor(f"pb{i}", [128, 512], F32))
                     for i in range(8)]
        self.pfree = [None] * 8

    def sb(self, name, shape, scope=None, dtype=F32):
        return (scope or self.ctx).enter_context(
            self.nc.sbuf_tensor(name, shape, dtype))

    def barrier(self):
        for ek in ('T', 'V', 'S', 'G', 'Y'):
            for s in ('T', 'V', 'S', 'G', 'DY', 'DG'):
                self.wait(ek, (s, self.cnt[s]))

    def vsw(self, inst):
        t = self.mark(inst, 'V')
        self.wait('V', t)
        return t

    def ssw(self, inst):
        t = self.mark(inst, 'S')
        self.wait('S', t)
        return t

    def mark(self, inst, s, step=None):
        if step is None:
            step = 16 if s.startswith('D') else 1
        self.cnt[s] += step
        inst.then_inc(self.sems[s], step)
        return (s, self.cnt[s])

    def tok(self, s):
        return (s, self.cnt[s])

    def wait(self, ek, tok):
        if tok is None:
            return
        s, v = tok
        if v <= 0:
            return
        key = (ek, s)
        if self.lastw.get(key, 0) >= v:
            return
        self.engs[ek].wait_ge(self.sems[s], v)
        self.lastw[key] = v

    def waits(self, ek, toks):
        for t in toks:
            self.wait(ek, t)

    def bank(self, i):
        return self.psum[i]

    def bank_wait(self, i):
        self.wait('T', self.pfree[i])

    def bank_set_free(self, i, tok):
        self.pfree[i] = tok


def build_program(debug_taps=()):
    nc = bass.Bass("TRN2", target_bir_lowering=False)
    ctx = ExitStack()

    d = {}

    def din(name, shape, dtype=F32):
        d[name] = nc.dram_tensor(name, list(shape), dtype, kind="ExternalInput").ap()
        return d[name]

    def dout(name, shape, dtype=F32):
        d[name] = nc.dram_tensor(name, list(shape), dtype, kind="ExternalOutput").ap()
        return d[name]

    def dint(name, shape, dtype=F32):
        d[name] = nc.dram_tensor(name, list(shape), dtype).ap()
        return d[name]

    din("x0T", (DM, TPC))
    din("wkT", (DM, 512)); din("wvT", (DM, 512))
    din("wlmkT", (DM, RDR)); din("wqrT", (DM, RDR))
    din("wqT", (DM, DM)); din("woT", (DM, DM))
    din("g_pre", (DM,)); din("g_chunk", (DM,)); din("g_pre_retr", (DM,))
    din("g_q_retr", (RDR,)); din("g_lmk", (RDR,))
    din("g_k", (HD,)); din("g_qn", (HD,))
    din("gateT", (DM, NEXP))
    din("RgT", (DM, RGD)); din("RuT", (DM, RGD)); din("RdT", (IDIM, RGD))
    din("UgT", (NEXP, RGD, RGD)); din("UuT", (NEXP, RGD, RGD)); din("UdT", (NEXP, RGD, RGD))
    din("CgT", (NEXP, RGD, IDIM)); din("CuT", (NEXP, RGD, IDIM)); din("CdT", (NEXP, RGD, DM))
    din("sRgT", (DM, RGD)); din("sRuT", (DM, RGD))
    din("sUgT", (RGD, RGD)); din("sUuT", (RGD, RGD)); din("sUdT", (RGD, RGD))
    din("sCgT", (RGD, SIDIM)); din("sCuT", (RGD, SIDIM))
    din("sRdT", (SIDIM, RGD)); din("sCdT", (RGD, DM))
    din("qaT", (DM, QLORA)); din("qbT", (QLORA, NH * QHD))
    din("kvaT", (DM, KVLORA + ROPE)); din("kvbT", (KVLORA, NH * (NOPE + VD)))
    din("owT", (DM, DM))
    din("qa_ln", (QLORA,)); din("kva_ln", (KVLORA,))
    din("w1", (2 * DFF, DM)); din("w2", (DM, DFF))
    din("ident", (128, 128))
    din("cosT", (ROPE, TPC)); din("sinT", (ROPE, TPC))
    din("l0mask", (NQT, 128, NCH))
    din("l1mask", (NQT, NBLK, 128, 512))
    dout("outT", (DM, TPC))
    dint("ag0k_in", (512, TPC)); dint("ag0k_out", (4, 512, TPC))
    dint("ag0v_in", (TPC, 512)); dint("ag0v_out", (4, TPC, 512))
    dint("ag0l_in", (RDR, 32)); dint("ag0l_out", (4, RDR, 32))
    dint("ag1_in", (KVLORA + ROPE, TPC)); dint("ag1_out", (4, KVLORA + ROPE, TPC))
    dint("qa_spill", (QLORA, TPC))
    dint("xq_spill", (DM, TPC))
    dint("xsum_spill", (DM, TPC))
    dint("qw1", (2 * NDT, DM, 128)); dint("s1", (2 * NDT, 128))
    dint("qw2", (NFT, DFF, 128)); dint("s2", (NFT, 128))

    kb = KB(nc, ctx)
    with ctx:
        with nc.Block() as _blk:
            _body(nc, kb, d, debug_taps, dout)
    return nc


def _body(nc, kb, d, debug_taps, dout):
    T, V, S, G, Y = kb.T, kb.V, kb.S, kb.G, kb.Y

    def tap(name, sbuf_ap, tok):
        if name not in debug_taps:
            return
        o = dout("dbg_" + name, list(sbuf_ap.shape))
        kb.wait('Y', tok)
        t = kb.mark(Y.dma_start(out=o[tuple(slice(None) for _ in sbuf_ap.shape)],
                                in_=sbuf_ap), 'DY')
        kb.wait('V', t)
        kb.wait('S', t)
        kb.wait('T', t)
        kb.wait('G', t)

    xsum = kb.sb("xsum", [128, NFT * TPC])
    rb = kb.sb("rb", [128, TPC])
    mb = kb.sb("mb", [128, TPC])
    stat = kb.sb("stat", [128, 32])
    stat2 = kb.sb("stat2", [128, 32])
    ones_col = kb.sb("ones_col", [128, 1])
    onesall = kb.sb("onesall", [1, 128])
    ident = kb.sb("identb", [128, 128])
    row1 = kb.sb("row1", [1, 1024])
    gain = kb.sb("gain", [128, 64])
    gl1 = kb.sb("gl1", [128, 16])
    sq = [kb.sb(f"sqt{i}", [128, 512]) for i in range(2)]

    cst = kb.sb("cst", [128, 4])
    tident = kb.mark(Y.dma_start(out=ident[:], in_=d["ident"][:, :]), 'DY')
    kb.wait('V', tident)
    V.memset(ones_col[:], 1.0)
    V.memset(cst[:, 0:1], EXPBIAS0)
    tones = kb.mark(V.memset(onesall[:], 1.0), 'V')

    for i in range(NFT):
        kb.mark(Y.dma_start(out=gain[:, i:i + 1],
                    in_=d["g_pre"][i * 128:(i + 1) * 128].rearrange("(p o) -> p o", o=1)), 'DY')
        kb.mark(Y.dma_start(out=gain[:, 16 + i:17 + i],
                    in_=d["g_chunk"][i * 128:(i + 1) * 128].rearrange("(p o) -> p o", o=1)), 'DY')
        kb.mark(Y.dma_start(out=gain[:, 32 + i:33 + i],
                    in_=d["g_pre_retr"][i * 128:(i + 1) * 128].rearrange("(p o) -> p o", o=1)), 'DY')
    for i in range(2):
        kb.mark(Y.dma_start(out=gain[:, 48 + i:49 + i],
                    in_=d["g_q_retr"][i * 128:(i + 1) * 128].rearrange("(p o) -> p o", o=1)), 'DY')
        kb.mark(Y.dma_start(out=gain[:, 50 + i:51 + i],
                    in_=d["g_lmk"][i * 128:(i + 1) * 128].rearrange("(p o) -> p o", o=1)), 'DY')
    kb.mark(Y.dma_start(out=gain[:, 53:54], in_=d["g_k"][:].rearrange("(p o) -> p o", o=1)), 'DY')
    tgn = kb.mark(Y.dma_start(out=gain[:, 54:55],
                              in_=d["g_qn"][:].rearrange("(p o) -> p o", o=1)), 'DY')
    for i in range(12):
        kb.mark(Y.dma_start(out=gl1[:, i:i + 1],
                    in_=d["qa_ln"][i * 128:(i + 1) * 128].rearrange("(p o) -> p o", o=1)), 'DY')
    for i in range(4):
        tgl = kb.mark(Y.dma_start(out=gl1[:, 12 + i:13 + i],
                                  in_=d["kva_ln"][i * 128:(i + 1) * 128].rearrange(
                                      "(p o) -> p o", o=1)), 'DY')
    kb.wait('V', tgn)
    tgc = kb.mark(V.tensor_tensor(gain[:, 52:53], gain[:, 53:54], gain[:, 54:55],
                                  op=ALU.mult), 'V')

    def XT(buf, i):
        return buf[:, i * TPC:(i + 1) * TPC]

    def XTc(buf, i, c0, n, stride=TPC):
        return buf[:, i * stride + c0: i * stride + c0 + n]

    sqfree = [None, None]
    guard = {'row': None, 'stat': None}

    def norm_stats(get_tile, idx, nj, kind, eps, stat_cols=0, src_tok=None):
        sum_bank, sq_bank = 6, 7
        kb.wait('V', guard['stat'])
        kb.wait('S', src_tok)
        kb.wait('T', src_tok)
        for j in range(nj):
            for k, i in enumerate(idx):
                ap = get_tile(i, j)
                sqt = sq[k % 2][:, 0:128]
                kb.wait('S', sqfree[k % 2])
                ts = kb.mark(S.activation(sqt, ap, AF.Square), 'S')
                kb.wait('T', ts)
                if k == 0:
                    kb.bank_wait(sum_bank)
                inst = T.matmul(kb.bank(sum_bank)[:, 0:1], sqt, ones_col[:],
                                start=(k == 0), stop=(k == len(idx) - 1))
                sqfree[k % 2] = kb.mark(inst, 'T')
            tmm = sqfree[(len(idx) - 1) % 2]
            kb.wait('V', tmm)
            tv = kb.mark(V.tensor_copy(stat[:, stat_cols + j:stat_cols + j + 1],
                                       kb.bank(sum_bank)[:, 0:1]), 'V')
            kb.bank_set_free(sum_bank, tv)
            if kind == 'ln':
                for k, i in enumerate(idx):
                    if k == 0:
                        kb.bank_wait(sq_bank)
                    inst = T.matmul(kb.bank(sq_bank)[:, 0:1], get_tile(i, j),
                                    ones_col[:], start=(k == 0), stop=(k == len(idx) - 1))
                tmm = kb.mark(inst, 'T')
                kb.wait('V', tmm)
                tv2 = kb.mark(V.tensor_copy(stat2[:, j:j + 1],
                                            kb.bank(sq_bank)[:, 0:1]), 'V')
                kb.bank_set_free(sq_bank, tv2)
        if DBG_NORM and DBG_NORM[0] == 'pending':
            DBG_NORM[0] = 'done'
            o1 = nc.dram_tensor("dbg_rawsums", [128, 64], F32, kind="ExternalOutput").ap()
            kb.wait('Y', kb.tok('V'))
            tdn = kb.mark(Y.dma_start(out=o1[:, 0:32], in_=stat[:]), 'DY')
            tdn = kb.mark(Y.dma_start(out=o1[:, 32:64], in_=stat2[:]), 'DY')
            kb.wait('V', tdn); kb.wait('T', tdn); kb.wait('S', tdn)
        nfeat = float(len(idx) * 128)
        sc = stat[:, stat_cols:stat_cols + nj]
        tmp = stat[:, 24:24 + nj]
        W = kb.vsw
        kb.wait('V', kb.tok('V'))
        if DBG_NORM and DBG_NORM[0] == 'postdiv':
            DBG_NORM[0] = 'done'
            o3 = nc.dram_tensor("dbg_postdiv", [128, 64], F32, kind="ExternalOutput").ap()
            kb.wait('Y', kb.tok('V'))
            tdn = kb.mark(Y.dma_start(out=o3[:, 0:32], in_=stat[:]), 'DY')
            tdn = kb.mark(Y.dma_start(out=o3[:, 32:64], in_=stat2[:]), 'DY')
            kb.wait('V', tdn); kb.wait('T', tdn); kb.wait('S', tdn)
        if kind == 'ln':
            W(V.tensor_scalar(stat2[:, 0:nj], stat2[:, 0:nj], 1.0 / nfeat, None, op0=ALU.mult))
            W(V.tensor_scalar(sc, sc, 1.0 / nfeat, None, op0=ALU.mult))
            W(V.tensor_tensor(tmp, stat2[:, 0:nj], stat2[:, 0:nj], op=ALU.mult))
            W(V.tensor_tensor(sc, sc, tmp, op=ALU.subtract))
        else:
            W(V.tensor_scalar(sc, sc, 1.0 / nfeat, None, op0=ALU.mult))
        tv = W(V.tensor_scalar(sc, sc, eps, None, op0=ALU.add))
        if DBG_NORM and DBG_NORM[0] == 'postvar':
            DBG_NORM[0] = 'done'
            o4 = nc.dram_tensor("dbg_postvar", [128, 64], F32, kind="ExternalOutput").ap()
            kb.wait('Y', tv)
            tdn = kb.mark(Y.dma_start(out=o4[:, 0:32], in_=stat[:]), 'DY')
            tdn = kb.mark(Y.dma_start(out=o4[:, 32:64], in_=stat2[:]), 'DY')
            kb.wait('V', tdn); kb.wait('T', tdn); kb.wait('S', tdn)
        t2 = stat2[:, 24:24 + nj]
        kb.wait('S', tv)
        t1 = kb.mark(S.activation(tmp, sc, AF.Sqrt), 'S')
        kb.wait('V', t1)
        W(V.reciprocal(t2, tmp))
        W(V.tensor_tensor(tmp, sc, t2, op=ALU.mult))
        W(V.tensor_tensor(tmp, tmp, t2, op=ALU.mult))
        W(V.tensor_scalar(tmp, tmp, -0.5, 1.5, op0=ALU.mult, op1=ALU.add))
        tv = W(V.tensor_tensor(sc, t2, tmp, op=ALU.mult))
        if kind == 'ln':
            tv = W(V.tensor_tensor(stat2[:, 0:nj], stat2[:, 0:nj], sc,
                                   op=ALU.mult))
        if DBG_NORM and DBG_NORM[0] == 'postmath':
            DBG_NORM[0] = 'done'
            o2 = nc.dram_tensor("dbg_postmath", [128, 64], F32, kind="ExternalOutput").ap()
            kb.wait('Y', tv)
            tdn = kb.mark(Y.dma_start(out=o2[:, 0:32], in_=stat[:]), 'DY')
            tdn = kb.mark(Y.dma_start(out=o2[:, 32:64], in_=stat2[:]), 'DY')
            kb.wait('V', tdn); kb.wait('T', tdn); kb.wait('S', tdn)
        return tv

    def bcast_stats(stat_ap, nj, out_bcast, stat_tok):
        kb.wait('T', stat_tok)
        tok = None
        for j0 in range(0, nj, 4):
            jn = min(4, nj - j0)
            kb.bank_wait(5)
            for j in range(j0, j0 + jn):
                kb.bank_wait(4)
                tT = kb.mark(T.transpose(kb.bank(4)[0:1, 0:128],
                                         stat_ap[:, j:j + 1], ident[:]), 'T')
                kb.wait('V', tT)
                kb.wait('V', guard['row'])
                trw = kb.mark(V.tensor_copy(row1[0:1, j * 128:(j + 1) * 128],
                                            kb.bank(4)[0:1, 0:128]), 'V')
                kb.bank_set_free(4, trw)
                kb.wait('T', trw)
                inst = T.matmul(kb.bank(5)[:, (j - j0) * 128:(j - j0 + 1) * 128],
                                onesall[0:1, 0:128],
                                row1[0:1, j * 128:(j + 1) * 128],
                                start=True, stop=True)
            tM = kb.mark(inst, 'T')
            guard['row'] = tM
            kb.wait('V', tM)
            tok = kb.mark(V.tensor_copy(out_bcast[:, j0 * 128:(j0 + jn) * 128],
                                        kb.bank(5)[:, 0:jn * 128]), 'V')
            kb.bank_set_free(5, tok)
        guard['stat'] = kb.tok('T')
        return tok

    def norm_bcast(get_tile, idx, nj, kind, eps, src_tok=None):
        t = norm_stats(get_tile, idx, nj, kind, eps, src_tok=src_tok)
        t1 = bcast_stats(stat[:, 0:nj], nj, rb[:, 0:nj * 128], t)
        t2 = None
        if kind == 'ln':
            t2 = bcast_stats(stat2[:, 0:nj], nj, mb[:, 0:nj * 128], t)
        return t1, t2

    def load_wcols(dst, wdram, nin, c0, w, row0=0):
        tl = None
        for i in range(nin):
            tl = kb.mark(Y.dma_start(
                out=dst[:, i * w:(i + 1) * w],
                in_=wdram[row0 + i * 128: row0 + (i + 1) * 128, c0:c0 + w]), 'DY')
        return tl

    ag0_done = _l0(nc, kb, d, tap, XT, XTc, norm_stats, bcast_stats, norm_bcast,
                   load_wcols, xsum, rb, mb, stat, stat2, ones_col, onesall,
                   ident, row1, gain, sq, tgc, tones, cst, debug_taps)

    _l1(nc, kb, d, tap, XT, XTc, norm_stats, bcast_stats, load_wcols, xsum,
        rb, mb, stat, stat2, ones_col, onesall, ident, gl1, sq, cst)
    Y.wait_ge(kb.sems['DY'], kb.cnt['DY'])


def _l0(nc, kb, d, tap, XT, XTc, norm_stats, bcast_stats, norm_bcast,
        load_wcols, xsum, rb, mb, stat, stat2, ones_col, onesall, ident,
        row1, gain, sq, tgc, tones, cst, debug_taps=()):
    T, V, S, G, Y = kb.T, kb.V, kb.S, kb.G, kb.Y

    l0sc = ExitStack()
    with l0sc:
        we_tm = kb.sb("we_tm", [128, NQT * NEXP], l0sc)
        tweT_tok = [None]
        rg = kb.sb("rg", [128, TPC], l0sc)
        ru = kb.sb("ru", [128, TPC], l0sc)

        xw_sc = ExitStack()
        with xw_sc:
            xwork = kb.sb("xwork", [128, NFT * TPC], xw_sc)
            w_all = kb.sb("w_all", [128, NQT * NCH], xw_sc)
            qstat = kb.sb("qstat", [128, NQT], xw_sc)

            tx0 = None
            for i in range(NFT):
                tx0 = kb.mark(Y.dma_start(out=XT(xsum, i),
                                          in_=d["x0T"][i * 128:(i + 1) * 128, :]), 'DY')
            kb.wait('S', tx0); kb.wait('T', tx0); kb.wait('T', tones)
            tap("x0raw", xsum[:, 0:NFT * TPC], tx0)

            tln = norm_bcast(lambda i, j: XTc(xsum, i, j * 128, 128),
                             list(range(NFT)), NQT, 'ln', EPS0, src_tok=tx0)
            kb.waits('V', [t for t in tln if t])
            tap("rbln", rb[:, 0:TPC], tln[0])
            tap("statln", stat[:, 0:32], tln[0])
            tap("stat2ln", stat2[:, 0:32], tln[0])
            if "STOPLN" in debug_taps:
                return
            tap("mbln", mb[:, 0:TPC], tln[1])
            for i in range(NFT):
                V.tensor_tensor(XT(xwork, i), XT(xsum, i), rb[:], op=ALU.mult)
                V.tensor_tensor(XT(xwork, i), XT(xwork, i), mb[:], op=ALU.subtract)
                tv = kb.mark(V.tensor_tensor(XT(xsum, i), XT(xsum, i), XT(xwork, i),
                                             op=ALU.add), 'V')
            txn0 = tv
            tap("xn0", xwork[:, 0:TPC], txn0)
            tap("xsum1", xsum[:, 0:TPC], txn0)

            p1b = ExitStack()
            with p1b:
                gateW = kb.sb("gateW", [128, NFT * 32], p1b)
                lg = kb.sb("lg", [128, NEXP], p1b)
                swork = kb.sb("swork", [128, NEXP], p1b)
                m8b = kb.sb("m8b", [128, 8], p1b)
                tgw = load_wcols(gateW, d["gateT"], NFT, 0, 32)
                kb.wait('T', txn0)
                kb.wait('T', tgw)
                for j in range(NQT):
                    bnk = 0
                    kb.bank_wait(bnk)
                    for i in range(NFT):
                        inst = T.matmul(kb.bank(bnk)[:, 0:NEXP],
                                        XTc(xwork, i, j * 128, 128),
                                        gateW[:, i * 32:(i + 1) * 32],
                                        start=(i == 0), stop=(i == NFT - 1))
                    tmm = kb.mark(inst, 'T')
                    kb.wait('V', tmm)
                    kb.vsw(V.reduce_max(stat[:, 0:1], kb.bank(bnk)[:, 0:NEXP], axis=AX.X))
                    tvm = kb.vsw(V.tensor_scalar(stat[:, 1:2], stat[:, 0:1], -1.0,
                                                 None, op0=ALU.mult))
                    kb.wait('S', tvm)
                    te = kb.mark(S.activation(lg[:], kb.bank(bnk)[:, 0:NEXP], AF.Exp,
                                              bias=stat[:, 1:2]), 'S')
                    kb.bank_set_free(bnk, te)
                    kb.wait('V', te)
                    kb.vsw(V.reduce_sum(stat[:, 2:3], lg[:], axis=AX.X))
                    kb.vsw(V.reciprocal(stat[:, 4:5], stat[:, 2:3]))
                    kb.vsw(V.tensor_scalar(lg[:], lg[:], stat[:, 4:5], None, op0=ALU.mult))
                    kb.vsw(V.max(out=m8b[:], in_=lg[:]))
                    kb.vsw(V.tensor_scalar(swork[:], lg[:], m8b[:, 3:4], None, op0=ALU.is_ge))
                    kb.vsw(V.tensor_tensor(swork[:], swork[:], lg[:], op=ALU.mult))
                    kb.vsw(V.reduce_sum(stat[:, 3:4], swork[:], axis=AX.X))
                    kb.vsw(V.tensor_scalar(stat[:, 3:4], stat[:, 3:4], 1e-20, None, op0=ALU.add))
                    kb.vsw(V.reciprocal(stat[:, 5:6], stat[:, 3:4]))
                    tv = kb.mark(V.tensor_scalar(we_tm[:, j * 32:(j + 1) * 32],
                                                 swork[:], stat[:, 5:6],
                                                 None, op0=ALU.mult), 'V')
                tweT = tv
                tweT_tok[0] = tweT
                tap("weT", we_tm[:, 0:NQT * 32], tweT)

                rgw = kb.sb("rgw", [128, NFT * 128], p1b)
                ruw = kb.sb("ruw", [128, NFT * 128], p1b)
                load_wcols(rgw, d["RgT"], NFT, 0, 128)
                twl = load_wcols(ruw, d["RuT"], NFT, 0, 128)
                kb.wait('T', twl)
                for qc in range(2):
                    for (dst, wsb, bnk) in ((rg, rgw, 0), (ru, ruw, 1)):
                        kb.bank_wait(bnk)
                        for i in range(NFT):
                            inst = T.matmul(kb.bank(bnk)[:, 0:512],
                                            wsb[:, i * 128:(i + 1) * 128],
                                            XTc(xwork, i, qc * 512, 512),
                                            start=(i == 0), stop=(i == NFT - 1))
                        tmm = kb.mark(inst, 'T')
                        kb.wait('V', tmm)
                        tv = kb.mark(V.tensor_copy(dst[:, qc * 512:(qc + 1) * 512],
                                                   kb.bank(bnk)[:, 0:512]), 'V')
                        kb.bank_set_free(bnk, tv)
                trgru = tv
                tap("rg", rg[:], trgru)

                sml = kb.sb("sml", [128, 3 * 128], p1b)
                kb.mark(Y.dma_start(out=sml[:, 0:128], in_=d["sUgT"][:, :]), 'DY')
                kb.mark(Y.dma_start(out=sml[:, 128:256], in_=d["sUuT"][:, :]), 'DY')
                tsm = kb.mark(Y.dma_start(out=sml[:, 256:384], in_=d["sUdT"][:, :]), 'DY')
                g2 = kb.sb("g2s", [128, 512], p1b)
                u2 = kb.sb("u2s", [128, 512], p1b)
                got = kb.sb("gots", [128, 512], p1b)
                uot = kb.sb("uots", [128, 512], p1b)
                rd2 = kb.sb("rd2s", [128, 512], p1b)
                wsl = [kb.sb(f"wsl{i}", [128, 128], p1b) for i in range(4)]
                wslf = [None] * 4
                kb.wait('T', tsm)
                tva = None
                tgot_free = None
                for qc in range(2):
                    for (dst, wd, bnk) in ((g2, "sRgT", 0), (u2, "sRuT", 1)):
                        kb.wait('Y', kb.tok('T'))
                        twl = load_wcols(rgw if dst is g2 else ruw, d[wd], NFT, 0, 128)
                        kb.bank_wait(bnk)
                        kb.wait('T', twl)
                        for i in range(NFT):
                            inst = T.matmul(kb.bank(bnk)[:, 0:512],
                                            (rgw if dst is g2 else ruw)[:, i * 128:(i + 1) * 128],
                                            XTc(xwork, i, qc * 512, 512),
                                            start=(i == 0), stop=(i == NFT - 1))
                        tmm = kb.mark(inst, 'T')
                        kb.wait('S', tmm)
                        tv = kb.mark(S.copy(dst[:], kb.bank(bnk)[:, 0:512]), 'S')
                        kb.bank_set_free(bnk, tv)
                    for (dst, wofs, bnk) in ((g2, 0, 0), (u2, 128, 1)):
                        kb.bank_wait(bnk)
                        kb.wait('T', tv)
                        tmm = kb.mark(T.matmul(kb.bank(bnk)[:, 0:512],
                                               sml[:, wofs:wofs + 128], dst[:],
                                               start=True, stop=True), 'T')
                        kb.wait('S', tmm)
                        tv = kb.mark(S.copy(dst[:], kb.bank(bnk)[:, 0:512]), 'S')
                        kb.bank_set_free(bnk, tv)
                    kb.bank_wait(2)
                    for ot in range(NST):
                        kb.wait('T', tv)
                        kb.wait('Y', wslf[0])
                        t0 = kb.mark(Y.dma_start(out=wsl[0][:],
                                                 in_=d["sCgT"][:, ot * 128:(ot + 1) * 128]), 'DY')
                        kb.wait('Y', wslf[1])
                        t1 = kb.mark(Y.dma_start(out=wsl[1][:],
                                                 in_=d["sCuT"][:, ot * 128:(ot + 1) * 128]), 'DY')
                        kb.wait('Y', wslf[2])
                        t2 = kb.mark(Y.dma_start(out=wsl[2][:],
                                                 in_=d["sRdT"][ot * 128:(ot + 1) * 128, :]), 'DY')
                        bg, bu = 0, 1
                        kb.bank_wait(bg)
                        kb.wait('T', t0)
                        tg = kb.mark(T.matmul(kb.bank(bg)[:, 0:512], wsl[0][:], g2[:],
                                              start=True, stop=True), 'T')
                        wslf[0] = tg
                        kb.bank_wait(bu)
                        kb.wait('T', t1)
                        tu = kb.mark(T.matmul(kb.bank(bu)[:, 0:512], wsl[1][:], u2[:],
                                              start=True, stop=True), 'T')
                        wslf[1] = tu
                        kb.wait('S', tg)
                        kb.wait('S', tgot_free)
                        tsg = kb.mark(S.activation(got[:], kb.bank(bg)[:, 0:512], AF.Silu), 'S')
                        kb.bank_set_free(bg, tsg)
                        kb.wait('V', tu)
                        kb.wait('V', tgot_free)
                        tsu = kb.mark(V.tensor_copy(uot[:], kb.bank(bu)[:, 0:512]), 'V')
                        kb.bank_set_free(bu, tsu)
                        kb.wait('V', tsg)
                        tgu = kb.mark(V.tensor_tensor(got[:], got[:], uot[:], op=ALU.mult), 'V')
                        kb.wait('T', tgu)
                        kb.wait('T', t2)
                        inst = T.matmul(kb.bank(2)[:, 0:512], wsl[2][:], got[:],
                                        start=(ot == 0), stop=(ot == NST - 1))
                        tgot_free = kb.mark(inst, 'T')
                        wslf[2] = tgot_free
                    kb.wait('S', tgot_free)
                    tv = kb.mark(S.copy(rd2[:], kb.bank(2)[:, 0:512]), 'S')
                    kb.bank_set_free(2, tv)
                    kb.bank_wait(3)
                    kb.wait('T', tv)
                    tmm = kb.mark(T.matmul(kb.bank(3)[:, 0:512], sml[:, 256:384], rd2[:],
                                           start=True, stop=True), 'T')
                    kb.wait('S', tmm)
                    tv = kb.mark(S.copy(rd2[:], kb.bank(3)[:, 0:512]), 'S')
                    kb.bank_set_free(3, tv)
                    for ot in range(NFT):
                        bnk = 4 + (ot % 2)
                        kb.wait('Y', wslf[3])
                        t3 = kb.mark(Y.dma_start(out=wsl[3][:],
                                                 in_=d["sCdT"][:, ot * 128:(ot + 1) * 128]), 'DY')
                        kb.bank_wait(bnk)
                        kb.wait('T', tv); kb.wait('T', t3)
                        tmm = kb.mark(T.matmul(kb.bank(bnk)[:, 0:512], wsl[3][:], rd2[:],
                                               start=True, stop=True), 'T')
                        wslf[3] = tmm
                        kb.wait('V', tmm)
                        tva = kb.mark(V.tensor_tensor(XTc(xsum, ot, qc * 512, 512),
                                                      XTc(xsum, ot, qc * 512, 512),
                                                      kb.bank(bnk)[:, 0:512], op=ALU.add), 'V')
                        kb.bank_set_free(bnk, tva)
                tap("xsum_sh", xsum[:, 0:TPC], tva)

            tnf = norm_bcast(lambda i, j: XTc(xwork, i, j * 128, 128),
                             list(range(NFT)), NQT, 'rms', EPS0, src_tok=txn0)
            kb.waits('V', [tnf[0]])
            for i in range(NFT):
                V.tensor_tensor(XT(xwork, i), XT(xwork, i), rb[:], op=ALU.mult)
                tv = kb.mark(V.tensor_scalar(XT(xwork, i), XT(xwork, i),
                                             gain[:, i:i + 1], None, op0=ALU.mult), 'V')
            txp = tv
            tap("xp", xwork[:, 0:TPC], txp)

            p1d = ExitStack()
            with p1d:
                CW = 128
                NCC = TPC // CW
                xc = kb.sb("xc", [128, NFT * CW], p1d)
                kown = kb.sb("kown", [128, 4 * TPC], p1d)
                wkv = kb.sb("wkv", [128, NFT * 128], p1d)
                vw = [kb.sb(f"vw{i}", [128, 512], p1d) for i in range(2)]
                vbc = kb.sb("vbc", [128, 512], p1d)
                xcm = kb.sb("xcm", [128, NFT * 32], p1d)
                lmr = kb.sb("lmr", [128, 2 * 32], p1d)
                lmn = kb.sb("lmn", [128, 2 * 32], p1d)
                twk_free = None
                tvd = None
                vwf = [None, None]
                kb.barrier()
                for cc in range(NCC):
                    c0 = cc * CW
                    tnf = norm_stats(lambda i, j: XTc(xwork, i, c0 + j * 128, 128),
                                     list(range(NFT)), CW // 128, 'rms', EPS0,
                                     src_tok=txp)
                    tnb = bcast_stats(stat[:, 0:CW // 128], CW // 128, rb[:, 0:CW], tnf)
                    kb.wait('V', tnb)
                    for i in range(NFT):
                        V.tensor_tensor(XTc(xc, i, 0, CW, CW), XTc(xwork, i, c0, CW),
                                        rb[:, 0:CW], op=ALU.mult)
                        tv = kb.mark(V.tensor_scalar(XTc(xc, i, 0, CW, CW),
                                                     XTc(xc, i, 0, CW, CW),
                                                     gain[:, 16 + i:17 + i], None,
                                                     op0=ALU.mult), 'V')
                    txc = tv
                    if cc == 0:
                        tap("xc", xc[:, 0:CW], txc)
                    kb.wait('V', txc)
                    nch_cc = CW // SCH
                    for i in range(NFT):
                        tv = kb.mark(V.reduce_sum(
                            xcm[:, i * 32 + cc * nch_cc: i * 32 + (cc + 1) * nch_cc].rearrange(
                                "p (c o) -> p c o", o=1),
                            XTc(xc, i, 0, CW, CW).rearrange("p (c s) -> p c s", s=SCH),
                            axis=AX.X), 'V')
                    for ot in range(4):
                        kb.wait('Y', twk_free)
                        twk = load_wcols(wkv, d["wkT"], NFT, ot * 128, 128)
                        bnk = 0
                        kb.bank_wait(bnk)
                        kb.wait('T', twk); kb.wait('T', txc)
                        for i in range(NFT):
                            inst = T.matmul(kb.bank(bnk)[:, 0:CW],
                                            wkv[:, i * 128:(i + 1) * 128],
                                            XTc(xc, i, 0, CW, CW),
                                            start=(i == 0), stop=(i == NFT - 1))
                        tmm = kb.mark(inst, 'T')
                        twk_free = tmm
                        kb.wait('V', tmm)
                        tv = kb.mark(V.tensor_copy(kown[:, ot * TPC + c0: ot * TPC + c0 + CW],
                                                   kb.bank(bnk)[:, 0:CW]), 'V')
                        kb.bank_set_free(bnk, tv)
                    for tt in range(CW // 128):
                        bnk = 1
                        kb.bank_wait(bnk)
                        for i in range(NFT):
                            b2 = i % 2
                            kb.wait('Y', vwf[b2])
                            tw = kb.mark(Y.dma_start(out=vw[b2][:],
                                                     in_=d["wvT"][i * 128:(i + 1) * 128, :]),
                                         'DY')
                            kb.wait('T', tw)
                            inst = T.matmul(kb.bank(bnk)[:, 0:512],
                                            XTc(xc, i, tt * 128, 128, CW),
                                            vw[b2][:],
                                            start=(i == 0), stop=(i == NFT - 1))
                            vwf[b2] = kb.mark(inst, 'T')
                        tmm = vwf[(NFT - 1) % 2]
                        kb.wait('V', tmm)
                        kb.wait('V', tvd)
                        tv = kb.mark(V.tensor_copy(vbc[:], kb.bank(bnk)[:, 0:512]), 'V')
                        kb.bank_set_free(bnk, tv)
                        kb.wait('Y', tv)
                        tvd = kb.mark(Y.dma_start(
                            out=d["ag0v_in"][c0 + tt * 128: c0 + (tt + 1) * 128, :],
                            in_=vbc[:]), 'DY')
                tv = kb.vsw(V.tensor_scalar(xcm[:], xcm[:], 1.0 / SCH, None,
                                            op0=ALU.mult))
                txcm = tv
                tkd = None
                for ot in range(4):
                    tnf = norm_stats(lambda i, j: XTc(kown, ot, j * 128, 128),
                                     [ot], NQT, 'rms', EPS0, src_tok=tv)
                    tnb = bcast_stats(stat[:, 0:NQT], NQT, rb[:, 0:TPC], tnf)
                    kb.wait('V', tnb)
                    V.tensor_tensor(XT(kown, ot), XT(kown, ot), rb[:], op=ALU.mult)
                    kb.wait('V', tgc)
                    tv = kb.mark(V.tensor_scalar(XT(kown, ot), XT(kown, ot),
                                                 gain[:, 52:53], None, op0=ALU.mult), 'V')
                    kb.wait('Y', tv)
                    tkd = kb.mark(Y.dma_start(out=d["ag0k_in"][ot * 128:(ot + 1) * 128, :],
                                              in_=XT(kown, ot)), 'DY')
                tap("kown", kown[:, 0:TPC], tv)
                kb.wait('T', txcm)
                for ot in range(2):
                    kb.wait('Y', twk_free)
                    tlm = load_wcols(wkv, d["wlmkT"], NFT, ot * 128, 128)
                    kb.wait('T', tlm)
                    bnk = 0
                    kb.bank_wait(bnk)
                    for i in range(NFT):
                        inst = T.matmul(kb.bank(bnk)[:, 0:32],
                                        wkv[:, i * 128:(i + 1) * 128],
                                        xcm[:, i * 32:(i + 1) * 32],
                                        start=(i == 0), stop=(i == NFT - 1))
                    tmm = kb.mark(inst, 'T')
                    twk_free = tmm
                    kb.wait('V', tmm)
                    tv = kb.mark(V.tensor_copy(lmr[:, ot * 32:(ot + 1) * 32],
                                               kb.bank(bnk)[:, 0:32]), 'V')
                    kb.bank_set_free(bnk, tv)
                kb.barrier()
                kb.wait('S', tv)
                S.activation(sq[0][:, 0:32], lmr[:, 0:32], AF.Square)
                ts2 = kb.mark(S.activation(sq[0][:, 32:64], lmr[:, 32:64], AF.Square), 'S')
                kb.bank_wait(6)
                kb.wait('T', ts2)
                T.matmul(kb.bank(6)[0:1, 0:32], ones_col[:], sq[0][:, 0:32],
                         start=True, stop=False)
                tmm = kb.mark(T.matmul(kb.bank(6)[0:1, 0:32], ones_col[:],
                                       sq[0][:, 32:64], start=False, stop=True), 'T')
                kb.wait('V', tmm)
                kb.vsw(V.tensor_scalar(sq[1][0:1, 0:32], kb.bank(6)[0:1, 0:32], 1.0 / RDR,
                                       EPS0, op0=ALU.mult, op1=ALU.add))
                tv = kb.vsw(V.tensor_copy(sq[1][0:1, 32:64], sq[1][0:1, 0:32]))
                kb.bank_set_free(6, tv)
                kb.wait('S', tv)
                t1 = kb.mark(S.activation(sq[1][0:1, 32:64], sq[1][0:1, 0:32], AF.Sqrt), 'S')
                kb.wait('V', t1)
                kb.vsw(V.reciprocal(sq[1][0:1, 64:96], sq[1][0:1, 32:64]))
                kb.vsw(V.tensor_tensor(sq[1][0:1, 32:64], sq[1][0:1, 0:32], sq[1][0:1, 64:96],
                                       op=ALU.mult))
                kb.vsw(V.tensor_tensor(sq[1][0:1, 32:64], sq[1][0:1, 32:64], sq[1][0:1, 64:96],
                                       op=ALU.mult))
                kb.vsw(V.tensor_scalar(sq[1][0:1, 32:64], sq[1][0:1, 32:64], -0.5, 1.5,
                                       op0=ALU.mult, op1=ALU.add))
                trr = kb.vsw(V.tensor_tensor(sq[1][0:1, 0:32], sq[1][0:1, 64:96],
                                             sq[1][0:1, 32:64], op=ALU.mult))
                kb.barrier()
                kb.bank_wait(5)
                kb.wait('T', trr)
                tbm = kb.mark(T.matmul(kb.bank(5)[:, 0:32], onesall[0:1, 0:128],
                                       sq[1][0:1, 0:32], start=True, stop=True), 'T')
                kb.barrier()
                kb.wait('V', tbm)
                for ot in range(2):
                    kb.vsw(V.tensor_tensor(lmn[:, ot * 32:(ot + 1) * 32],
                                           lmr[:, ot * 32:(ot + 1) * 32],
                                           kb.bank(5)[:, 0:32], op=ALU.mult))
                    tv = kb.vsw(V.tensor_scalar(lmn[:, ot * 32:(ot + 1) * 32],
                                                lmn[:, ot * 32:(ot + 1) * 32],
                                                gain[:, 50 + ot:51 + ot], None,
                                                op0=ALU.mult))
                kb.bank_set_free(5, tv)
                tap("xcm", xcm[:, 0:NFT * 32], txcm)
                tap("lmr", lmr[:, 0:64], tv)
                tap("lmn", lmn[:, 0:64], tv)
                kb.barrier()
                kb.wait('Y', tv)
                tld = None
                for ot in range(2):
                    tld = kb.mark(Y.dma_start(out=d["ag0l_in"][ot * 128:(ot + 1) * 128, :],
                                              in_=lmn[:, ot * 32:(ot + 1) * 32]), 'DY')
                kb.wait('G', kb.tok('DY'))
                G.collective_compute("AllGather", ALU.bypass,
                                     replica_groups=[[0, 1, 2, 3], [4, 5, 6, 7]],
                                     ins=[d["ag0k_in"][:, :]],
                                     outs=[d["ag0k_out"][:, :, :]]).then_inc(kb.sems['CC'], 1)
                G.collective_compute("AllGather", ALU.bypass,
                                     replica_groups=[[0, 1, 2, 3], [4, 5, 6, 7]],
                                     ins=[d["ag0v_in"][:, :]],
                                     outs=[d["ag0v_out"][:, :, :]]).then_inc(kb.sems['CC'], 1)
                G.collective_compute("AllGather", ALU.bypass,
                                     replica_groups=[[0, 1, 2, 3], [4, 5, 6, 7]],
                                     ins=[d["ag0l_in"][:, :]],
                                     outs=[d["ag0l_out"][:, :, :]]).then_inc(kb.sems['CC'], 1)
                kb.cnt['CC'] += 3
            tag0 = kb.tok('CC')

            p1e = ExitStack()
            with p1e:
                CW = 256
                NCC = TPC // CW
                xc = kb.sb("xc2", [128, NFT * CW], p1e)
                qr = kb.sb("qr", [128, 2 * TPC], p1e)
                wkv = kb.sb("wkv2", [128, NFT * 128], p1e)
                lmf = kb.sb("lmf", [128, 2 * NCH], p1e)
                l0m = kb.sb("l0m", [128, NQT * NCH], p1e)
                ssb = kb.sb("ssb", [128, NCH], p1e)
                m8b = kb.sb("m8b2", [128, 8], p1e)
                kb.barrier()
                twk_free = None
                for cc in range(NCC):
                    c0 = cc * CW
                    tnf = norm_stats(lambda i, j: XTc(xwork, i, c0 + j * 128, 128),
                                     list(range(NFT)), CW // 128, 'rms', EPS0,
                                     src_tok=txp)
                    tnb = bcast_stats(stat[:, 0:CW // 128], CW // 128, rb[:, 0:CW], tnf)
                    kb.wait('V', tnb)
                    for i in range(NFT):
                        V.tensor_tensor(XTc(xc, i, 0, CW, CW), XTc(xwork, i, c0, CW),
                                        rb[:, 0:CW], op=ALU.mult)
                        tv = kb.mark(V.tensor_scalar(XTc(xc, i, 0, CW, CW),
                                                     XTc(xc, i, 0, CW, CW),
                                                     gain[:, 32 + i:33 + i], None,
                                                     op0=ALU.mult), 'V')
                    txr = tv
                    for ot in range(2):
                        kb.wait('Y', twk_free)
                        twq = load_wcols(wkv, d["wqrT"], NFT, ot * 128, 128)
                        bnk = 0
                        kb.bank_wait(bnk)
                        kb.wait('T', twq); kb.wait('T', txr)
                        for i in range(NFT):
                            inst = T.matmul(kb.bank(bnk)[:, 0:CW],
                                            wkv[:, i * 128:(i + 1) * 128],
                                            XTc(xc, i, 0, CW, CW),
                                            start=(i == 0), stop=(i == NFT - 1))
                        tmm = kb.mark(inst, 'T')
                        twk_free = tmm
                        kb.wait('V', tmm)
                        tv = kb.mark(V.tensor_copy(qr[:, ot * TPC + c0: ot * TPC + c0 + CW],
                                                   kb.bank(bnk)[:, 0:CW]), 'V')
                        kb.bank_set_free(bnk, tv)
                tnf = norm_stats(lambda i, j: XTc(qr, i, j * 128, 128), [0, 1],
                                 NQT, 'rms', EPS0, src_tok=tv)
                tnb = bcast_stats(stat[:, 0:NQT], NQT, rb[:, 0:TPC], tnf)
                kb.wait('V', tnb)
                for ot in range(2):
                    V.tensor_tensor(XT(qr, ot), XT(qr, ot), rb[:], op=ALU.mult)
                    tv = kb.mark(V.tensor_scalar(XT(qr, ot), XT(qr, ot),
                                                 gain[:, 48 + ot:49 + ot], None,
                                                 op0=ALU.mult), 'V')
                tqr = tv
                tap("qr", qr[:, 0:TPC], tqr)
                kb.wait('Y', tag0)
                tlf = None
                for gb in range(NBLK):
                    r = min(gb, 7 - gb); half = 0 if gb < 4 else 1
                    for ot in range(2):
                        tlf = kb.mark(Y.dma_start(
                            out=lmf[:, ot * NCH + gb * 16: ot * NCH + (gb + 1) * 16],
                            in_=d["ag0l_out"][r, ot * 128:(ot + 1) * 128,
                                              half * 16:(half + 1) * 16]), 'DY')
                tap("lmf", lmf[:, 0:2 * NCH], tlf)
                tl0 = None
                for qt in range(NQT):
                    tl0 = kb.mark(Y.dma_start(out=l0m[:, qt * NCH:(qt + 1) * NCH],
                                              in_=d["l0mask"][qt, :, :]), 'DY')
                kb.wait('T', tqr); kb.wait('T', tlf)
                for qt in range(NQT):
                    bnk = 0
                    kb.bank_wait(bnk)
                    for ot in range(2):
                        inst = T.matmul(kb.bank(bnk)[:, 0:NCH],
                                        qr[:, ot * TPC + qt * 128: ot * TPC + (qt + 1) * 128],
                                        lmf[:, ot * NCH:(ot + 1) * NCH],
                                        start=(ot == 0), stop=(ot == 1))
                    tmm = kb.mark(inst, 'T')
                    kb.wait('V', tmm); kb.wait('V', tl0)
                    kb.vsw(V.tensor_scalar(ssb[:], kb.bank(bnk)[:, 0:NCH], INVSQ_RD, None,
                                           op0=ALU.mult))
                    tv = kb.vsw(V.tensor_tensor(ssb[:], ssb[:],
                                                l0m[:, qt * NCH:(qt + 1) * NCH],
                                                op=ALU.add))
                    kb.bank_set_free(bnk, tv)
                    kb.vsw(V.max(out=m8b[:], in_=ssb[:]))
                    kb.vsw(V.tensor_scalar(stat[:, 0:1], m8b[:, 0:1], -1e29, None, op0=ALU.max))
                    tvm = kb.vsw(V.tensor_scalar(stat[:, 1:2], stat[:, 0:1], -1.0,
                                                 None, op0=ALU.mult))
                    kb.wait('S', tvm)
                    te = kb.mark(S.activation(sq[0][:, 0:8], m8b[:], AF.Exp,
                                              bias=stat[:, 1:2]), 'S')
                    kb.wait('V', te)
                    kb.vsw(V.reduce_sum(stat[:, 2:3], sq[0][:, 0:8], axis=AX.X))
                    kb.vsw(V.tensor_scalar(stat[:, 2:3], stat[:, 2:3], 1e-30, None, op0=ALU.max))
                    kb.vsw(V.reciprocal(stat[:, 4:5], stat[:, 2:3]))
                    kb.vsw(V.tensor_scalar(stat[:, 3:4], m8b[:, 0:1], -1e29, None, op0=ALU.is_gt))
                    tvz = kb.vsw(V.tensor_tensor(stat[:, 2:3], stat[:, 4:5],
                                                 stat[:, 3:4], op=ALU.mult))
                    kb.wait('S', tvz)
                    te = kb.mark(S.activation(w_all[:, qt * NCH:(qt + 1) * NCH], ssb[:],
                                              AF.Exp, bias=stat[:, 1:2]), 'S')
                    kb.wait('V', te)
                    kb.vsw(V.tensor_scalar(ssb[:], ssb[:], m8b[:, 7:8], None, op0=ALU.is_ge))
                    kb.vsw(V.tensor_tensor(w_all[:, qt * NCH:(qt + 1) * NCH],
                                           w_all[:, qt * NCH:(qt + 1) * NCH], ssb[:], op=ALU.mult))
                    tv = kb.vsw(V.tensor_scalar(w_all[:, qt * NCH:(qt + 1) * NCH],
                                                w_all[:, qt * NCH:(qt + 1) * NCH],
                                                stat[:, 2:3], None, op0=ALU.mult))
                twall = tv
                tap("w_all", w_all[:, 0:NCH * NQT], twall)

            p2 = ExitStack()
            with p2:
                qh = kb.sb("qh", [128, TPC], p2)
                kkb = [kb.sb(f"kkb{i}", [128, 512], p2) for i in range(2)]
                vkb = [kb.sb(f"vkb{i}", [128, 512], p2) for i in range(2)]
                psb = [kb.sb(f"psb{i}", [128, 512], p2) for i in range(2)]
                ptb = kb.sb("ptb", [128, 512], p2)
                ssum = kb.sb("ssum", [128, 16], p2)
                mw = kb.sb("mwb", [128, 16], p2)
                abuf = [kb.sb(f"abuf{i}", [128, TPC], p2) for i in range(4)]
                wo = kb.sb("wob", [128, 512], p2)
                wq1 = kb.sb("wq1", [128, NFT * 128], p2)
                kfree = [None, None]
                vfree = [None, None]
                tT_prev = None
                tpv_prev = None
                twq_free = None
                te_prev_g = [None]
                kb.barrier()
                kb.wait('G', tag0)
                tap("xw_attn", xwork[:, 0:NFT * TPC], kb.tok('V'))
                tev = None
                tva = None
                for kvh in range(HKV):
                    for qi in range(4):
                        g = kvh * 4 + qi
                        kb.wait('Y', twq_free)
                        twq = load_wcols(wq1, d["wqT"], NFT, g * 128, 128)
                        if g == 0:
                            tap("wq10", wq1[:, 0:NFT * 128], twq)
                        kb.wait('T', twq)
                        for qc in range(2):
                            bnk = 0
                            kb.bank_wait(bnk)
                            for i in range(NFT):
                                inst = T.matmul(kb.bank(bnk)[:, 0:512],
                                                wq1[:, i * 128:(i + 1) * 128],
                                                XTc(xwork, i, qc * 512, 512),
                                                start=(i == 0), stop=(i == NFT - 1))
                            tmm = kb.mark(inst, 'T')
                            twq_free = tmm
                            kb.wait('V', tmm)
                            tv = kb.mark(V.tensor_copy(qh[:, qc * 512:(qc + 1) * 512],
                                                       kb.bank(bnk)[:, 0:512]), 'V')
                            kb.bank_set_free(bnk, tv)
                        tnf = norm_stats(lambda i, j: XTc(qh, 0, j * 128, 128), [0],
                                         NQT, 'rms', EPS0, src_tok=tv)
                        if g == 0:
                            tap("qh0", qh[:, 0:TPC], tv)
                        kb.wait('V', tnf)
                        kb.wait('V', te_prev_g[0])
                        tqs = kb.vsw(V.tensor_scalar(qstat[:], stat[:, 0:NQT],
                                                     INVSQ_HD, None, op0=ALU.mult))
                        tq = tv
                        for qt in range(NQT):
                            ob = 3
                            tTpb = [tT_prev, tT_prev]
                            pend = None

                            def flush_pv(pend_kbi, pend_tv):
                                nonlocal tT_prev, tpv_prev, tev
                                pb2 = pend_kbi % 2
                                tb = 2
                                kb.bank_wait(tb)
                                kb.wait('T', pend_tv)
                                for j in range(4):
                                    inst = T.transpose(
                                        kb.bank(tb)[:, j * 128:(j + 1) * 128],
                                        psb[pb2][:, j * 128:(j + 1) * 128], ident[:])
                                tT = kb.mark(inst, 'T')
                                tT_prev = tT
                                tTpb[pb2] = tT
                                kb.wait('S', tT)
                                kb.wait('S', tpv_prev)
                                tpt = kb.mark(S.copy(ptb[:], kb.bank(tb)[:, 0:512]), 'S')
                                kb.bank_set_free(tb, tpt)
                                kb.wait('T', tpt)
                                if pend_kbi == 0:
                                    kb.bank_wait(ob)
                                for j in range(4):
                                    inst = T.matmul(kb.bank(ob)[:, 0:128],
                                                    vkb[pend_kbi % 2][:, j * 128:(j + 1) * 128],
                                                    ptb[:, j * 128:(j + 1) * 128],
                                                    start=(pend_kbi == 0 and j == 0),
                                                    stop=(pend_kbi == NBLK - 1 and j == 3))
                                tpv = kb.mark(inst, 'T')
                                tpv_prev = tpv
                                vfree[pend_kbi % 2] = tpv
                                return tpv

                            for kbi in range(NBLK):
                                r = min(kbi, 7 - kbi); half = 0 if kbi < 4 else 1
                                buf = kbi % 2
                                kb.wait('Y', kfree[buf])
                                tkl = kb.mark(Y.dma_start(
                                    out=kkb[buf][:],
                                    in_=d["ag0k_out"][r, kvh * 128:(kvh + 1) * 128,
                                                      half * 512:(half + 1) * 512]), 'DY')
                                kb.wait('G', vfree[buf])
                                tvl = kb.mark(G.dma_start(
                                    out=vkb[buf][:].rearrange("p (j v) -> p j v", v=128),
                                    in_=d["ag0v_out"][r, half * 512:(half + 1) * 512,
                                                      kvh * 128:(kvh + 1) * 128].rearrange(
                                        "(j p) v -> p j v", p=128)), 'DG')
                                sb_ = 0 + buf
                                kb.bank_wait(sb_)
                                kb.wait('T', tq); kb.wait('T', tkl); kb.wait('T', tvl)
                                tmm = kb.mark(T.matmul(kb.bank(sb_)[:, 0:512],
                                                       qh[:, qt * 128:(qt + 1) * 128],
                                                       kkb[buf][:], start=True, stop=True),
                                              'T')
                                kfree[buf] = tmm
                                kb.wait('S', tmm)
                                kb.wait('S', tTpb[buf])
                                kb.wait('S', tqs)
                                te = kb.mark(S.activation(psb[buf][:], kb.bank(sb_)[:, 0:512],
                                                          AF.Exp, bias=cst[:, 0:1],
                                                          scale=qstat[:, qt:qt + 1]), 'S')
                                te_prev_g[0] = te
                                kb.bank_set_free(sb_, te)
                                kb.wait('V', te)
                                kb.vsw(V.reduce_sum(
                                    ssum[:, 0:16].rearrange("p (c o) -> p c o", o=1),
                                    psb[buf][:].rearrange("p (c s) -> p c s", s=SCH),
                                    axis=AX.X))
                                kb.vsw(V.reciprocal(mw[:], ssum[:]))
                                kb.vsw(V.tensor_tensor(
                                    mw[:],
                                    w_all[:, qt * NCH + kbi * 16:
                                          qt * NCH + (kbi + 1) * 16],
                                    mw[:], op=ALU.mult))
                                tv = kb.mark(V.tensor_tensor(
                                    psb[buf][:].rearrange("p (c s) -> p c s", s=SCH),
                                    psb[buf][:].rearrange("p (c s) -> p c s", s=SCH),
                                    mw[:].rearrange("p (c o) -> p c o", o=1).to_broadcast(
                                        [128, 16, SCH]), op=ALU.mult), 'V')
                                if pend is not None:
                                    flush_pv(*pend)
                                pend = (kbi, tv)
                            tpv = flush_pv(*pend)
                            kb.wait('V', tpv)
                            tev = kb.mark(V.tensor_copy(abuf[qi][:, qt * 128:(qt + 1) * 128],
                                                        kb.bank(ob)[:, 0:128]), 'V')
                            kb.bank_set_free(ob, tev)
                        if g == 0:
                            tap("abuf0", abuf[0][:, 0:TPC], tev)
                    wo_free = None
                    for ot in range(NFT):
                        kb.wait('Y', wo_free)
                        towl = None
                        for qi in range(4):
                            g = kvh * 4 + qi
                            towl = kb.mark(Y.dma_start(out=wo[:, qi * 128:(qi + 1) * 128],
                                                       in_=d["woT"][g * 128:(g + 1) * 128,
                                                                    ot * 128:(ot + 1) * 128]),
                                           'DY')
                        for qc in range(2):
                            bnk = 4 + (qc % 2)
                            kb.bank_wait(bnk)
                            kb.wait('T', towl); kb.wait('T', tev)
                            for qi in range(4):
                                inst = T.matmul(kb.bank(bnk)[:, 0:512],
                                                wo[:, qi * 128:(qi + 1) * 128],
                                                abuf[qi][:, qc * 512:(qc + 1) * 512],
                                                start=(qi == 0), stop=(qi == 3))
                            tmm = kb.mark(inst, 'T')
                            wo_free = tmm
                            kb.wait('V', tmm)
                            tva = kb.mark(V.tensor_tensor(XTc(xsum, ot, qc * 512, 512),
                                                          XTc(xsum, ot, qc * 512, 512),
                                                          kb.bank(bnk)[:, 0:512],
                                                          op=ALU.add), 'V')
                            kb.bank_set_free(bnk, tva)
                tap("xsum_a0", xsum[:, 0:TPC], tva)
            trgru_final = kb.tok('V')
        _moe(nc, kb, d, tap, XT, XTc, xsum, we_tm, rg, ru, onesall, ones_col,
             sq, bcast_stats, tweT_tok[0])
    return None


def _moe(nc, kb, d, tap, XT, XTc, xsum, we_tm, rg, ru, onesall, ones_col, sq,
         bcast_stats, tweT):
    T, V, S, G, Y = kb.T, kb.V, kb.S, kb.G, kb.Y
    p3 = ExitStack()
    with p3:
        u1g = kb.sb("u1g", [128, TPC], p3)
        u1u = kb.sb("u1u", [128, TPC], p3)
        web = kb.sb("web", [128, TPC], p3)
        rdw = kb.sb("rdw", [128, TPC], p3)
        got = kb.sb("gotm", [128, 512], p3)
        uot = kb.sb("uotm", [128, 512], p3)
        u3blk = [kb.sb(f"u3b{i}", [128, TPC], p3) for i in range(8)]
        ew = [{'ug': kb.sb(f"ug{i}", [128, 128], p3), 'uu': kb.sb(f"uu{i}", [128, 128], p3),
               'ud': kb.sb(f"ud{i}", [128, 128], p3),
               'cg': kb.sb(f"cg{i}", [128, IDIM], p3),
               'cu': kb.sb(f"cu{i}", [128, IDIM], p3)} for i in range(2)]
        cdb = [kb.sb(f"cdb{i}", [128, 128], p3) for i in range(2)]
        rdT = kb.sb("rdT", [128, NIT * 128], p3)
        trd = None
        for i in range(NIT):
            trd = kb.mark(Y.dma_start(out=rdT[:, i * 128:(i + 1) * 128],
                                      in_=d["RdT"][i * 128:(i + 1) * 128, :]), 'DY')
        ewload = [None, None]
        ewfree = [None, None]
        kb.barrier()

        def load_expert(e):
            buf = e % 2
            kb.wait('Y', ewfree[buf])
            kb.mark(Y.dma_start(out=ew[buf]['ug'][:], in_=d["UgT"][e, :, :]), 'DY')
            kb.mark(Y.dma_start(out=ew[buf]['uu'][:], in_=d["UuT"][e, :, :]), 'DY')
            kb.mark(Y.dma_start(out=ew[buf]['ud'][:], in_=d["UdT"][e, :, :]), 'DY')
            kb.mark(Y.dma_start(out=ew[buf]['cg'][:], in_=d["CgT"][e, :, :]), 'DY')
            ewload[buf] = kb.mark(Y.dma_start(out=ew[buf]['cu'][:],
                                              in_=d["CuT"][e, :, :]), 'DY')

        load_expert(0)
        kb.wait('T', trd)
        tgot_free = None
        tcd = None
        tse = None
        tse2 = None
        for eb in range(4):
            for ei in range(8):
                e = eb * 8 + ei
                buf = e % 2
                if e + 1 < NEXP:
                    load_expert(e + 1)
                kb.wait('T', ewload[buf])
                we_e = we_tm[:].rearrange("p (j e) -> p j e", e=NEXP)[:, :, e]
                tvw = bcast_stats(we_e, NQT, web[:, 0:TPC], tweT)
                for qc in range(2):
                    for (dst, wkey, bnk) in ((u1g, 'ug', 6), (u1u, 'uu', 7)):
                        kb.bank_wait(bnk)
                        tmm = kb.mark(T.matmul(kb.bank(bnk)[:, 0:512], ew[buf][wkey][:],
                                               (rg if dst is u1g else ru)[:,
                                               qc * 512:(qc + 1) * 512],
                                               start=True, stop=True), 'T')
                        kb.wait('S', tmm)
                        tse = kb.mark(S.copy(dst[:, qc * 512:(qc + 1) * 512],
                                             kb.bank(bnk)[:, 0:512]), 'S')
                        kb.bank_set_free(bnk, tse)
                for qc in range(2):
                    kb.bank_wait(2)
                    gbuf = [got[:], sq[0][:]]
                    ubuf = [uot[:], sq[1][:]]
                    gfree = [tgot_free, tgot_free]
                    tgu_p = [None, None]
                    for ot in range(NIT):
                        pb_ = ot % 2
                        bg = (0, 6)[pb_]
                        bu = (1, 7)[pb_]
                        kb.bank_wait(bg)
                        kb.wait('T', tse)
                        tg = kb.mark(T.matmul(kb.bank(bg)[:, 0:512],
                                              ew[buf]['cg'][:, ot * 128:(ot + 1) * 128],
                                              u1g[:, qc * 512:(qc + 1) * 512],
                                              start=True, stop=True), 'T')
                        kb.bank_wait(bu)
                        tu = kb.mark(T.matmul(kb.bank(bu)[:, 0:512],
                                              ew[buf]['cu'][:, ot * 128:(ot + 1) * 128],
                                              u1u[:, qc * 512:(qc + 1) * 512],
                                              start=True, stop=True), 'T')
                        if ot > 0:
                            po = (ot - 1) % 2
                            kb.wait('T', tgu_p[po])
                            inst = T.matmul(kb.bank(2)[:, 0:512],
                                            rdT[:, (ot - 1) * 128:ot * 128], gbuf[po],
                                            start=(ot - 1 == 0), stop=False)
                            gfree[po] = kb.mark(inst, 'T')
                        kb.wait('S', tg)
                        kb.wait('S', gfree[pb_])
                        tsg = kb.mark(S.activation(gbuf[pb_], kb.bank(bg)[:, 0:512],
                                                   AF.Silu), 'S')
                        kb.bank_set_free(bg, tsg)
                        kb.wait('V', tu)
                        kb.wait('V', gfree[pb_])
                        tsu = kb.mark(V.tensor_copy(ubuf[pb_], kb.bank(bu)[:, 0:512]), 'V')
                        kb.bank_set_free(bu, tsu)
                        kb.wait('V', tsg)
                        tgu_p[pb_] = kb.mark(V.tensor_tensor(gbuf[pb_], gbuf[pb_],
                                                             ubuf[pb_], op=ALU.mult), 'V')
                    po = (NIT - 1) % 2
                    kb.wait('T', tgu_p[po])
                    inst = T.matmul(kb.bank(2)[:, 0:512],
                                    rdT[:, (NIT - 1) * 128:NIT * 128], gbuf[po],
                                    start=False, stop=True)
                    tgot_free = kb.mark(inst, 'T')
                    trdm = tgot_free
                    kb.wait('V', trdm); kb.wait('V', tvw)
                    trw = kb.mark(V.tensor_tensor(rdw[:, qc * 512:(qc + 1) * 512],
                                                  kb.bank(2)[:, 0:512],
                                                  web[:, qc * 512:(qc + 1) * 512],
                                                  op=ALU.mult), 'V')
                    kb.bank_set_free(2, trw)
                    kb.bank_wait(3)
                    kb.wait('T', trw)
                    tmm = kb.mark(T.matmul(kb.bank(3)[:, 0:512], ew[buf]['ud'][:],
                                           rdw[:, qc * 512:(qc + 1) * 512],
                                           start=True, stop=True), 'T')
                    kb.wait('S', tmm)
                    tse2 = kb.mark(S.copy(u3blk[ei][:, qc * 512:(qc + 1) * 512],
                                          kb.bank(3)[:, 0:512]), 'S')
                    kb.bank_set_free(3, tse2)
                ewfree[buf] = tse2
            cd_free = [None, None]
            for ot in range(NFT):
                kb.bank_wait(4)
                kb.bank_wait(5)
                for ei in range(8):
                    e = eb * 8 + ei
                    buf2 = ei % 2
                    kb.wait('Y', cd_free[buf2])
                    tcl = kb.mark(Y.dma_start(out=cdb[buf2][:],
                                              in_=d["CdT"][e, :, ot * 128:(ot + 1) * 128]),
                                  'DY')
                    kb.wait('T', tcl)
                    kb.wait('T', tse2)
                    for qc in range(2):
                        inst = T.matmul(kb.bank(4 + qc)[:, 0:512], cdb[buf2][:],
                                        u3blk[ei][:, qc * 512:(qc + 1) * 512],
                                        start=(ei == 0), stop=(ei == 7))
                    cd_free[buf2] = kb.mark(inst, 'T')
                tmm = cd_free[1]
                kb.wait('V', tmm)
                for qc in range(2):
                    tcd = kb.mark(V.tensor_tensor(XTc(xsum, ot, qc * 512, 512),
                                                  XTc(xsum, ot, qc * 512, 512),
                                                  kb.bank(4 + qc)[:, 0:512],
                                                  op=ALU.add), 'V')
                kb.bank_set_free(4, tcd)
                kb.bank_set_free(5, tcd)
        tap("xsum_moe", xsum[:, 0:TPC], tcd)



def _l1(nc, kb, d, tap, XT, XTc, norm_stats, bcast_stats, load_wcols, xsum,
        rb, mb, stat, stat2, ones_col, onesall, ident, gl1, sq, cst):
    T, V, S, G, Y = kb.T, kb.V, kb.S, kb.G, kb.Y
    EPSL = EPS1

    l1a = ExitStack()
    with l1a:
        kb.barrier()
        rqa = kb.sb("rqa", [128, 8], l1a)
        rqab = kb.sb("rqab", [128, TPC], l1a)
        cosb = kb.sb("cosb", [64, TPC], l1a)
        sinb = kb.sb("sinb", [64, TPC], l1a)
        Y.dma_start(out=cosb[:], in_=d["cosT"][:, :]).then_inc(kb.sems['DY'], 16)
        kb.cnt['DY'] += 16
        tcs = kb.mark(Y.dma_start(out=sinb[:], in_=d["sinT"][:, :]), 'DY')

        tsrc = kb.tok('V')
        tln = norm_stats(lambda i, j: XTc(xsum, i, j * 128, 128),
                         list(range(NFT)), NQT, 'ln', EPS0, src_tok=tsrc)
        t1 = bcast_stats(stat[:, 0:NQT], NQT, rb[:, 0:TPC], tln)
        t2 = bcast_stats(stat2[:, 0:NQT], NQT, mb[:, 0:TPC], tln)
        kb.waits('V', [t1, t2])

        pA = ExitStack()
        with pA:
            xn1c = kb.sb("xn1c", [128, NFT * 512], pA)
            qac = kb.sb("qac", [128, 12 * 512], pA)
            ckvt = kb.sb("ckvt", [128, 4 * 512], pA)
            kpet = kb.sb("kpet", [64, 512], pA)
            kper = kb.sb("kper", [64, 512], pA)
            xtm = kb.sb("xtm", [128, 2048], pA)
            xint = kb.sb("xint", [128, 2048], pA, dtype=I32)
            wbufA = kb.sb("wbufA", [128, NFT * 128], pA)
            kb.barrier()
            tqaD = None
            for qc in range(2):
                c0 = qc * 512
                kb.barrier()
                for i in range(NFT):
                    V.tensor_tensor(XTc(xn1c, i, 0, 512, 512), XTc(xsum, i, c0, 512),
                                    rb[:, c0:c0 + 512], op=ALU.mult)
                    txn1 = kb.mark(V.tensor_tensor(XTc(xn1c, i, 0, 512, 512),
                                                   XTc(xn1c, i, 0, 512, 512),
                                                   mb[:, c0:c0 + 512], op=ALU.subtract), 'V')
                if qc == 0:
                    tap("xn1", xn1c[:, 0:512], txn1)
                twf = None
                tqa = None
                for ot in range(12):
                    kb.wait('Y', twf)
                    twl = load_wcols(wbufA, d["qaT"], NFT, ot * 128, 128)
                    bnk = 0
                    kb.bank_wait(bnk)
                    kb.wait('T', twl); kb.wait('T', txn1)
                    for i in range(NFT):
                        inst = T.matmul(kb.bank(bnk)[:, 0:512],
                                        wbufA[:, i * 128:(i + 1) * 128],
                                        XTc(xn1c, i, 0, 512, 512),
                                        start=(i == 0), stop=(i == NFT - 1))
                    tmm = kb.mark(inst, 'T')
                    twf = tmm
                    kb.wait('V', tmm)
                    tqa = kb.mark(V.tensor_copy(qac[:, ot * 512:(ot + 1) * 512],
                                                kb.bank(bnk)[:, 0:512]), 'V')
                    kb.bank_set_free(bnk, tqa)
                tnf = norm_stats(lambda i, j: qac[:, i * 512 + j * 128: i * 512 + (j + 1) * 128],
                                 list(range(12)), 4, 'rms', EPSL, src_tok=tqa)
                kb.wait('V', tnf)
                trq = kb.vsw(V.tensor_copy(rqa[:, qc * 4:(qc + 1) * 4], stat[:, 0:4]))
                kb.wait('Y', tqa)
                for ot in range(12):
                    tqaD = kb.mark(Y.dma_start(out=d["qa_spill"][ot * 128:(ot + 1) * 128,
                                                                 c0:c0 + 512],
                                               in_=qac[:, ot * 512:(ot + 1) * 512]), 'DY')
                tck = None
                for ot in range(5):
                    w = 128 if ot < 4 else 64
                    kb.wait('Y', twf)
                    twl = load_wcols(wbufA, d["kvaT"], NFT, ot * 128, w)
                    bnk = 1
                    kb.bank_wait(bnk)
                    kb.wait('T', twl)
                    for i in range(NFT):
                        inst = T.matmul(kb.bank(bnk)[0:w, 0:512],
                                        wbufA[:, i * w:(i + 1) * w],
                                        XTc(xn1c, i, 0, 512, 512),
                                        start=(i == 0), stop=(i == NFT - 1))
                    tmm = kb.mark(inst, 'T')
                    twf = tmm
                    kb.wait('V', tmm)
                    if ot < 4:
                        tck = kb.mark(V.tensor_copy(ckvt[:, ot * 512:(ot + 1) * 512],
                                                    kb.bank(bnk)[:, 0:512]), 'V')
                    else:
                        tck = kb.mark(V.tensor_copy(kpet[:], kb.bank(bnk)[0:64, 0:512]), 'V')
                    kb.bank_set_free(bnk, tck)
                tnf = norm_stats(lambda i, j: ckvt[:, i * 512 + j * 128: i * 512 + (j + 1) * 128],
                                 list(range(4)), 4, 'rms', EPSL, src_tok=tck)
                tnb = bcast_stats(stat[:, 0:4], 4, rb[:, 0:512], tnf)
                kb.wait('V', tnb)
                for ot in range(4):
                    V.tensor_tensor(ckvt[:, ot * 512:(ot + 1) * 512],
                                    ckvt[:, ot * 512:(ot + 1) * 512],
                                    rb[:, 0:512], op=ALU.mult)
                    tcn = kb.mark(V.tensor_scalar(ckvt[:, ot * 512:(ot + 1) * 512],
                                                  ckvt[:, ot * 512:(ot + 1) * 512],
                                                  gl1[:, 12 + ot:13 + ot], None,
                                                  op0=ALU.mult), 'V')
                kb.wait('Y', tcn)
                for ot in range(4):
                    kb.mark(Y.dma_start(out=d["ag1_in"][ot * 128:(ot + 1) * 128, c0:c0 + 512],
                                        in_=ckvt[:, ot * 512:(ot + 1) * 512]), 'DY')
                kb.wait('V', tcs)
                kb.vsw(V.tensor_copy(kper[0:32, :], kpet[32:64, :]))
                kb.vsw(V.tensor_scalar(kper[0:32, :], kper[0:32, :], -1.0, None, op0=ALU.mult))
                kb.vsw(V.tensor_copy(kper[32:64, :], kpet[0:32, :]))
                kb.vsw(V.tensor_tensor(kper[:], kper[:], sinb[:, c0:c0 + 512], op=ALU.mult))
                kb.vsw(V.tensor_tensor(kpet[:], kpet[:], cosb[:, c0:c0 + 512], op=ALU.mult))
                tkp = kb.vsw(V.tensor_tensor(kper[:], kper[:], kpet[:], op=ALU.add))
                kb.wait('Y', tkp)
                kb.mark(Y.dma_start(out=d["ag1_in"][512:576, c0:c0 + 512], in_=kper[:]), 'DY')
                for tt in range(4):
                    tm = xtm[:, 0:2048]
                    tb = 2
                    for i4 in range(4):
                        kb.bank_wait(tb)
                        for i in range(4):
                            ii = i4 * 4 + i
                            inst = T.transpose(kb.bank(tb)[:, i * 128:(i + 1) * 128],
                                               XTc(xn1c, ii, tt * 128, 128, 512), ident[:])
                        tTx = kb.mark(inst, 'T')
                        kb.wait('V', tTx)
                        tvv = kb.mark(V.tensor_copy(tm[:, i4 * 512:(i4 + 1) * 512],
                                                    kb.bank(tb)[:, 0:512]), 'V')
                        kb.bank_set_free(tb, tvv)
                    kb.wait('V', tvv)
                    kb.vsw(V.tensor_reduce(stat[:, 0:1], tm, axis=AX.X, op=ALU.max,
                                           apply_absolute_value=True))
                    kb.vsw(V.tensor_scalar(stat[:, 0:1], stat[:, 0:1], 1.0 / 127.0, 1e-8,
                                           op0=ALU.mult, op1=ALU.max))
                    kb.vsw(V.reciprocal(stat[:, 1:2], stat[:, 0:1]))
                    kb.vsw(V.tensor_scalar(tm, tm, stat[:, 1:2], None, op0=ALU.mult))
                    kb.vsw(V.tensor_copy(xint[:], tm))
                    kb.vsw(V.tensor_copy(tm, xint[:]))
                    tql = kb.vsw(V.tensor_scalar(tm, tm, stat[:, 0:1], None, op0=ALU.mult))
                    kb.wait('T', tql)
                    for i4 in range(4):
                        tb = 3
                        kb.bank_wait(tb)
                        for i in range(4):
                            inst = T.transpose(kb.bank(tb)[:, i * 128:(i + 1) * 128],
                                               tm[:, (i4 * 4 + i) * 128:(i4 * 4 + i + 1) * 128],
                                               ident[:])
                        tTx = kb.mark(inst, 'T')
                        kb.wait('V', tTx)
                        for i in range(4):
                            ii = i4 * 4 + i
                            tvq = kb.mark(V.tensor_copy(
                                XTc(xn1c, ii, tt * 128, 128, 512),
                                kb.bank(tb)[:, i * 128:(i + 1) * 128]), 'V')
                        kb.bank_set_free(tb, tvq)
                kb.wait('Y', tvq)
                for i in range(NFT):
                    kb.mark(Y.dma_start(out=d["xq_spill"][i * 128:(i + 1) * 128, c0:c0 + 512],
                                        in_=XTc(xn1c, i, 0, 512, 512)), 'DY')
            kb.wait('G', kb.tok('DY'))
            G.collective_compute("AllGather", ALU.bypass,
                                 replica_groups=[[0, 1, 2, 3], [4, 5, 6, 7]],
                                 ins=[d["ag1_in"][:, :]],
                                 outs=[d["ag1_out"][:, :, :]]).then_inc(kb.sems['CC'], 1)
            kb.cnt['CC'] += 1
        tag1 = kb.tok('CC')

        pB = ExitStack()
        with pB:
            wn = kb.sb("wn", [128, DFF], pB)
            wi2 = kb.sb("wi2", [128, DFF], pB, dtype=I32)
            wqe = kb.sb("wqe", [128, 512], pB)
            kb.barrier()
            tdq = None
            for (nj, wd, kdim, qwd, sd) in ((2 * NDT, "w1", DM, "qw1", "s1"),
                                            (NFT, "w2", DFF, "qw2", "s2")):
                nkt = kdim // 128
                for j in range(nj):
                    kb.wait('Y', kb.tok('T'))
                    twl = kb.mark(Y.dma_start(out=wn[:, 0:kdim],
                                              in_=d[wd][j * 128:(j + 1) * 128, :]), 'DY')
                    kb.wait('V', twl)
                    kb.vsw(V.tensor_reduce(stat[:, 0:1], wn[:, 0:kdim], axis=AX.X,
                                           op=ALU.max, apply_absolute_value=True))
                    kb.vsw(V.tensor_scalar(stat[:, 0:1], stat[:, 0:1], 1.0 / 127.0, 1e-8,
                                           op0=ALU.mult, op1=ALU.max))
                    kb.vsw(V.reciprocal(stat[:, 1:2], stat[:, 0:1]))
                    kb.vsw(V.tensor_scalar(wn[:, 0:kdim], wn[:, 0:kdim], stat[:, 1:2],
                                           None, op0=ALU.mult))
                    kb.vsw(V.tensor_copy(wi2[:, 0:kdim], wn[:, 0:kdim]))
                    tvw = kb.vsw(V.tensor_copy(wn[:, 0:kdim], wi2[:, 0:kdim]))
                    kb.wait('Y', tvw)
                    tsd = kb.mark(Y.dma_start(out=d[sd][j, :].rearrange("(p o) -> p o", o=1),
                                              in_=stat[:, 0:1]), 'DY')
                    kb.wait('T', tvw)
                    for k4 in range((nkt + 3) // 4):
                        tb = 4
                        kb.bank_wait(tb)
                        kn4 = min(4, nkt - k4 * 4)
                        for k in range(kn4):
                            inst = T.transpose(kb.bank(tb)[:, k * 128:(k + 1) * 128],
                                               wn[:, (k4 * 4 + k) * 128:(k4 * 4 + k + 1) * 128],
                                               ident[:])
                        tTx = kb.mark(inst, 'T')
                        kb.wait('V', tTx)
                        kb.wait('V', tdq)
                        tvq = kb.mark(V.tensor_copy(wqe[:, 0:kn4 * 128],
                                                    kb.bank(tb)[:, 0:kn4 * 128]), 'V')
                        kb.bank_set_free(tb, tvq)
                        kb.wait('Y', tvq)
                        for k in range(kn4):
                            tdq = kb.mark(Y.dma_start(
                                out=d[qwd][j, (k4 * 4 + k) * 128:(k4 * 4 + k + 1) * 128, :],
                                in_=wqe[:, k * 128:(k + 1) * 128]), 'DY')

        pC = ExitStack()
        with pC:
            kpf = kb.sb("kpf", [64, LSEQ], pC)
            kn = kb.sb("kn", [128, LSEQ], pC)
            vt = kb.sb("vt", [128, LSEQ], pC)
            s_sb = kb.sb("s_sb", [128, LSEQ], pC)
            qbs = kb.sb("qbs", [128, 12 * 192], pC)
            kvbk = kb.sb("kvbk", [128, 4 * 128], pC)
            kvbv = kb.sb("kvbv", [128, 4 * 128], pC)
            ckvb = kb.sb("ckvb", [128, 4 * 512], pC)
            qas = kb.sb("qas", [128, 12 * 128], pC)
            qn2 = kb.sb("qn2", [128, 128], pC)
            qp2 = kb.sb("qp2", [64, 128], pC)
            qpr = kb.sb("qpr", [64, 128], pC)
            psb2 = kb.sb("psb2", [128, 512], pC)
            ptb2 = kb.sb("ptb2", [128, 512], pC)
            mb2 = [kb.sb(f"mb2_{i}", [128, 512], pC) for i in range(2)]
            ab4 = [kb.sb(f"ab4_{i}", [128, TPC], pC) for i in range(2)]
            owb = kb.sb("owb", [128, 512], pC)
            mx8 = kb.sb("mx8", [128, 16], pC)
            z8 = kb.sb("z8", [128, 16], pC)
            kb.barrier()
            trq2 = bcast_stats(rqa[:, 0:8], 8, rqab[:, 0:TPC], kb.tok('V'))
            kb.wait('Y', tag1)
            tkpf = None
            for gb in range(NBLK):
                r = min(gb, 7 - gb); half = 0 if gb < 4 else 1
                tkpf = kb.mark(Y.dma_start(out=kpf[:, gb * 512:(gb + 1) * 512],
                                           in_=d["ag1_out"][r, 512:576,
                                                            half * 512:(half + 1) * 512]), 'DY')
            tev = None
            tva = None
            mfree = [None, None]
            tT_prev = None
            tpv_prev = None
            te_prev = [None]
            tqb_free = None
            for h in range(NH):
                kb.wait('Y', tqb_free)
                tqb = None
                for i in range(12):
                    tqb = kb.mark(Y.dma_start(out=qbs[:, i * 192:(i + 1) * 192],
                                              in_=d["qbT"][i * 128:(i + 1) * 128,
                                                           h * 192:(h + 1) * 192]), 'DY')
                kb.wait('V', tqb)
                for i in range(12):
                    tqb2 = kb.mark(V.tensor_scalar(qbs[:, i * 192:(i + 1) * 192],
                                                   qbs[:, i * 192:(i + 1) * 192],
                                                   gl1[:, i:i + 1], None, op0=ALU.mult), 'V')
                for ct in range(4):
                    Y.dma_start(out=kvbk[:, ct * 128:(ct + 1) * 128],
                                in_=d["kvbT"][ct * 128:(ct + 1) * 128,
                                              h * 256:h * 256 + 128]).then_inc(
                        kb.sems['DY'], 16)
                    kb.cnt['DY'] += 16
                    tkv = kb.mark(Y.dma_start(out=kvbv[:, ct * 128:(ct + 1) * 128],
                                              in_=d["kvbT"][ct * 128:(ct + 1) * 128,
                                                            h * 256 + 128:h * 256 + 256]), 'DY')
                tknv = None
                for cb in range(NBLK):
                    r = min(cb, 7 - cb); half = 0 if cb < 4 else 1
                    kb.wait('Y', tknv)
                    tcl = kb.mark(Y.dma_start(
                        out=ckvb[:].rearrange("p (c t) -> p c t", t=512),
                        in_=d["ag1_out"][r, 0:512, half * 512:(half + 1) * 512].rearrange(
                            "(c p) t -> p c t", p=128)), 'DY')
                    bnk = 0
                    kb.bank_wait(bnk)
                    kb.wait('T', tcl); kb.wait('T', tkv)
                    for ct in range(4):
                        inst = T.matmul(kb.bank(bnk)[:, 0:512],
                                        kvbk[:, ct * 128:(ct + 1) * 128],
                                        ckvb[:, ct * 512:(ct + 1) * 512],
                                        start=(ct == 0), stop=(ct == 3))
                    tmm = kb.mark(inst, 'T')
                    kb.wait('V', tmm)
                    tknv = kb.mark(V.tensor_copy(kn[:, cb * 512:(cb + 1) * 512],
                                                 kb.bank(bnk)[:, 0:512]), 'V')
                    kb.bank_set_free(bnk, tknv)
                    for tt in range(4):
                        bnk = 1
                        kb.bank_wait(bnk)
                        for ct in range(4):
                            inst = T.matmul(kb.bank(bnk)[:, 0:128],
                                            ckvb[:, ct * 512 + tt * 128:
                                                 ct * 512 + (tt + 1) * 128],
                                            kvbv[:, ct * 128:(ct + 1) * 128],
                                            start=(ct == 0), stop=(ct == 3))
                        tmm = kb.mark(inst, 'T')
                        kb.wait('V', tmm)
                        tknv = kb.mark(V.tensor_copy(
                            vt[:, (cb * 4 + tt) * 128:(cb * 4 + tt + 1) * 128],
                            kb.bank(bnk)[:, 0:128]), 'V')
                        kb.bank_set_free(bnk, tknv)
                for qt in range(NQT):
                    tqs2 = None
                    for i in range(12):
                        tqs2 = kb.mark(Y.dma_start(out=qas[:, i * 128:(i + 1) * 128],
                                                   in_=d["qa_spill"][i * 128:(i + 1) * 128,
                                                                     qt * 128:(qt + 1) * 128]),
                                       'DY')
                    bn_, bp_ = 2, 2
                    kb.bank_wait(bn_)
                    kb.wait('T', tqs2); kb.wait('T', tqb2)
                    for i in range(12):
                        inst = T.matmul(kb.bank(bn_)[:, 0:128],
                                        qbs[:, i * 192:i * 192 + 128],
                                        qas[:, i * 128:(i + 1) * 128],
                                        start=(i == 0), stop=(i == 11))
                    tq1 = kb.mark(inst, 'T')
                    for i in range(12):
                        inst = T.matmul(kb.bank(bp_)[0:64, 128:256],
                                        qbs[:, i * 192 + 128:(i + 1) * 192],
                                        qas[:, i * 128:(i + 1) * 128],
                                        start=(i == 0), stop=(i == 11))
                    tq2 = kb.mark(inst, 'T')
                    kb.wait('V', tq2); kb.wait('V', trq2)
                    kb.vsw(V.tensor_tensor(qn2[:], kb.bank(bn_)[:, 0:128],
                                           rqab[:, qt * 128:(qt + 1) * 128], op=ALU.mult))
                    tqn = kb.vsw(V.tensor_tensor(qp2[:], kb.bank(bp_)[0:64, 128:256],
                                                 rqab[0:64, qt * 128:(qt + 1) * 128],
                                                 op=ALU.mult))
                    kb.bank_set_free(bn_, tqn)
                    kb.vsw(V.tensor_copy(qpr[0:32, :], qp2[32:64, :]))
                    kb.vsw(V.tensor_scalar(qpr[0:32, :], qpr[0:32, :], -1.0, None,
                                           op0=ALU.mult))
                    kb.vsw(V.tensor_copy(qpr[32:64, :], qp2[0:32, :]))
                    kb.vsw(V.tensor_tensor(qpr[:], qpr[:],
                                           sinb[:, qt * 128:(qt + 1) * 128], op=ALU.mult))
                    kb.vsw(V.tensor_tensor(qp2[:], qp2[:],
                                           cosb[:, qt * 128:(qt + 1) * 128], op=ALU.mult))
                    tqp = kb.vsw(V.tensor_tensor(qp2[:], qp2[:], qpr[:], op=ALU.add))
                    for kbi in range(NBLK):
                        mbuf = kbi % 2
                        kb.wait('Y', mfree[mbuf])
                        tml = kb.mark(Y.dma_start(out=mb2[mbuf][:],
                                                  in_=d["l1mask"][qt, kbi, :, :]), 'DY')
                        sb_ = 3
                        kb.bank_wait(sb_)
                        kb.wait('T', tqp)
                        T.matmul(kb.bank(sb_)[:, 0:512], qn2[:],
                                 kn[:, kbi * 512:(kbi + 1) * 512], start=True, stop=False)
                        inst = T.matmul(kb.bank(sb_)[:, 0:512], qp2[:],
                                        kpf[:, kbi * 512:(kbi + 1) * 512],
                                        start=False, stop=True)
                        tmm = kb.mark(inst, 'T')
                        kb.wait('V', tmm); kb.wait('V', tml)
                        tse = kb.mark(V.tensor_tensor(s_sb[:, kbi * 512:(kbi + 1) * 512],
                                                      kb.bank(sb_)[:, 0:512],
                                                      mb2[mbuf][:], op=ALU.add), 'V')
                        kb.bank_set_free(sb_, tse)
                        mfree[mbuf] = tse
                        tse2 = kb.mark(V.reduce_max(mx8[:, kbi:kbi + 1],
                                                    s_sb[:, kbi * 512:(kbi + 1) * 512],
                                                    axis=AX.X), 'V')
                    kb.wait('V', tse2)
                    kb.vsw(V.reduce_max(stat[:, 0:1], mx8[:, 0:8], axis=AX.X))
                    tbn = kb.vsw(V.tensor_scalar(stat[:, 1:2], stat[:, 0:1], -SCALE1,
                                                 None, op0=ALU.mult))
                    teA = None
                    for kbi in range(NBLK):
                        kb.wait('S', tbn)
                        kb.wait('S', tT_prev)
                        teA = kb.mark(S.activation(psb2[:], s_sb[:, kbi * 512:(kbi + 1) * 512],
                                                   AF.Exp, bias=stat[:, 1:2], scale=SCALE1,
                                                   accum_out=z8[:, kbi:kbi + 1]), 'S')
                    kb.wait('V', teA)
                    kb.vsw(V.reduce_sum(stat[:, 2:3], z8[:, 0:8], axis=AX.X))
                    tvz = kb.tok('V')
                    kb.wait('S', tvz)
                    tlz = kb.mark(S.activation(stat[:, 3:4], stat[:, 2:3], AF.Ln), 'S')
                    kb.wait('V', tlz)
                    tb2 = kb.vsw(V.tensor_tensor(stat[:, 1:2], stat[:, 1:2], stat[:, 3:4],
                                                 op=ALU.subtract))
                    ob = 5
                    for kbi in range(NBLK):
                        kb.wait('S', tb2)
                        kb.wait('S', tT_prev)
                        teB = kb.mark(S.activation(psb2[:], s_sb[:, kbi * 512:(kbi + 1) * 512],
                                                   AF.Exp, bias=stat[:, 1:2], scale=SCALE1), 'S')
                        tb_ = 4
                        kb.bank_wait(tb_)
                        kb.wait('T', teB)
                        for j in range(4):
                            inst = T.transpose(kb.bank(tb_)[:, j * 128:(j + 1) * 128],
                                               psb2[:, j * 128:(j + 1) * 128], ident[:])
                        tT = kb.mark(inst, 'T')
                        tT_prev = tT
                        kb.wait('S', tT)
                        kb.wait('S', tpv_prev)
                        tpt = kb.mark(S.copy(ptb2[:], kb.bank(tb_)[:, 0:512]), 'S')
                        kb.bank_set_free(tb_, tpt)
                        if kbi == 0:
                            kb.bank_wait(ob)
                        kb.wait('T', tpt)
                        for j in range(4):
                            inst = T.matmul(kb.bank(ob)[:, 0:128],
                                            vt[:, (kbi * 4 + j) * 128:(kbi * 4 + j + 1) * 128],
                                            ptb2[:, j * 128:(j + 1) * 128],
                                            start=(kbi == 0 and j == 0),
                                            stop=(kbi == NBLK - 1 and j == 3))
                        tpv = kb.mark(inst, 'T')
                        tpv_prev = tpv
                    kb.wait('V', tpv)
                    tev = kb.mark(V.tensor_copy(ab4[h % 2][:, qt * 128:(qt + 1) * 128],
                                                kb.bank(ob)[:, 0:128]), 'V')
                    kb.bank_set_free(ob, tev)
                tqb_free = tpv
                if h % 2 == 1:
                    hb = h // 2
                    wo_free = None
                    for ot in range(NFT):
                        kb.wait('Y', wo_free)
                        towl = None
                        for qi in range(2):
                            gg = hb * 2 + qi
                            towl = kb.mark(Y.dma_start(
                                out=owb[:, qi * 128:(qi + 1) * 128],
                                in_=d["owT"][gg * 128:(gg + 1) * 128,
                                             ot * 128:(ot + 1) * 128]), 'DY')
                        for qc in range(2):
                            bnk = 6 + (qc % 2)
                            kb.bank_wait(bnk)
                            kb.wait('T', towl); kb.wait('T', tev)
                            for qi in range(2):
                                inst = T.matmul(kb.bank(bnk)[:, 0:512],
                                                owb[:, qi * 128:(qi + 1) * 128],
                                                ab4[qi][:, qc * 512:(qc + 1) * 512],
                                                start=(qi == 0), stop=(qi == 1))
                            tmm = kb.mark(inst, 'T')
                            wo_free = tmm
                            kb.wait('V', tmm)
                            tva = kb.mark(V.tensor_tensor(XTc(xsum, ot, qc * 512, 512),
                                                          XTc(xsum, ot, qc * 512, 512),
                                                          kb.bank(bnk)[:, 0:512],
                                                          op=ALU.add), 'V')
                            kb.bank_set_free(bnk, tva)
            tap("xsum_a1", xsum[:, 0:TPC], tva)
            kb.wait('Y', tva)
            for i in range(NFT):
                kb.mark(Y.dma_start(out=d["xsum_spill"][i * 128:(i + 1) * 128, :],
                                    in_=XT(xsum, i)), 'DY')

        pD = ExitStack()
        with pD:
            TC = 256
            htc = xsum
            gt = xsum[:, 11264:11264 + TC]
            ut = xsum[:, 11264 + TC:11264 + 2 * TC]
            ot_ = xsum[:, 11264 + 2 * TC:11264 + 3 * TC]
            xqc = xsum[:, 12288:12288 + NFT * TC]
            htm = kb.sb("htm", [128, 2 * DFF], pD)
            hint = kb.sb("hint", [128, DFF], pD, dtype=I32)
            xsc = kb.sb("xsc", [128, NFT * TC], pD)
            qwb = [kb.sb(f"qwb{i}", [128, NFT * 128], pD) for i in range(2)]
            s1b = kb.sb("s1b", [128, 4], pD)
            kb.barrier()
            qw_free = [None, None]
            tq2f = None
            tod = [None]
            tglast = [None, None]
            thj_prev = [None]
            for tc in range(4):
                c0 = tc * TC
                txl = None
                for i in range(NFT):
                    txl = kb.mark(Y.dma_start(out=XTc(xqc, i, 0, TC, TC),
                                              in_=d["xq_spill"][i * 128:(i + 1) * 128,
                                                                c0:c0 + TC]), 'DY')
                for j in range(NDT):
                    if DBG_NORM and 'PDBAR' in DBG_NORM:
                        kb.barrier()
                    for (jj, dst, col) in ((j, gt, 0), (j + NDT, ut, 1)):
                        buf = jj % 2
                        kb.wait('Y', qw_free[buf])
                        twl = kb.mark(Y.dma_start(
                            out=qwb[buf][:].rearrange("p (a b) -> p a b", b=128),
                            in_=d["qw1"][jj, :, :].rearrange("(a p) b -> p a b", p=128)), 'DY')
                        kb.wait('Y', tglast[col])
                        tsl = kb.mark(Y.dma_start(out=s1b[:, col:col + 1],
                                                  in_=d["s1"][jj, :].rearrange(
                                                      "(p o) -> p o", o=1)), 'DY')
                        bnk = 0 + col
                        kb.bank_wait(bnk)
                        kb.wait('T', twl); kb.wait('T', txl)
                        for i in range(NFT):
                            inst = T.matmul(kb.bank(bnk)[:, 0:TC],
                                            qwb[buf][:, i * 128:(i + 1) * 128],
                                            XTc(xqc, i, 0, TC, TC),
                                            start=(i == 0), stop=(i == NFT - 1))
                        tmm = kb.mark(inst, 'T')
                        qw_free[buf] = tmm
                        if col == 0:
                            kb.wait('S', tmm); kb.wait('S', tsl)
                            kb.wait('S', thj_prev[0])
                            tgv = kb.mark(S.activation(dst, kb.bank(bnk)[:, 0:TC],
                                                       AF.Silu, scale=s1b[:, 0:1]), 'S')
                        else:
                            kb.wait('V', tmm); kb.wait('V', tsl)
                            tgv = kb.mark(V.tensor_scalar(dst, kb.bank(bnk)[:, 0:TC],
                                                          s1b[:, 1:2], None, op0=ALU.mult),
                                          'V')
                        kb.bank_set_free(bnk, tgv)
                        tglast[col] = tgv
                        if col == 0:
                            tg_s = tgv
                    kb.wait('V', tgv)
                    kb.wait('V', tg_s)
                    thj = kb.mark(V.tensor_tensor(htc[:, j * TC:(j + 1) * TC], gt, ut,
                                                  op=ALU.mult), 'V')
                    thj_prev[0] = thj
                kb.wait('T', thj)
                for tt in range(2):
                    for j4 in range(11):
                        tb = 2
                        kb.bank_wait(tb)
                        for j in range(4):
                            jj = j4 * 4 + j
                            inst = T.transpose(kb.bank(tb)[:, j * 128:(j + 1) * 128],
                                               htc[:, jj * TC + tt * 128:
                                                   jj * TC + (tt + 1) * 128], ident[:])
                        tTx = kb.mark(inst, 'T')
                        kb.wait('V', tTx)
                        tvv = kb.mark(V.tensor_copy(htm[:, tt * DFF + j4 * 512:
                                                        tt * DFF + (j4 + 1) * 512],
                                                    kb.bank(tb)[:, 0:512]), 'V')
                        kb.bank_set_free(tb, tvv)
                kb.wait('V', tvv)
                for tt in range(2):
                    if DBG_NORM and 'PDBAR' in DBG_NORM:
                        kb.barrier()
                    tm = htm[:, tt * DFF:(tt + 1) * DFF]
                    kb.vsw(V.tensor_reduce(stat[:, 0:1], tm, axis=AX.X, op=ALU.max,
                                           apply_absolute_value=True))
                    kb.vsw(V.tensor_scalar(stat[:, 0:1], stat[:, 0:1], 1.0 / 127.0, 1e-8,
                                           op0=ALU.mult, op1=ALU.max))
                    kb.vsw(V.reciprocal(stat[:, 1:2], stat[:, 0:1]))
                    kb.vsw(V.tensor_scalar(tm, tm, stat[:, 1:2], None, op0=ALU.mult))
                    kb.vsw(V.tensor_copy(hint[:], tm))
                    kb.vsw(V.tensor_copy(tm, hint[:]))
                    tql = kb.vsw(V.tensor_scalar(tm, tm, stat[:, 0:1], None, op0=ALU.mult))
                    kb.wait('T', tql)
                    for j4 in range(11):
                        tb = 3
                        kb.bank_wait(tb)
                        for j in range(4):
                            inst = T.transpose(kb.bank(tb)[:, j * 128:(j + 1) * 128],
                                               tm[:, (j4 * 4 + j) * 128:(j4 * 4 + j + 1) * 128],
                                               ident[:])
                        tTx = kb.mark(inst, 'T')
                        kb.wait('V', tTx)
                        for j in range(4):
                            jj = j4 * 4 + j
                            tvq = kb.mark(V.tensor_copy(
                                htc[:, jj * TC + tt * 128: jj * TC + (tt + 1) * 128],
                                kb.bank(tb)[:, j * 128:(j + 1) * 128]), 'V')
                        kb.bank_set_free(tb, tvq)
                txs = None
                for i in range(NFT):
                    txs = kb.mark(Y.dma_start(out=XTc(xsc, i, 0, TC, TC),
                                              in_=d["xsum_spill"][i * 128:(i + 1) * 128,
                                                                  c0:c0 + TC]), 'DY')
                for ot in range(NFT):
                    if DBG_NORM and 'PDBAR' in DBG_NORM:
                        kb.barrier()
                    kb.wait('Y', tod[0])
                    tsl = kb.mark(Y.dma_start(out=s1b[:, 2:3],
                                              in_=d["s2"][ot, :].rearrange(
                                                  "(p o) -> p o", o=1)), 'DY')
                    bnk = 0
                    kb.bank_wait(bnk)
                    kb.wait('T', tvq)
                    for kg in range(3):
                        k0 = kg * 16
                        kn_ = min(16, NDT - k0)
                        buf2 = kg % 2
                        kb.wait('Y', qw_free[buf2])
                        tw2 = kb.mark(Y.dma_start(
                            out=qwb[buf2][:, 0:kn_ * 128].rearrange(
                                "p (a b) -> p a b", b=128),
                            in_=d["qw2"][ot, k0 * 128:(k0 + kn_) * 128, :].rearrange(
                                "(a p) b -> p a b", p=128)), 'DY')
                        kb.wait('T', tw2)
                        for kt in range(kn_):
                            inst = T.matmul(kb.bank(bnk)[:, 0:TC],
                                            qwb[buf2][:, kt * 128:(kt + 1) * 128],
                                            htc[:, (k0 + kt) * TC:(k0 + kt + 1) * TC],
                                            start=(k0 + kt == 0),
                                            stop=(k0 + kt == NDT - 1))
                        qw_free[buf2] = kb.mark(inst, 'T')
                    tmm = qw_free[0]
                    tq2f = tmm
                    kb.wait('V', tmm); kb.wait('V', tsl); kb.wait('V', txs)
                    kb.wait('V', tod[0])
                    kb.vsw(V.tensor_scalar(ot_, kb.bank(bnk)[:, 0:TC], s1b[:, 2:3],
                                           None, op0=ALU.mult))
                    tfo = kb.mark(V.tensor_tensor(ot_, ot_, XTc(xsc, ot, 0, TC, TC),
                                                  op=ALU.add), 'V')
                    kb.bank_set_free(bnk, tfo)
                    kb.wait('Y', tfo)
                    tod[0] = kb.mark(Y.dma_start(
                        out=d["outT"][ot * 128:(ot + 1) * 128, c0:c0 + TC],
                        in_=ot_), 'DY')


def _host_prep(inputs):
    f32 = np.float32
    x = np.asarray(inputs["L0_attn_hidden_states"], f32)

    shared = {}
    tr = lambda a: np.ascontiguousarray(np.asarray(a, f32).T)
    shared["wkT"] = tr(inputs["L0_attn_w_k"])
    shared["wvT"] = tr(inputs["L0_attn_w_v"])
    shared["wlmkT"] = tr(inputs["L0_attn_w_lmk"])
    shared["wqrT"] = tr(inputs["L0_attn_w_q_retr"])
    shared["wqT"] = tr(inputs["L0_attn_w_q"])
    shared["woT"] = tr(inputs["L0_attn_w_o"])
    for nm, key in (("g_pre", "L0_attn_g_pre"), ("g_chunk", "L0_attn_g_chunk_norm"),
                    ("g_pre_retr", "L0_attn_g_pre_retr"), ("g_q_retr", "L0_attn_g_q_retr"),
                    ("g_lmk", "L0_attn_g_lmk"), ("g_k", "L0_attn_g_k_norm"),
                    ("g_qn", "L0_attn_g_q_norm")):
        shared[nm] = np.ascontiguousarray(np.asarray(inputs[key], f32))
    shared["gateT"] = tr(inputs["L0_ffn_gate_w"])
    shared["RgT"] = tr(inputs["L0_ffn_Rg"])
    shared["RuT"] = tr(inputs["L0_ffn_Ru"])
    shared["RdT"] = tr(inputs["L0_ffn_Rd"])
    for nm, key in (("UgT", "L0_ffn_Ug"), ("UuT", "L0_ffn_Uu"), ("UdT", "L0_ffn_Ud"),
                    ("CgT", "L0_ffn_Cg"), ("CuT", "L0_ffn_Cu"), ("CdT", "L0_ffn_Cd")):
        shared[nm] = np.ascontiguousarray(np.asarray(inputs[key], f32).transpose(0, 2, 1))
    shared["sRgT"] = tr(inputs["L0_ffn_s_Rg"])
    shared["sRuT"] = tr(inputs["L0_ffn_s_Ru"])
    shared["sRdT"] = tr(inputs["L0_ffn_s_Rd"])
    shared["sUgT"] = tr(inputs["L0_ffn_s_Ug"])
    shared["sUuT"] = tr(inputs["L0_ffn_s_Uu"])
    shared["sUdT"] = tr(inputs["L0_ffn_s_Ud"])
    shared["sCgT"] = tr(inputs["L0_ffn_s_Cg"])
    shared["sCuT"] = tr(inputs["L0_ffn_s_Cu"])
    shared["sCdT"] = tr(inputs["L0_ffn_s_Cd"])
    shared["qaT"] = tr(inputs["L1_attn_q_a_w"])
    shared["qbT"] = tr(inputs["L1_attn_q_b_w"])
    shared["kvaT"] = tr(inputs["L1_attn_kv_a_w"])
    shared["kvbT"] = tr(inputs["L1_attn_kv_b_w"])
    shared["owT"] = tr(inputs["L1_attn_o_w"])
    shared["qa_ln"] = np.ascontiguousarray(np.asarray(inputs["L1_attn_q_a_ln_w"], f32))
    shared["kva_ln"] = np.ascontiguousarray(np.asarray(inputs["L1_attn_kv_a_ln_w"], f32))
    shared["w1"] = np.ascontiguousarray(np.asarray(inputs["L1_ffn_w_gate_up"], f32))
    shared["w2"] = np.ascontiguousarray(np.asarray(inputs["L1_ffn_w_down"], f32))
    shared["ident"] = np.eye(128, dtype=f32)

    inv = 1.0 / (10000.0 ** (np.arange(0, ROPE, 2, dtype=np.float64) / ROPE))
    t = np.arange(LSEQ, dtype=np.float64)
    fr = np.outer(t, inv)
    emb = np.concatenate([fr, fr], -1)
    cos = np.cos(emb).astype(f32)
    sin = np.sin(emb).astype(f32)

    in_maps = []
    for c in range(NCORES):
        seq = c // 4
        a = c % 4
        b = 7 - a
        rowsA = slice(512 * a, 512 * a + 512)
        rowsB = slice(512 * b, 512 * b + 512)
        xo = np.concatenate([x[seq][rowsA], x[seq][rowsB]], 0)
        m = dict(shared)
        m["x0T"] = np.ascontiguousarray(xo.T)
        m["cosT"] = np.ascontiguousarray(np.concatenate([cos[rowsA], cos[rowsB]], 0).T)
        m["sinT"] = np.ascontiguousarray(np.concatenate([sin[rowsA], sin[rowsB]], 0).T)
        gtiles = [4 * a + j for j in range(4)] + [4 * b + j for j in range(4)]
        l0m = np.zeros((NQT, 128, NCH), f32)
        l1m = np.zeros((NQT, NBLK, 128, 512), f32)
        for qt, g in enumerate(gtiles):
            pos = 128 * g + np.arange(128)
            ends = (np.arange(NCH) + 1) * SCH
            l0m[qt] = np.where(pos[:, None] >= ends[None, :], 0.0, NEG)
            for kbi in range(NBLK):
                kpos = 512 * kbi + np.arange(512)
                l1m[qt, kbi] = np.where(pos[:, None] >= kpos[None, :], 0.0, NEG)
        m["l0mask"] = l0m
        m["l1mask"] = l1m
        in_maps.append(m)
    return in_maps


def kernel(**inputs):
    in_maps = _host_prep(inputs)
    nc = build_program()
    res = run_bass_kernel_spmd(nc, in_maps, core_ids=list(range(NCORES)))
    out = np.zeros((BB, LSEQ, DM), np.float32)
    for c in range(NCORES):
        seq = c // 4
        a = c % 4
        b = 7 - a
        oT = res.results[c]["outT"]
        out[seq, 512 * a:512 * a + 512] = oT[:, :512].T
        out[seq, 512 * b:512 * b + 512] = oT[:, 512:].T
    return out


# version 16 (final)
# speedup vs baseline: 1.1058x; 1.1058x over previous
import numpy as np
from contextlib import ExitStack

import concourse.bass as bass
import concourse.mybir as mybir
from concourse.bass_utils import run_bass_kernel_spmd

F32 = mybir.dt.float32
I32 = mybir.dt.int32
AF = mybir.ActivationFunctionType
ALU = mybir.AluOpType
AX = mybir.AxisListType

NCORES = 8
BB, LSEQ, DM = 2, 4096, 2048
TPC = 1024
NFT = DM // 128
NQT = TPC // 128
NBLK = 8
HKV, HD, NH, SCH, NCH, RDR = 4, 128, 16, 32, 128, 256
INVSQ_HD = 1.0 / np.sqrt(128.0)
INVSQ_RD = 1.0 / 16.0
EXPBIAS0 = -12.0
NEXP, RGD, IDIM, SIDIM = 32, 128, 1408, 2816
NIT = IDIM // 128
NST = SIDIM // 128
NOPE, ROPE, VD, QHD = 128, 64, 128, 192
QLORA, KVLORA = 1536, 512
SCALE1 = float(QHD) ** -0.5
DFF = 5632
NDT = DFF // 128
EPS0, EPS1 = 1e-5, 1e-6
NEG = -1.0e30
DBG_NORM = []


class KB:
    def __init__(self, nc, ctx):
        self.nc = nc
        self.ctx = ctx
        self.T, self.V, self.S, self.G, self.Y = (
            nc.tensor, nc.vector, nc.scalar, nc.gpsimd, nc.sync)
        self.engs = {'T': self.T, 'V': self.V, 'S': self.S,
                     'G': self.G, 'Y': self.Y}
        self.sems = {}
        self.cnt = {}
        self.lastw = {}
        for s in ('T', 'V', 'S', 'G', 'DY', 'DG', 'CC'):
            self.sems[s] = ctx.enter_context(nc.semaphore("sem_" + s))
            self.cnt[s] = 0
        self.psum = [ctx.enter_context(nc.psum_tensor(f"pb{i}", [128, 512], F32))
                     for i in range(8)]
        self.pfree = [None] * 8

    def sb(self, name, shape, scope=None, dtype=F32):
        return (scope or self.ctx).enter_context(
            self.nc.sbuf_tensor(name, shape, dtype))

    def barrier(self):
        for ek in ('T', 'V', 'S', 'G', 'Y'):
            for s in ('T', 'V', 'S', 'G', 'DY', 'DG'):
                self.wait(ek, (s, self.cnt[s]))

    def vsw(self, inst):
        t = self.mark(inst, 'V')
        self.wait('V', t)
        return t

    def ssw(self, inst):
        t = self.mark(inst, 'S')
        self.wait('S', t)
        return t

    def mark(self, inst, s, step=None):
        if step is None:
            step = 16 if s.startswith('D') else 1
        self.cnt[s] += step
        inst.then_inc(self.sems[s], step)
        return (s, self.cnt[s])

    def tok(self, s):
        return (s, self.cnt[s])

    def wait(self, ek, tok):
        if tok is None:
            return
        s, v = tok
        if v <= 0:
            return
        key = (ek, s)
        if self.lastw.get(key, 0) >= v:
            return
        self.engs[ek].wait_ge(self.sems[s], v)
        self.lastw[key] = v

    def waits(self, ek, toks):
        for t in toks:
            self.wait(ek, t)

    def bank(self, i):
        return self.psum[i]

    def bank_wait(self, i):
        self.wait('T', self.pfree[i])

    def bank_set_free(self, i, tok):
        self.pfree[i] = tok


def build_program(debug_taps=()):
    nc = bass.Bass("TRN2", target_bir_lowering=False)
    ctx = ExitStack()

    d = {}

    def din(name, shape, dtype=F32):
        d[name] = nc.dram_tensor(name, list(shape), dtype, kind="ExternalInput").ap()
        return d[name]

    def dout(name, shape, dtype=F32):
        d[name] = nc.dram_tensor(name, list(shape), dtype, kind="ExternalOutput").ap()
        return d[name]

    def dint(name, shape, dtype=F32):
        d[name] = nc.dram_tensor(name, list(shape), dtype).ap()
        return d[name]

    din("x0T", (DM, TPC))
    din("wkT", (DM, 512)); din("wvT", (DM, 512))
    din("wlmkT", (DM, RDR)); din("wqrT", (DM, RDR))
    din("wqT", (DM, DM)); din("woT", (DM, DM))
    din("g_pre", (DM,)); din("g_chunk", (DM,)); din("g_pre_retr", (DM,))
    din("g_q_retr", (RDR,)); din("g_lmk", (RDR,))
    din("g_k", (HD,)); din("g_qn", (HD,))
    din("gateT", (DM, NEXP))
    din("RgT", (DM, RGD)); din("RuT", (DM, RGD)); din("RdT", (IDIM, RGD))
    din("UgT", (NEXP, RGD, RGD)); din("UuT", (NEXP, RGD, RGD)); din("UdT", (NEXP, RGD, RGD))
    din("CgT", (NEXP, RGD, IDIM)); din("CuT", (NEXP, RGD, IDIM)); din("CdT", (NEXP, RGD, DM))
    din("sRgT", (DM, RGD)); din("sRuT", (DM, RGD))
    din("sUgT", (RGD, RGD)); din("sUuT", (RGD, RGD)); din("sUdT", (RGD, RGD))
    din("sCgT", (RGD, SIDIM)); din("sCuT", (RGD, SIDIM))
    din("sRdT", (SIDIM, RGD)); din("sCdT", (RGD, DM))
    din("qaT", (DM, QLORA)); din("qbT", (QLORA, NH * QHD))
    din("kvaT", (DM, KVLORA + ROPE)); din("kvbT", (KVLORA, NH * (NOPE + VD)))
    din("owT", (DM, DM))
    din("qa_ln", (QLORA,)); din("kva_ln", (KVLORA,))
    din("w1", (2 * DFF, DM)); din("w2", (DM, DFF))
    din("ident", (128, 128))
    din("cosT", (ROPE, TPC)); din("sinT", (ROPE, TPC))
    din("l0mask", (NQT, 128, NCH))
    din("l1mask", (NQT, NBLK, 128, 512))
    dout("outT", (DM, TPC))
    dint("ag0k_in", (512, TPC)); dint("ag0k_out", (4, 512, TPC))
    dint("ag0v_in", (TPC, 512)); dint("ag0v_out", (4, TPC, 512))
    dint("ag0l_in", (RDR, 32)); dint("ag0l_out", (4, RDR, 32))
    dint("ag1_in", (KVLORA + ROPE, TPC)); dint("ag1_out", (4, KVLORA + ROPE, TPC))
    dint("qa_spill", (QLORA, TPC))
    dint("xq_spill", (DM, TPC))
    dint("xsum_spill", (DM, TPC))
    dint("qw1", (2 * NDT, DM, 128)); dint("s1", (2 * NDT, 128))
    dint("qw2", (NFT, DFF, 128)); dint("s2", (NFT, 128))

    kb = KB(nc, ctx)
    with ctx:
        with nc.Block() as _blk:
            _body(nc, kb, d, debug_taps, dout)
    return nc


def _body(nc, kb, d, debug_taps, dout):
    T, V, S, G, Y = kb.T, kb.V, kb.S, kb.G, kb.Y

    def tap(name, sbuf_ap, tok):
        if name not in debug_taps:
            return
        o = dout("dbg_" + name, list(sbuf_ap.shape))
        kb.wait('Y', tok)
        t = kb.mark(Y.dma_start(out=o[tuple(slice(None) for _ in sbuf_ap.shape)],
                                in_=sbuf_ap), 'DY')
        kb.wait('V', t)
        kb.wait('S', t)
        kb.wait('T', t)
        kb.wait('G', t)

    xsum = kb.sb("xsum", [128, NFT * TPC])
    rb = kb.sb("rb", [128, TPC])
    mb = kb.sb("mb", [128, TPC])
    stat = kb.sb("stat", [128, 32])
    stat2 = kb.sb("stat2", [128, 32])
    ones_col = kb.sb("ones_col", [128, 1])
    onesall = kb.sb("onesall", [1, 128])
    ident = kb.sb("identb", [128, 128])
    row1 = kb.sb("row1", [1, 1024])
    gain = kb.sb("gain", [128, 64])
    gl1 = kb.sb("gl1", [128, 16])
    sq = [kb.sb(f"sqt{i}", [128, 512]) for i in range(2)]

    cst = kb.sb("cst", [128, 4])
    tident = kb.mark(Y.dma_start(out=ident[:], in_=d["ident"][:, :]), 'DY')
    kb.wait('V', tident)
    V.memset(ones_col[:], 1.0)
    V.memset(cst[:, 0:1], EXPBIAS0)
    tones = kb.mark(V.memset(onesall[:], 1.0), 'V')

    for i in range(NFT):
        kb.mark(Y.dma_start(out=gain[:, i:i + 1],
                    in_=d["g_pre"][i * 128:(i + 1) * 128].rearrange("(p o) -> p o", o=1)), 'DY')
        kb.mark(Y.dma_start(out=gain[:, 16 + i:17 + i],
                    in_=d["g_chunk"][i * 128:(i + 1) * 128].rearrange("(p o) -> p o", o=1)), 'DY')
        kb.mark(Y.dma_start(out=gain[:, 32 + i:33 + i],
                    in_=d["g_pre_retr"][i * 128:(i + 1) * 128].rearrange("(p o) -> p o", o=1)), 'DY')
    for i in range(2):
        kb.mark(Y.dma_start(out=gain[:, 48 + i:49 + i],
                    in_=d["g_q_retr"][i * 128:(i + 1) * 128].rearrange("(p o) -> p o", o=1)), 'DY')
        kb.mark(Y.dma_start(out=gain[:, 50 + i:51 + i],
                    in_=d["g_lmk"][i * 128:(i + 1) * 128].rearrange("(p o) -> p o", o=1)), 'DY')
    kb.mark(Y.dma_start(out=gain[:, 53:54], in_=d["g_k"][:].rearrange("(p o) -> p o", o=1)), 'DY')
    tgn = kb.mark(Y.dma_start(out=gain[:, 54:55],
                              in_=d["g_qn"][:].rearrange("(p o) -> p o", o=1)), 'DY')
    for i in range(12):
        kb.mark(Y.dma_start(out=gl1[:, i:i + 1],
                    in_=d["qa_ln"][i * 128:(i + 1) * 128].rearrange("(p o) -> p o", o=1)), 'DY')
    for i in range(4):
        tgl = kb.mark(Y.dma_start(out=gl1[:, 12 + i:13 + i],
                                  in_=d["kva_ln"][i * 128:(i + 1) * 128].rearrange(
                                      "(p o) -> p o", o=1)), 'DY')
    kb.wait('V', tgn)
    tgc = kb.mark(V.tensor_tensor(gain[:, 52:53], gain[:, 53:54], gain[:, 54:55],
                                  op=ALU.mult), 'V')

    def XT(buf, i):
        return buf[:, i * TPC:(i + 1) * TPC]

    def XTc(buf, i, c0, n, stride=TPC):
        return buf[:, i * stride + c0: i * stride + c0 + n]

    sqfree = [None, None]
    guard = {'row': None, 'stat': None}

    def norm_stats(get_tile, idx, nj, kind, eps, stat_cols=0, src_tok=None):
        sum_bank, sq_bank = 6, 7
        kb.wait('V', guard['stat'])
        kb.wait('S', src_tok)
        kb.wait('T', src_tok)
        for j in range(nj):
            for k, i in enumerate(idx):
                ap = get_tile(i, j)
                sqt = sq[k % 2][:, 0:128]
                kb.wait('S', sqfree[k % 2])
                ts = kb.mark(S.activation(sqt, ap, AF.Square), 'S')
                kb.wait('T', ts)
                if k == 0:
                    kb.bank_wait(sum_bank)
                inst = T.matmul(kb.bank(sum_bank)[:, 0:1], sqt, ones_col[:],
                                start=(k == 0), stop=(k == len(idx) - 1))
                sqfree[k % 2] = kb.mark(inst, 'T')
            tmm = sqfree[(len(idx) - 1) % 2]
            kb.wait('V', tmm)
            tv = kb.mark(V.tensor_copy(stat[:, stat_cols + j:stat_cols + j + 1],
                                       kb.bank(sum_bank)[:, 0:1]), 'V')
            kb.bank_set_free(sum_bank, tv)
            if kind == 'ln':
                for k, i in enumerate(idx):
                    if k == 0:
                        kb.bank_wait(sq_bank)
                    inst = T.matmul(kb.bank(sq_bank)[:, 0:1], get_tile(i, j),
                                    ones_col[:], start=(k == 0), stop=(k == len(idx) - 1))
                tmm = kb.mark(inst, 'T')
                kb.wait('V', tmm)
                tv2 = kb.mark(V.tensor_copy(stat2[:, j:j + 1],
                                            kb.bank(sq_bank)[:, 0:1]), 'V')
                kb.bank_set_free(sq_bank, tv2)
        if DBG_NORM and DBG_NORM[0] == 'pending':
            DBG_NORM[0] = 'done'
            o1 = nc.dram_tensor("dbg_rawsums", [128, 64], F32, kind="ExternalOutput").ap()
            kb.wait('Y', kb.tok('V'))
            tdn = kb.mark(Y.dma_start(out=o1[:, 0:32], in_=stat[:]), 'DY')
            tdn = kb.mark(Y.dma_start(out=o1[:, 32:64], in_=stat2[:]), 'DY')
            kb.wait('V', tdn); kb.wait('T', tdn); kb.wait('S', tdn)
        nfeat = float(len(idx) * 128)
        sc = stat[:, stat_cols:stat_cols + nj]
        tmp = stat[:, 24:24 + nj]
        W = kb.vsw
        kb.wait('V', kb.tok('V'))
        if DBG_NORM and DBG_NORM[0] == 'postdiv':
            DBG_NORM[0] = 'done'
            o3 = nc.dram_tensor("dbg_postdiv", [128, 64], F32, kind="ExternalOutput").ap()
            kb.wait('Y', kb.tok('V'))
            tdn = kb.mark(Y.dma_start(out=o3[:, 0:32], in_=stat[:]), 'DY')
            tdn = kb.mark(Y.dma_start(out=o3[:, 32:64], in_=stat2[:]), 'DY')
            kb.wait('V', tdn); kb.wait('T', tdn); kb.wait('S', tdn)
        if kind == 'ln':
            W(V.tensor_scalar(stat2[:, 0:nj], stat2[:, 0:nj], 1.0 / nfeat, None, op0=ALU.mult))
            W(V.tensor_scalar(sc, sc, 1.0 / nfeat, None, op0=ALU.mult))
            W(V.tensor_tensor(tmp, stat2[:, 0:nj], stat2[:, 0:nj], op=ALU.mult))
            W(V.tensor_tensor(sc, sc, tmp, op=ALU.subtract))
        else:
            W(V.tensor_scalar(sc, sc, 1.0 / nfeat, None, op0=ALU.mult))
        tv = W(V.tensor_scalar(sc, sc, eps, None, op0=ALU.add))
        if DBG_NORM and DBG_NORM[0] == 'postvar':
            DBG_NORM[0] = 'done'
            o4 = nc.dram_tensor("dbg_postvar", [128, 64], F32, kind="ExternalOutput").ap()
            kb.wait('Y', tv)
            tdn = kb.mark(Y.dma_start(out=o4[:, 0:32], in_=stat[:]), 'DY')
            tdn = kb.mark(Y.dma_start(out=o4[:, 32:64], in_=stat2[:]), 'DY')
            kb.wait('V', tdn); kb.wait('T', tdn); kb.wait('S', tdn)
        t2 = stat2[:, 24:24 + nj]
        kb.wait('S', tv)
        t1 = kb.mark(S.activation(tmp, sc, AF.Sqrt), 'S')
        kb.wait('V', t1)
        W(V.reciprocal(t2, tmp))
        W(V.tensor_tensor(tmp, sc, t2, op=ALU.mult))
        W(V.tensor_tensor(tmp, tmp, t2, op=ALU.mult))
        W(V.tensor_scalar(tmp, tmp, -0.5, 1.5, op0=ALU.mult, op1=ALU.add))
        tv = W(V.tensor_tensor(sc, t2, tmp, op=ALU.mult))
        if kind == 'ln':
            tv = W(V.tensor_tensor(stat2[:, 0:nj], stat2[:, 0:nj], sc,
                                   op=ALU.mult))
        if DBG_NORM and DBG_NORM[0] == 'postmath':
            DBG_NORM[0] = 'done'
            o2 = nc.dram_tensor("dbg_postmath", [128, 64], F32, kind="ExternalOutput").ap()
            kb.wait('Y', tv)
            tdn = kb.mark(Y.dma_start(out=o2[:, 0:32], in_=stat[:]), 'DY')
            tdn = kb.mark(Y.dma_start(out=o2[:, 32:64], in_=stat2[:]), 'DY')
            kb.wait('V', tdn); kb.wait('T', tdn); kb.wait('S', tdn)
        return tv

    def bcast_stats(stat_ap, nj, out_bcast, stat_tok):
        kb.wait('T', stat_tok)
        tok = None
        for j0 in range(0, nj, 4):
            jn = min(4, nj - j0)
            kb.bank_wait(5)
            for j in range(j0, j0 + jn):
                kb.bank_wait(4)
                tT = kb.mark(T.transpose(kb.bank(4)[0:1, 0:128],
                                         stat_ap[:, j:j + 1], ident[:]), 'T')
                kb.wait('V', tT)
                kb.wait('V', guard['row'])
                trw = kb.mark(V.tensor_copy(row1[0:1, j * 128:(j + 1) * 128],
                                            kb.bank(4)[0:1, 0:128]), 'V')
                kb.bank_set_free(4, trw)
                kb.wait('T', trw)
                inst = T.matmul(kb.bank(5)[:, (j - j0) * 128:(j - j0 + 1) * 128],
                                onesall[0:1, 0:128],
                                row1[0:1, j * 128:(j + 1) * 128],
                                start=True, stop=True)
            tM = kb.mark(inst, 'T')
            guard['row'] = tM
            kb.wait('V', tM)
            tok = kb.mark(V.tensor_copy(out_bcast[:, j0 * 128:(j0 + jn) * 128],
                                        kb.bank(5)[:, 0:jn * 128]), 'V')
            kb.bank_set_free(5, tok)
        guard['stat'] = kb.tok('T')
        return tok

    def norm_bcast(get_tile, idx, nj, kind, eps, src_tok=None):
        t = norm_stats(get_tile, idx, nj, kind, eps, src_tok=src_tok)
        t1 = bcast_stats(stat[:, 0:nj], nj, rb[:, 0:nj * 128], t)
        t2 = None
        if kind == 'ln':
            t2 = bcast_stats(stat2[:, 0:nj], nj, mb[:, 0:nj * 128], t)
        return t1, t2

    def load_wcols(dst, wdram, nin, c0, w, row0=0):
        tl = None
        for i in range(nin):
            tl = kb.mark(Y.dma_start(
                out=dst[:, i * w:(i + 1) * w],
                in_=wdram[row0 + i * 128: row0 + (i + 1) * 128, c0:c0 + w]), 'DY')
        return tl

    ag0_done = _l0(nc, kb, d, tap, XT, XTc, norm_stats, bcast_stats, norm_bcast,
                   load_wcols, xsum, rb, mb, stat, stat2, ones_col, onesall,
                   ident, row1, gain, sq, tgc, tones, cst, debug_taps)

    _l1(nc, kb, d, tap, XT, XTc, norm_stats, bcast_stats, load_wcols, xsum,
        rb, mb, stat, stat2, ones_col, onesall, ident, gl1, sq, cst)
    Y.wait_ge(kb.sems['DY'], kb.cnt['DY'])


def _l0(nc, kb, d, tap, XT, XTc, norm_stats, bcast_stats, norm_bcast,
        load_wcols, xsum, rb, mb, stat, stat2, ones_col, onesall, ident,
        row1, gain, sq, tgc, tones, cst, debug_taps=()):
    T, V, S, G, Y = kb.T, kb.V, kb.S, kb.G, kb.Y

    l0sc = ExitStack()
    with l0sc:
        we_tm = kb.sb("we_tm", [128, NQT * NEXP], l0sc)
        tweT_tok = [None]
        rg = kb.sb("rg", [128, TPC], l0sc)
        ru = kb.sb("ru", [128, TPC], l0sc)

        xw_sc = ExitStack()
        with xw_sc:
            xwork = kb.sb("xwork", [128, NFT * TPC], xw_sc)
            w_all = kb.sb("w_all", [128, NQT * NCH], xw_sc)
            qstat = kb.sb("qstat", [128, NQT], xw_sc)

            tx0 = None
            for i in range(NFT):
                tx0 = kb.mark(Y.dma_start(out=XT(xsum, i),
                                          in_=d["x0T"][i * 128:(i + 1) * 128, :]), 'DY')
            kb.wait('S', tx0); kb.wait('T', tx0); kb.wait('T', tones)
            tap("x0raw", xsum[:, 0:NFT * TPC], tx0)

            tln = norm_bcast(lambda i, j: XTc(xsum, i, j * 128, 128),
                             list(range(NFT)), NQT, 'ln', EPS0, src_tok=tx0)
            kb.waits('V', [t for t in tln if t])
            tap("rbln", rb[:, 0:TPC], tln[0])
            tap("statln", stat[:, 0:32], tln[0])
            tap("stat2ln", stat2[:, 0:32], tln[0])
            if "STOPLN" in debug_taps:
                return
            tap("mbln", mb[:, 0:TPC], tln[1])
            for i in range(NFT):
                V.tensor_tensor(XT(xwork, i), XT(xsum, i), rb[:], op=ALU.mult)
                V.tensor_tensor(XT(xwork, i), XT(xwork, i), mb[:], op=ALU.subtract)
                tv = kb.mark(V.tensor_tensor(XT(xsum, i), XT(xsum, i), XT(xwork, i),
                                             op=ALU.add), 'V')
            txn0 = tv
            tap("xn0", xwork[:, 0:TPC], txn0)
            tap("xsum1", xsum[:, 0:TPC], txn0)

            p1b = ExitStack()
            with p1b:
                gateW = kb.sb("gateW", [128, NFT * 32], p1b)
                lg = kb.sb("lg", [128, NEXP], p1b)
                swork = kb.sb("swork", [128, NEXP], p1b)
                m8b = kb.sb("m8b", [128, 8], p1b)
                tgw = load_wcols(gateW, d["gateT"], NFT, 0, 32)
                kb.wait('T', txn0)
                kb.wait('T', tgw)
                for j in range(NQT):
                    bnk = 0
                    kb.bank_wait(bnk)
                    for i in range(NFT):
                        inst = T.matmul(kb.bank(bnk)[:, 0:NEXP],
                                        XTc(xwork, i, j * 128, 128),
                                        gateW[:, i * 32:(i + 1) * 32],
                                        start=(i == 0), stop=(i == NFT - 1))
                    tmm = kb.mark(inst, 'T')
                    kb.wait('V', tmm)
                    kb.vsw(V.reduce_max(stat[:, 0:1], kb.bank(bnk)[:, 0:NEXP], axis=AX.X))
                    tvm = kb.vsw(V.tensor_scalar(stat[:, 1:2], stat[:, 0:1], -1.0,
                                                 None, op0=ALU.mult))
                    kb.wait('S', tvm)
                    te = kb.mark(S.activation(lg[:], kb.bank(bnk)[:, 0:NEXP], AF.Exp,
                                              bias=stat[:, 1:2]), 'S')
                    kb.bank_set_free(bnk, te)
                    kb.wait('V', te)
                    kb.vsw(V.reduce_sum(stat[:, 2:3], lg[:], axis=AX.X))
                    kb.vsw(V.reciprocal(stat[:, 4:5], stat[:, 2:3]))
                    kb.vsw(V.tensor_scalar(lg[:], lg[:], stat[:, 4:5], None, op0=ALU.mult))
                    kb.vsw(V.max(out=m8b[:], in_=lg[:]))
                    kb.vsw(V.tensor_scalar(swork[:], lg[:], m8b[:, 3:4], None, op0=ALU.is_ge))
                    kb.vsw(V.tensor_tensor(swork[:], swork[:], lg[:], op=ALU.mult))
                    kb.vsw(V.reduce_sum(stat[:, 3:4], swork[:], axis=AX.X))
                    kb.vsw(V.tensor_scalar(stat[:, 3:4], stat[:, 3:4], 1e-20, None, op0=ALU.add))
                    kb.vsw(V.reciprocal(stat[:, 5:6], stat[:, 3:4]))
                    tv = kb.mark(V.tensor_scalar(we_tm[:, j * 32:(j + 1) * 32],
                                                 swork[:], stat[:, 5:6],
                                                 None, op0=ALU.mult), 'V')
                tweT = tv
                tweT_tok[0] = tweT
                tap("weT", we_tm[:, 0:NQT * 32], tweT)

                rgw = kb.sb("rgw", [128, NFT * 128], p1b)
                ruw = kb.sb("ruw", [128, NFT * 128], p1b)
                load_wcols(rgw, d["RgT"], NFT, 0, 128)
                twl = load_wcols(ruw, d["RuT"], NFT, 0, 128)
                kb.wait('T', twl)
                for qc in range(2):
                    for (dst, wsb, bnk) in ((rg, rgw, 0), (ru, ruw, 1)):
                        kb.bank_wait(bnk)
                        for i in range(NFT):
                            inst = T.matmul(kb.bank(bnk)[:, 0:512],
                                            wsb[:, i * 128:(i + 1) * 128],
                                            XTc(xwork, i, qc * 512, 512),
                                            start=(i == 0), stop=(i == NFT - 1))
                        tmm = kb.mark(inst, 'T')
                        kb.wait('V', tmm)
                        tv = kb.mark(V.tensor_copy(dst[:, qc * 512:(qc + 1) * 512],
                                                   kb.bank(bnk)[:, 0:512]), 'V')
                        kb.bank_set_free(bnk, tv)
                trgru = tv
                tap("rg", rg[:], trgru)

                sml = kb.sb("sml", [128, 3 * 128], p1b)
                kb.mark(Y.dma_start(out=sml[:, 0:128], in_=d["sUgT"][:, :]), 'DY')
                kb.mark(Y.dma_start(out=sml[:, 128:256], in_=d["sUuT"][:, :]), 'DY')
                tsm = kb.mark(Y.dma_start(out=sml[:, 256:384], in_=d["sUdT"][:, :]), 'DY')
                g2 = kb.sb("g2s", [128, 512], p1b)
                u2 = kb.sb("u2s", [128, 512], p1b)
                got = kb.sb("gots", [128, 512], p1b)
                uot = kb.sb("uots", [128, 512], p1b)
                rd2 = kb.sb("rd2s", [128, 512], p1b)
                wsl = [kb.sb(f"wsl{i}", [128, 128], p1b) for i in range(4)]
                wslf = [None] * 4
                kb.wait('T', tsm)
                tva = None
                tgot_free = None
                for qc in range(2):
                    for (dst, wd, bnk) in ((g2, "sRgT", 0), (u2, "sRuT", 1)):
                        kb.wait('Y', kb.tok('T'))
                        twl = load_wcols(rgw if dst is g2 else ruw, d[wd], NFT, 0, 128)
                        kb.bank_wait(bnk)
                        kb.wait('T', twl)
                        for i in range(NFT):
                            inst = T.matmul(kb.bank(bnk)[:, 0:512],
                                            (rgw if dst is g2 else ruw)[:, i * 128:(i + 1) * 128],
                                            XTc(xwork, i, qc * 512, 512),
                                            start=(i == 0), stop=(i == NFT - 1))
                        tmm = kb.mark(inst, 'T')
                        kb.wait('S', tmm)
                        tv = kb.mark(S.copy(dst[:], kb.bank(bnk)[:, 0:512]), 'S')
                        kb.bank_set_free(bnk, tv)
                    for (dst, wofs, bnk) in ((g2, 0, 0), (u2, 128, 1)):
                        kb.bank_wait(bnk)
                        kb.wait('T', tv)
                        tmm = kb.mark(T.matmul(kb.bank(bnk)[:, 0:512],
                                               sml[:, wofs:wofs + 128], dst[:],
                                               start=True, stop=True), 'T')
                        kb.wait('S', tmm)
                        tv = kb.mark(S.copy(dst[:], kb.bank(bnk)[:, 0:512]), 'S')
                        kb.bank_set_free(bnk, tv)
                    kb.bank_wait(2)
                    for ot in range(NST):
                        kb.wait('T', tv)
                        kb.wait('Y', wslf[0])
                        t0 = kb.mark(Y.dma_start(out=wsl[0][:],
                                                 in_=d["sCgT"][:, ot * 128:(ot + 1) * 128]), 'DY')
                        kb.wait('Y', wslf[1])
                        t1 = kb.mark(Y.dma_start(out=wsl[1][:],
                                                 in_=d["sCuT"][:, ot * 128:(ot + 1) * 128]), 'DY')
                        kb.wait('Y', wslf[2])
                        t2 = kb.mark(Y.dma_start(out=wsl[2][:],
                                                 in_=d["sRdT"][ot * 128:(ot + 1) * 128, :]), 'DY')
                        bg, bu = 0, 1
                        kb.bank_wait(bg)
                        kb.wait('T', t0)
                        tg = kb.mark(T.matmul(kb.bank(bg)[:, 0:512], wsl[0][:], g2[:],
                                              start=True, stop=True), 'T')
                        wslf[0] = tg
                        kb.bank_wait(bu)
                        kb.wait('T', t1)
                        tu = kb.mark(T.matmul(kb.bank(bu)[:, 0:512], wsl[1][:], u2[:],
                                              start=True, stop=True), 'T')
                        wslf[1] = tu
                        kb.wait('S', tg)
                        kb.wait('S', tgot_free)
                        tsg = kb.mark(S.activation(got[:], kb.bank(bg)[:, 0:512], AF.Silu), 'S')
                        kb.bank_set_free(bg, tsg)
                        kb.wait('V', tu)
                        kb.wait('V', tgot_free)
                        tsu = kb.mark(V.tensor_copy(uot[:], kb.bank(bu)[:, 0:512]), 'V')
                        kb.bank_set_free(bu, tsu)
                        kb.wait('V', tsg)
                        tgu = kb.mark(V.tensor_tensor(got[:], got[:], uot[:], op=ALU.mult), 'V')
                        kb.wait('T', tgu)
                        kb.wait('T', t2)
                        inst = T.matmul(kb.bank(2)[:, 0:512], wsl[2][:], got[:],
                                        start=(ot == 0), stop=(ot == NST - 1))
                        tgot_free = kb.mark(inst, 'T')
                        wslf[2] = tgot_free
                    kb.wait('S', tgot_free)
                    tv = kb.mark(S.copy(rd2[:], kb.bank(2)[:, 0:512]), 'S')
                    kb.bank_set_free(2, tv)
                    kb.bank_wait(3)
                    kb.wait('T', tv)
                    tmm = kb.mark(T.matmul(kb.bank(3)[:, 0:512], sml[:, 256:384], rd2[:],
                                           start=True, stop=True), 'T')
                    kb.wait('S', tmm)
                    tv = kb.mark(S.copy(rd2[:], kb.bank(3)[:, 0:512]), 'S')
                    kb.bank_set_free(3, tv)
                    for ot in range(NFT):
                        bnk = 4 + (ot % 2)
                        kb.wait('Y', wslf[3])
                        t3 = kb.mark(Y.dma_start(out=wsl[3][:],
                                                 in_=d["sCdT"][:, ot * 128:(ot + 1) * 128]), 'DY')
                        kb.bank_wait(bnk)
                        kb.wait('T', tv); kb.wait('T', t3)
                        tmm = kb.mark(T.matmul(kb.bank(bnk)[:, 0:512], wsl[3][:], rd2[:],
                                               start=True, stop=True), 'T')
                        wslf[3] = tmm
                        kb.wait('V', tmm)
                        tva = kb.mark(V.tensor_tensor(XTc(xsum, ot, qc * 512, 512),
                                                      XTc(xsum, ot, qc * 512, 512),
                                                      kb.bank(bnk)[:, 0:512], op=ALU.add), 'V')
                        kb.bank_set_free(bnk, tva)
                tap("xsum_sh", xsum[:, 0:TPC], tva)

            tnf = norm_bcast(lambda i, j: XTc(xwork, i, j * 128, 128),
                             list(range(NFT)), NQT, 'rms', EPS0, src_tok=txn0)
            kb.waits('V', [tnf[0]])
            for i in range(NFT):
                V.tensor_tensor(XT(xwork, i), XT(xwork, i), rb[:], op=ALU.mult)
                tv = kb.mark(V.tensor_scalar(XT(xwork, i), XT(xwork, i),
                                             gain[:, i:i + 1], None, op0=ALU.mult), 'V')
            txp = tv
            tap("xp", xwork[:, 0:TPC], txp)

            p1d = ExitStack()
            with p1d:
                CW = 128
                NCC = TPC // CW
                xc = kb.sb("xc", [128, NFT * CW], p1d)
                kown = kb.sb("kown", [128, 4 * TPC], p1d)
                wkv = kb.sb("wkv", [128, NFT * 128], p1d)
                vw = [kb.sb(f"vw{i}", [128, 512], p1d) for i in range(2)]
                vbc = kb.sb("vbc", [128, 512], p1d)
                xcm = kb.sb("xcm", [128, NFT * 32], p1d)
                lmr = kb.sb("lmr", [128, 2 * 32], p1d)
                lmn = kb.sb("lmn", [128, 2 * 32], p1d)
                twk_free = None
                tvd = None
                vwf = [None, None]
                kb.barrier()
                for cc in range(NCC):
                    c0 = cc * CW
                    tnf = norm_stats(lambda i, j: XTc(xwork, i, c0 + j * 128, 128),
                                     list(range(NFT)), CW // 128, 'rms', EPS0,
                                     src_tok=txp)
                    tnb = bcast_stats(stat[:, 0:CW // 128], CW // 128, rb[:, 0:CW], tnf)
                    kb.wait('V', tnb)
                    for i in range(NFT):
                        V.tensor_tensor(XTc(xc, i, 0, CW, CW), XTc(xwork, i, c0, CW),
                                        rb[:, 0:CW], op=ALU.mult)
                        tv = kb.mark(V.tensor_scalar(XTc(xc, i, 0, CW, CW),
                                                     XTc(xc, i, 0, CW, CW),
                                                     gain[:, 16 + i:17 + i], None,
                                                     op0=ALU.mult), 'V')
                    txc = tv
                    if cc == 0:
                        tap("xc", xc[:, 0:CW], txc)
                    kb.wait('V', txc)
                    nch_cc = CW // SCH
                    for i in range(NFT):
                        tv = kb.mark(V.reduce_sum(
                            xcm[:, i * 32 + cc * nch_cc: i * 32 + (cc + 1) * nch_cc].rearrange(
                                "p (c o) -> p c o", o=1),
                            XTc(xc, i, 0, CW, CW).rearrange("p (c s) -> p c s", s=SCH),
                            axis=AX.X), 'V')
                    for ot in range(4):
                        kb.wait('Y', twk_free)
                        twk = load_wcols(wkv, d["wkT"], NFT, ot * 128, 128)
                        bnk = 0
                        kb.bank_wait(bnk)
                        kb.wait('T', twk); kb.wait('T', txc)
                        for i in range(NFT):
                            inst = T.matmul(kb.bank(bnk)[:, 0:CW],
                                            wkv[:, i * 128:(i + 1) * 128],
                                            XTc(xc, i, 0, CW, CW),
                                            start=(i == 0), stop=(i == NFT - 1))
                        tmm = kb.mark(inst, 'T')
                        twk_free = tmm
                        kb.wait('V', tmm)
                        tv = kb.mark(V.tensor_copy(kown[:, ot * TPC + c0: ot * TPC + c0 + CW],
                                                   kb.bank(bnk)[:, 0:CW]), 'V')
                        kb.bank_set_free(bnk, tv)
                    for tt in range(CW // 128):
                        bnk = 1
                        kb.bank_wait(bnk)
                        for i in range(NFT):
                            b2 = i % 2
                            kb.wait('Y', vwf[b2])
                            tw = kb.mark(Y.dma_start(out=vw[b2][:],
                                                     in_=d["wvT"][i * 128:(i + 1) * 128, :]),
                                         'DY')
                            kb.wait('T', tw)
                            inst = T.matmul(kb.bank(bnk)[:, 0:512],
                                            XTc(xc, i, tt * 128, 128, CW),
                                            vw[b2][:],
                                            start=(i == 0), stop=(i == NFT - 1))
                            vwf[b2] = kb.mark(inst, 'T')
                        tmm = vwf[(NFT - 1) % 2]
                        kb.wait('V', tmm)
                        kb.wait('V', tvd)
                        tv = kb.mark(V.tensor_copy(vbc[:], kb.bank(bnk)[:, 0:512]), 'V')
                        kb.bank_set_free(bnk, tv)
                        kb.wait('Y', tv)
                        tvd = kb.mark(Y.dma_start(
                            out=d["ag0v_in"][c0 + tt * 128: c0 + (tt + 1) * 128, :],
                            in_=vbc[:]), 'DY')
                tv = kb.vsw(V.tensor_scalar(xcm[:], xcm[:], 1.0 / SCH, None,
                                            op0=ALU.mult))
                txcm = tv
                tkd = None
                for ot in range(4):
                    tnf = norm_stats(lambda i, j: XTc(kown, ot, j * 128, 128),
                                     [ot], NQT, 'rms', EPS0, src_tok=tv)
                    tnb = bcast_stats(stat[:, 0:NQT], NQT, rb[:, 0:TPC], tnf)
                    kb.wait('V', tnb)
                    V.tensor_tensor(XT(kown, ot), XT(kown, ot), rb[:], op=ALU.mult)
                    kb.wait('V', tgc)
                    tv = kb.mark(V.tensor_scalar(XT(kown, ot), XT(kown, ot),
                                                 gain[:, 52:53], None, op0=ALU.mult), 'V')
                    kb.wait('Y', tv)
                    tkd = kb.mark(Y.dma_start(out=d["ag0k_in"][ot * 128:(ot + 1) * 128, :],
                                              in_=XT(kown, ot)), 'DY')
                tap("kown", kown[:, 0:TPC], tv)
                kb.wait('T', txcm)
                for ot in range(2):
                    kb.wait('Y', twk_free)
                    tlm = load_wcols(wkv, d["wlmkT"], NFT, ot * 128, 128)
                    kb.wait('T', tlm)
                    bnk = 0
                    kb.bank_wait(bnk)
                    for i in range(NFT):
                        inst = T.matmul(kb.bank(bnk)[:, 0:32],
                                        wkv[:, i * 128:(i + 1) * 128],
                                        xcm[:, i * 32:(i + 1) * 32],
                                        start=(i == 0), stop=(i == NFT - 1))
                    tmm = kb.mark(inst, 'T')
                    twk_free = tmm
                    kb.wait('V', tmm)
                    tv = kb.mark(V.tensor_copy(lmr[:, ot * 32:(ot + 1) * 32],
                                               kb.bank(bnk)[:, 0:32]), 'V')
                    kb.bank_set_free(bnk, tv)
                kb.barrier()
                kb.wait('S', tv)
                S.activation(sq[0][:, 0:32], lmr[:, 0:32], AF.Square)
                ts2 = kb.mark(S.activation(sq[0][:, 32:64], lmr[:, 32:64], AF.Square), 'S')
                kb.bank_wait(6)
                kb.wait('T', ts2)
                T.matmul(kb.bank(6)[0:1, 0:32], ones_col[:], sq[0][:, 0:32],
                         start=True, stop=False)
                tmm = kb.mark(T.matmul(kb.bank(6)[0:1, 0:32], ones_col[:],
                                       sq[0][:, 32:64], start=False, stop=True), 'T')
                kb.wait('V', tmm)
                kb.vsw(V.tensor_scalar(sq[1][0:1, 0:32], kb.bank(6)[0:1, 0:32], 1.0 / RDR,
                                       EPS0, op0=ALU.mult, op1=ALU.add))
                tv = kb.vsw(V.tensor_copy(sq[1][0:1, 32:64], sq[1][0:1, 0:32]))
                kb.bank_set_free(6, tv)
                kb.wait('S', tv)
                t1 = kb.mark(S.activation(sq[1][0:1, 32:64], sq[1][0:1, 0:32], AF.Sqrt), 'S')
                kb.wait('V', t1)
                kb.vsw(V.reciprocal(sq[1][0:1, 64:96], sq[1][0:1, 32:64]))
                kb.vsw(V.tensor_tensor(sq[1][0:1, 32:64], sq[1][0:1, 0:32], sq[1][0:1, 64:96],
                                       op=ALU.mult))
                kb.vsw(V.tensor_tensor(sq[1][0:1, 32:64], sq[1][0:1, 32:64], sq[1][0:1, 64:96],
                                       op=ALU.mult))
                kb.vsw(V.tensor_scalar(sq[1][0:1, 32:64], sq[1][0:1, 32:64], -0.5, 1.5,
                                       op0=ALU.mult, op1=ALU.add))
                trr = kb.vsw(V.tensor_tensor(sq[1][0:1, 0:32], sq[1][0:1, 64:96],
                                             sq[1][0:1, 32:64], op=ALU.mult))
                kb.barrier()
                kb.bank_wait(5)
                kb.wait('T', trr)
                tbm = kb.mark(T.matmul(kb.bank(5)[:, 0:32], onesall[0:1, 0:128],
                                       sq[1][0:1, 0:32], start=True, stop=True), 'T')
                kb.barrier()
                kb.wait('V', tbm)
                for ot in range(2):
                    kb.vsw(V.tensor_tensor(lmn[:, ot * 32:(ot + 1) * 32],
                                           lmr[:, ot * 32:(ot + 1) * 32],
                                           kb.bank(5)[:, 0:32], op=ALU.mult))
                    tv = kb.vsw(V.tensor_scalar(lmn[:, ot * 32:(ot + 1) * 32],
                                                lmn[:, ot * 32:(ot + 1) * 32],
                                                gain[:, 50 + ot:51 + ot], None,
                                                op0=ALU.mult))
                kb.bank_set_free(5, tv)
                tap("xcm", xcm[:, 0:NFT * 32], txcm)
                tap("lmr", lmr[:, 0:64], tv)
                tap("lmn", lmn[:, 0:64], tv)
                kb.barrier()
                kb.wait('Y', tv)
                tld = None
                for ot in range(2):
                    tld = kb.mark(Y.dma_start(out=d["ag0l_in"][ot * 128:(ot + 1) * 128, :],
                                              in_=lmn[:, ot * 32:(ot + 1) * 32]), 'DY')
                kb.wait('G', kb.tok('DY'))
                G.collective_compute("AllGather", ALU.bypass,
                                     replica_groups=[[0, 1, 2, 3], [4, 5, 6, 7]],
                                     ins=[d["ag0k_in"][:, :]],
                                     outs=[d["ag0k_out"][:, :, :]]).then_inc(kb.sems['CC'], 1)
                G.collective_compute("AllGather", ALU.bypass,
                                     replica_groups=[[0, 1, 2, 3], [4, 5, 6, 7]],
                                     ins=[d["ag0v_in"][:, :]],
                                     outs=[d["ag0v_out"][:, :, :]]).then_inc(kb.sems['CC'], 1)
                G.collective_compute("AllGather", ALU.bypass,
                                     replica_groups=[[0, 1, 2, 3], [4, 5, 6, 7]],
                                     ins=[d["ag0l_in"][:, :]],
                                     outs=[d["ag0l_out"][:, :, :]]).then_inc(kb.sems['CC'], 1)
                kb.cnt['CC'] += 3
            tag0 = kb.tok('CC')

            p1e = ExitStack()
            with p1e:
                CW = 256
                NCC = TPC // CW
                xc = kb.sb("xc2", [128, NFT * CW], p1e)
                qr = kb.sb("qr", [128, 2 * TPC], p1e)
                wkv = kb.sb("wkv2", [128, NFT * 128], p1e)
                lmf = kb.sb("lmf", [128, 2 * NCH], p1e)
                l0m = kb.sb("l0m", [128, NQT * NCH], p1e)
                ssb = kb.sb("ssb", [128, NCH], p1e)
                m8b = kb.sb("m8b2", [128, 8], p1e)
                kb.barrier()
                twk_free = None
                for cc in range(NCC):
                    c0 = cc * CW
                    tnf = norm_stats(lambda i, j: XTc(xwork, i, c0 + j * 128, 128),
                                     list(range(NFT)), CW // 128, 'rms', EPS0,
                                     src_tok=txp)
                    tnb = bcast_stats(stat[:, 0:CW // 128], CW // 128, rb[:, 0:CW], tnf)
                    kb.wait('V', tnb)
                    for i in range(NFT):
                        V.tensor_tensor(XTc(xc, i, 0, CW, CW), XTc(xwork, i, c0, CW),
                                        rb[:, 0:CW], op=ALU.mult)
                        tv = kb.mark(V.tensor_scalar(XTc(xc, i, 0, CW, CW),
                                                     XTc(xc, i, 0, CW, CW),
                                                     gain[:, 32 + i:33 + i], None,
                                                     op0=ALU.mult), 'V')
                    txr = tv
                    for ot in range(2):
                        kb.wait('Y', twk_free)
                        twq = load_wcols(wkv, d["wqrT"], NFT, ot * 128, 128)
                        bnk = 0
                        kb.bank_wait(bnk)
                        kb.wait('T', twq); kb.wait('T', txr)
                        for i in range(NFT):
                            inst = T.matmul(kb.bank(bnk)[:, 0:CW],
                                            wkv[:, i * 128:(i + 1) * 128],
                                            XTc(xc, i, 0, CW, CW),
                                            start=(i == 0), stop=(i == NFT - 1))
                        tmm = kb.mark(inst, 'T')
                        twk_free = tmm
                        kb.wait('V', tmm)
                        tv = kb.mark(V.tensor_copy(qr[:, ot * TPC + c0: ot * TPC + c0 + CW],
                                                   kb.bank(bnk)[:, 0:CW]), 'V')
                        kb.bank_set_free(bnk, tv)
                tnf = norm_stats(lambda i, j: XTc(qr, i, j * 128, 128), [0, 1],
                                 NQT, 'rms', EPS0, src_tok=tv)
                tnb = bcast_stats(stat[:, 0:NQT], NQT, rb[:, 0:TPC], tnf)
                kb.wait('V', tnb)
                for ot in range(2):
                    V.tensor_tensor(XT(qr, ot), XT(qr, ot), rb[:], op=ALU.mult)
                    tv = kb.mark(V.tensor_scalar(XT(qr, ot), XT(qr, ot),
                                                 gain[:, 48 + ot:49 + ot], None,
                                                 op0=ALU.mult), 'V')
                tqr = tv
                tap("qr", qr[:, 0:TPC], tqr)
                kb.wait('Y', tag0)
                tlf = None
                for gb in range(NBLK):
                    r = min(gb, 7 - gb); half = 0 if gb < 4 else 1
                    for ot in range(2):
                        tlf = kb.mark(Y.dma_start(
                            out=lmf[:, ot * NCH + gb * 16: ot * NCH + (gb + 1) * 16],
                            in_=d["ag0l_out"][r, ot * 128:(ot + 1) * 128,
                                              half * 16:(half + 1) * 16]), 'DY')
                tap("lmf", lmf[:, 0:2 * NCH], tlf)
                tl0 = None
                for qt in range(NQT):
                    tl0 = kb.mark(Y.dma_start(out=l0m[:, qt * NCH:(qt + 1) * NCH],
                                              in_=d["l0mask"][qt, :, :]), 'DY')
                kb.wait('T', tqr); kb.wait('T', tlf)
                for qt in range(NQT):
                    bnk = 0
                    kb.bank_wait(bnk)
                    for ot in range(2):
                        inst = T.matmul(kb.bank(bnk)[:, 0:NCH],
                                        qr[:, ot * TPC + qt * 128: ot * TPC + (qt + 1) * 128],
                                        lmf[:, ot * NCH:(ot + 1) * NCH],
                                        start=(ot == 0), stop=(ot == 1))
                    tmm = kb.mark(inst, 'T')
                    kb.wait('V', tmm); kb.wait('V', tl0)
                    kb.vsw(V.tensor_scalar(ssb[:], kb.bank(bnk)[:, 0:NCH], INVSQ_RD, None,
                                           op0=ALU.mult))
                    tv = kb.vsw(V.tensor_tensor(ssb[:], ssb[:],
                                                l0m[:, qt * NCH:(qt + 1) * NCH],
                                                op=ALU.add))
                    kb.bank_set_free(bnk, tv)
                    kb.vsw(V.max(out=m8b[:], in_=ssb[:]))
                    kb.vsw(V.tensor_scalar(stat[:, 0:1], m8b[:, 0:1], -1e29, None, op0=ALU.max))
                    tvm = kb.vsw(V.tensor_scalar(stat[:, 1:2], stat[:, 0:1], -1.0,
                                                 None, op0=ALU.mult))
                    kb.wait('S', tvm)
                    te = kb.mark(S.activation(sq[0][:, 0:8], m8b[:], AF.Exp,
                                              bias=stat[:, 1:2]), 'S')
                    kb.wait('V', te)
                    kb.vsw(V.reduce_sum(stat[:, 2:3], sq[0][:, 0:8], axis=AX.X))
                    kb.vsw(V.tensor_scalar(stat[:, 2:3], stat[:, 2:3], 1e-30, None, op0=ALU.max))
                    kb.vsw(V.reciprocal(stat[:, 4:5], stat[:, 2:3]))
                    kb.vsw(V.tensor_scalar(stat[:, 3:4], m8b[:, 0:1], -1e29, None, op0=ALU.is_gt))
                    tvz = kb.vsw(V.tensor_tensor(stat[:, 2:3], stat[:, 4:5],
                                                 stat[:, 3:4], op=ALU.mult))
                    kb.wait('S', tvz)
                    te = kb.mark(S.activation(w_all[:, qt * NCH:(qt + 1) * NCH], ssb[:],
                                              AF.Exp, bias=stat[:, 1:2]), 'S')
                    kb.wait('V', te)
                    kb.vsw(V.tensor_scalar(ssb[:], ssb[:], m8b[:, 7:8], None, op0=ALU.is_ge))
                    kb.vsw(V.tensor_tensor(w_all[:, qt * NCH:(qt + 1) * NCH],
                                           w_all[:, qt * NCH:(qt + 1) * NCH], ssb[:], op=ALU.mult))
                    tv = kb.vsw(V.tensor_scalar(w_all[:, qt * NCH:(qt + 1) * NCH],
                                                w_all[:, qt * NCH:(qt + 1) * NCH],
                                                stat[:, 2:3], None, op0=ALU.mult))
                twall = tv
                tap("w_all", w_all[:, 0:NCH * NQT], twall)

            p2 = ExitStack()
            with p2:
                qh = kb.sb("qh", [128, TPC], p2)
                kkb = [kb.sb(f"kkb{i}", [128, 512], p2) for i in range(2)]
                vkb = [kb.sb(f"vkb{i}", [128, 512], p2) for i in range(2)]
                psb = [kb.sb(f"psb{i}", [128, 512], p2) for i in range(2)]
                ptb = kb.sb("ptb", [128, 512], p2)
                ssum = kb.sb("ssum", [128, 16], p2)
                mw = kb.sb("mwb", [128, 16], p2)
                abuf = [kb.sb(f"abuf{i}", [128, TPC], p2) for i in range(4)]
                wo = kb.sb("wob", [128, 512], p2)
                wq1 = kb.sb("wq1", [128, NFT * 128], p2)
                kfree = [None, None]
                vfree = [None, None]
                tT_prev = None
                tpv_prev = None
                twq_free = None
                te_prev_g = [None]
                kb.barrier()
                kb.wait('G', tag0)
                tap("xw_attn", xwork[:, 0:NFT * TPC], kb.tok('V'))
                tev = None
                tva = None
                for kvh in range(HKV):
                    for qi in range(4):
                        g = kvh * 4 + qi
                        kb.wait('Y', twq_free)
                        twq = load_wcols(wq1, d["wqT"], NFT, g * 128, 128)
                        if g == 0:
                            tap("wq10", wq1[:, 0:NFT * 128], twq)
                        kb.wait('T', twq)
                        for qc in range(2):
                            bnk = 0
                            kb.bank_wait(bnk)
                            for i in range(NFT):
                                inst = T.matmul(kb.bank(bnk)[:, 0:512],
                                                wq1[:, i * 128:(i + 1) * 128],
                                                XTc(xwork, i, qc * 512, 512),
                                                start=(i == 0), stop=(i == NFT - 1))
                            tmm = kb.mark(inst, 'T')
                            twq_free = tmm
                            kb.wait('V', tmm)
                            tv = kb.mark(V.tensor_copy(qh[:, qc * 512:(qc + 1) * 512],
                                                       kb.bank(bnk)[:, 0:512]), 'V')
                            kb.bank_set_free(bnk, tv)
                        tnf = norm_stats(lambda i, j: XTc(qh, 0, j * 128, 128), [0],
                                         NQT, 'rms', EPS0, src_tok=tv)
                        if g == 0:
                            tap("qh0", qh[:, 0:TPC], tv)
                        kb.wait('V', tnf)
                        kb.wait('V', te_prev_g[0])
                        tqs = kb.vsw(V.tensor_scalar(qstat[:], stat[:, 0:NQT],
                                                     INVSQ_HD, None, op0=ALU.mult))
                        tq = tv
                        for qt in range(NQT):
                            ob = 3
                            tTpb = [tT_prev, tT_prev]
                            pend = None

                            def flush_pv(pend_kbi, pend_tv, pend_tvl):
                                nonlocal tT_prev, tpv_prev, tev
                                pb2 = pend_kbi % 2
                                tb = 2
                                kb.bank_wait(tb)
                                kb.wait('T', pend_tv)
                                for j in range(4):
                                    inst = T.transpose(
                                        kb.bank(tb)[:, j * 128:(j + 1) * 128],
                                        psb[pb2][:, j * 128:(j + 1) * 128], ident[:])
                                tT = kb.mark(inst, 'T')
                                tT_prev = tT
                                tTpb[pb2] = tT
                                kb.wait('S', tT)
                                kb.wait('S', tpv_prev)
                                tpt = kb.mark(S.copy(ptb[:], kb.bank(tb)[:, 0:512]), 'S')
                                kb.bank_set_free(tb, tpt)
                                kb.wait('T', tpt)
                                kb.wait('T', pend_tvl)
                                if pend_kbi == 0:
                                    kb.bank_wait(ob)
                                for j in range(4):
                                    inst = T.matmul(kb.bank(ob)[:, 0:128],
                                                    vkb[pend_kbi % 2][:, j * 128:(j + 1) * 128],
                                                    ptb[:, j * 128:(j + 1) * 128],
                                                    start=(pend_kbi == 0 and j == 0),
                                                    stop=(pend_kbi == NBLK - 1 and j == 3))
                                tpv = kb.mark(inst, 'T')
                                tpv_prev = tpv
                                vfree[pend_kbi % 2] = tpv
                                return tpv

                            for kbi in range(NBLK):
                                r = min(kbi, 7 - kbi); half = 0 if kbi < 4 else 1
                                buf = kbi % 2
                                kb.wait('Y', kfree[buf])
                                tkl = kb.mark(Y.dma_start(
                                    out=kkb[buf][:],
                                    in_=d["ag0k_out"][r, kvh * 128:(kvh + 1) * 128,
                                                      half * 512:(half + 1) * 512]), 'DY')
                                kb.wait('G', vfree[buf])
                                tvl = kb.mark(G.dma_start(
                                    out=vkb[buf][:].rearrange("p (j v) -> p j v", v=128),
                                    in_=d["ag0v_out"][r, half * 512:(half + 1) * 512,
                                                      kvh * 128:(kvh + 1) * 128].rearrange(
                                        "(j p) v -> p j v", p=128)), 'DG')
                                sb_ = 0 + buf
                                kb.bank_wait(sb_)
                                kb.wait('T', tq); kb.wait('T', tkl)
                                tmm = kb.mark(T.matmul(kb.bank(sb_)[:, 0:512],
                                                       qh[:, qt * 128:(qt + 1) * 128],
                                                       kkb[buf][:], start=True, stop=True),
                                              'T')
                                kfree[buf] = tmm
                                kb.wait('S', tmm)
                                kb.wait('S', tTpb[buf])
                                kb.wait('S', tqs)
                                te = kb.mark(S.activation(psb[buf][:], kb.bank(sb_)[:, 0:512],
                                                          AF.Exp, bias=cst[:, 0:1],
                                                          scale=qstat[:, qt:qt + 1]), 'S')
                                te_prev_g[0] = te
                                kb.bank_set_free(sb_, te)
                                kb.wait('V', te)
                                kb.vsw(V.reduce_sum(
                                    ssum[:, 0:16].rearrange("p (c o) -> p c o", o=1),
                                    psb[buf][:].rearrange("p (c s) -> p c s", s=SCH),
                                    axis=AX.X))
                                kb.vsw(V.reciprocal(mw[:], ssum[:]))
                                kb.vsw(V.tensor_tensor(
                                    mw[:],
                                    w_all[:, qt * NCH + kbi * 16:
                                          qt * NCH + (kbi + 1) * 16],
                                    mw[:], op=ALU.mult))
                                tv = kb.mark(V.tensor_tensor(
                                    psb[buf][:].rearrange("p (c s) -> p c s", s=SCH),
                                    psb[buf][:].rearrange("p (c s) -> p c s", s=SCH),
                                    mw[:].rearrange("p (c o) -> p c o", o=1).to_broadcast(
                                        [128, 16, SCH]), op=ALU.mult), 'V')
                                if pend is not None:
                                    flush_pv(*pend)
                                pend = (kbi, tv, tvl)
                            tpv = flush_pv(*pend)
                            kb.wait('V', tpv)
                            tev = kb.mark(V.tensor_copy(abuf[qi][:, qt * 128:(qt + 1) * 128],
                                                        kb.bank(ob)[:, 0:128]), 'V')
                            kb.bank_set_free(ob, tev)
                        if g == 0:
                            tap("abuf0", abuf[0][:, 0:TPC], tev)
                    wo_free = None
                    for ot in range(NFT):
                        kb.wait('Y', wo_free)
                        towl = None
                        for qi in range(4):
                            g = kvh * 4 + qi
                            towl = kb.mark(Y.dma_start(out=wo[:, qi * 128:(qi + 1) * 128],
                                                       in_=d["woT"][g * 128:(g + 1) * 128,
                                                                    ot * 128:(ot + 1) * 128]),
                                           'DY')
                        for qc in range(2):
                            bnk = 4 + (qc % 2)
                            kb.bank_wait(bnk)
                            kb.wait('T', towl); kb.wait('T', tev)
                            for qi in range(4):
                                inst = T.matmul(kb.bank(bnk)[:, 0:512],
                                                wo[:, qi * 128:(qi + 1) * 128],
                                                abuf[qi][:, qc * 512:(qc + 1) * 512],
                                                start=(qi == 0), stop=(qi == 3))
                            tmm = kb.mark(inst, 'T')
                            wo_free = tmm
                            kb.wait('V', tmm)
                            tva = kb.mark(V.tensor_tensor(XTc(xsum, ot, qc * 512, 512),
                                                          XTc(xsum, ot, qc * 512, 512),
                                                          kb.bank(bnk)[:, 0:512],
                                                          op=ALU.add), 'V')
                            kb.bank_set_free(bnk, tva)
                tap("xsum_a0", xsum[:, 0:TPC], tva)
            trgru_final = kb.tok('V')
        _moe(nc, kb, d, tap, XT, XTc, xsum, we_tm, rg, ru, onesall, ones_col,
             sq, bcast_stats, tweT_tok[0])
    return None


def _moe(nc, kb, d, tap, XT, XTc, xsum, we_tm, rg, ru, onesall, ones_col, sq,
         bcast_stats, tweT):
    T, V, S, G, Y = kb.T, kb.V, kb.S, kb.G, kb.Y
    p3 = ExitStack()
    with p3:
        u1g = kb.sb("u1g", [128, TPC], p3)
        u1u = kb.sb("u1u", [128, TPC], p3)
        web = kb.sb("web", [128, TPC], p3)
        rdw = kb.sb("rdw", [128, TPC], p3)
        got = kb.sb("gotm", [128, 512], p3)
        uot = kb.sb("uotm", [128, 512], p3)
        u3blk = [kb.sb(f"u3b{i}", [128, TPC], p3) for i in range(8)]
        ew = [{'ug': kb.sb(f"ug{i}", [128, 128], p3), 'uu': kb.sb(f"uu{i}", [128, 128], p3),
               'ud': kb.sb(f"ud{i}", [128, 128], p3),
               'cg': kb.sb(f"cg{i}", [128, IDIM], p3),
               'cu': kb.sb(f"cu{i}", [128, IDIM], p3)} for i in range(2)]
        cdb = [kb.sb(f"cdb{i}", [128, 128], p3) for i in range(2)]
        rdT = kb.sb("rdT", [128, NIT * 128], p3)
        trd = None
        for i in range(NIT):
            trd = kb.mark(Y.dma_start(out=rdT[:, i * 128:(i + 1) * 128],
                                      in_=d["RdT"][i * 128:(i + 1) * 128, :]), 'DY')
        ewload = [None, None]
        ewfree = [None, None]
        kb.barrier()

        def load_expert(e):
            buf = e % 2
            kb.wait('Y', ewfree[buf])
            kb.mark(Y.dma_start(out=ew[buf]['ug'][:], in_=d["UgT"][e, :, :]), 'DY')
            kb.mark(Y.dma_start(out=ew[buf]['uu'][:], in_=d["UuT"][e, :, :]), 'DY')
            kb.mark(Y.dma_start(out=ew[buf]['ud'][:], in_=d["UdT"][e, :, :]), 'DY')
            kb.mark(Y.dma_start(out=ew[buf]['cg'][:], in_=d["CgT"][e, :, :]), 'DY')
            ewload[buf] = kb.mark(Y.dma_start(out=ew[buf]['cu'][:],
                                              in_=d["CuT"][e, :, :]), 'DY')

        load_expert(0)
        kb.wait('T', trd)
        tgot_free = None
        tcd = None
        tse = None
        tse2 = None
        for eb in range(4):
            for ei in range(8):
                e = eb * 8 + ei
                buf = e % 2
                if e + 1 < NEXP:
                    load_expert(e + 1)
                kb.wait('T', ewload[buf])
                we_e = we_tm[:].rearrange("p (j e) -> p j e", e=NEXP)[:, :, e]
                tvw = bcast_stats(we_e, NQT, web[:, 0:TPC], tweT)
                for qc in range(2):
                    for (dst, wkey, bnk) in ((u1g, 'ug', 6), (u1u, 'uu', 7)):
                        kb.bank_wait(bnk)
                        tmm = kb.mark(T.matmul(kb.bank(bnk)[:, 0:512], ew[buf][wkey][:],
                                               (rg if dst is u1g else ru)[:,
                                               qc * 512:(qc + 1) * 512],
                                               start=True, stop=True), 'T')
                        kb.wait('S', tmm)
                        tse = kb.mark(S.copy(dst[:, qc * 512:(qc + 1) * 512],
                                             kb.bank(bnk)[:, 0:512]), 'S')
                        kb.bank_set_free(bnk, tse)
                for qc in range(2):
                    kb.bank_wait(2)
                    gbuf = [got[:], sq[0][:]]
                    ubuf = [uot[:], sq[1][:]]
                    gfree = [tgot_free, tgot_free]
                    tgu_p = [None, None]
                    for ot in range(NIT):
                        pb_ = ot % 2
                        bg = (0, 6)[pb_]
                        bu = (1, 7)[pb_]
                        kb.bank_wait(bg)
                        kb.wait('T', tse)
                        tg = kb.mark(T.matmul(kb.bank(bg)[:, 0:512],
                                              ew[buf]['cg'][:, ot * 128:(ot + 1) * 128],
                                              u1g[:, qc * 512:(qc + 1) * 512],
                                              start=True, stop=True), 'T')
                        kb.bank_wait(bu)
                        tu = kb.mark(T.matmul(kb.bank(bu)[:, 0:512],
                                              ew[buf]['cu'][:, ot * 128:(ot + 1) * 128],
                                              u1u[:, qc * 512:(qc + 1) * 512],
                                              start=True, stop=True), 'T')
                        if ot > 0:
                            po = (ot - 1) % 2
                            kb.wait('T', tgu_p[po])
                            inst = T.matmul(kb.bank(2)[:, 0:512],
                                            rdT[:, (ot - 1) * 128:ot * 128], gbuf[po],
                                            start=(ot - 1 == 0), stop=False)
                            gfree[po] = kb.mark(inst, 'T')
                        kb.wait('S', tg)
                        kb.wait('S', gfree[pb_])
                        tsg = kb.mark(S.activation(gbuf[pb_], kb.bank(bg)[:, 0:512],
                                                   AF.Silu), 'S')
                        kb.bank_set_free(bg, tsg)
                        kb.wait('V', tu)
                        kb.wait('V', gfree[pb_])
                        tsu = kb.mark(V.tensor_copy(ubuf[pb_], kb.bank(bu)[:, 0:512]), 'V')
                        kb.bank_set_free(bu, tsu)
                        kb.wait('V', tsg)
                        tgu_p[pb_] = kb.mark(V.tensor_tensor(gbuf[pb_], gbuf[pb_],
                                                             ubuf[pb_], op=ALU.mult), 'V')
                    po = (NIT - 1) % 2
                    kb.wait('T', tgu_p[po])
                    inst = T.matmul(kb.bank(2)[:, 0:512],
                                    rdT[:, (NIT - 1) * 128:NIT * 128], gbuf[po],
                                    start=False, stop=True)
                    tgot_free = kb.mark(inst, 'T')
                    trdm = tgot_free
                    kb.wait('V', trdm); kb.wait('V', tvw)
                    trw = kb.mark(V.tensor_tensor(rdw[:, qc * 512:(qc + 1) * 512],
                                                  kb.bank(2)[:, 0:512],
                                                  web[:, qc * 512:(qc + 1) * 512],
                                                  op=ALU.mult), 'V')
                    kb.bank_set_free(2, trw)
                    kb.bank_wait(3)
                    kb.wait('T', trw)
                    tmm = kb.mark(T.matmul(kb.bank(3)[:, 0:512], ew[buf]['ud'][:],
                                           rdw[:, qc * 512:(qc + 1) * 512],
                                           start=True, stop=True), 'T')
                    kb.wait('S', tmm)
                    tse2 = kb.mark(S.copy(u3blk[ei][:, qc * 512:(qc + 1) * 512],
                                          kb.bank(3)[:, 0:512]), 'S')
                    kb.bank_set_free(3, tse2)
                ewfree[buf] = tse2
            cd_free = [None, None]
            for ot in range(NFT):
                kb.bank_wait(4)
                kb.bank_wait(5)
                for ei in range(8):
                    e = eb * 8 + ei
                    buf2 = ei % 2
                    kb.wait('Y', cd_free[buf2])
                    tcl = kb.mark(Y.dma_start(out=cdb[buf2][:],
                                              in_=d["CdT"][e, :, ot * 128:(ot + 1) * 128]),
                                  'DY')
                    kb.wait('T', tcl)
                    kb.wait('T', tse2)
                    for qc in range(2):
                        inst = T.matmul(kb.bank(4 + qc)[:, 0:512], cdb[buf2][:],
                                        u3blk[ei][:, qc * 512:(qc + 1) * 512],
                                        start=(ei == 0), stop=(ei == 7))
                    cd_free[buf2] = kb.mark(inst, 'T')
                tmm = cd_free[1]
                kb.wait('V', tmm)
                for qc in range(2):
                    tcd = kb.mark(V.tensor_tensor(XTc(xsum, ot, qc * 512, 512),
                                                  XTc(xsum, ot, qc * 512, 512),
                                                  kb.bank(4 + qc)[:, 0:512],
                                                  op=ALU.add), 'V')
                kb.bank_set_free(4, tcd)
                kb.bank_set_free(5, tcd)
        tap("xsum_moe", xsum[:, 0:TPC], tcd)



def _l1(nc, kb, d, tap, XT, XTc, norm_stats, bcast_stats, load_wcols, xsum,
        rb, mb, stat, stat2, ones_col, onesall, ident, gl1, sq, cst):
    T, V, S, G, Y = kb.T, kb.V, kb.S, kb.G, kb.Y
    EPSL = EPS1

    l1a = ExitStack()
    with l1a:
        kb.barrier()
        rqa = kb.sb("rqa", [128, 8], l1a)
        rqab = kb.sb("rqab", [128, TPC], l1a)
        cosb = kb.sb("cosb", [64, TPC], l1a)
        sinb = kb.sb("sinb", [64, TPC], l1a)
        Y.dma_start(out=cosb[:], in_=d["cosT"][:, :]).then_inc(kb.sems['DY'], 16)
        kb.cnt['DY'] += 16
        tcs = kb.mark(Y.dma_start(out=sinb[:], in_=d["sinT"][:, :]), 'DY')

        tsrc = kb.tok('V')
        tln = norm_stats(lambda i, j: XTc(xsum, i, j * 128, 128),
                         list(range(NFT)), NQT, 'ln', EPS0, src_tok=tsrc)
        t1 = bcast_stats(stat[:, 0:NQT], NQT, rb[:, 0:TPC], tln)
        t2 = bcast_stats(stat2[:, 0:NQT], NQT, mb[:, 0:TPC], tln)
        kb.waits('V', [t1, t2])

        pA = ExitStack()
        with pA:
            xn1c = kb.sb("xn1c", [128, NFT * 512], pA)
            qac = kb.sb("qac", [128, 12 * 512], pA)
            ckvt = kb.sb("ckvt", [128, 4 * 512], pA)
            kpet = kb.sb("kpet", [64, 512], pA)
            kper = kb.sb("kper", [64, 512], pA)
            xtm = kb.sb("xtm", [128, 2048], pA)
            xint = kb.sb("xint", [128, 2048], pA, dtype=I32)
            wbufA = kb.sb("wbufA", [128, NFT * 128], pA)
            kb.barrier()
            tqaD = None
            for qc in range(2):
                c0 = qc * 512
                kb.barrier()
                for i in range(NFT):
                    V.tensor_tensor(XTc(xn1c, i, 0, 512, 512), XTc(xsum, i, c0, 512),
                                    rb[:, c0:c0 + 512], op=ALU.mult)
                    txn1 = kb.mark(V.tensor_tensor(XTc(xn1c, i, 0, 512, 512),
                                                   XTc(xn1c, i, 0, 512, 512),
                                                   mb[:, c0:c0 + 512], op=ALU.subtract), 'V')
                if qc == 0:
                    tap("xn1", xn1c[:, 0:512], txn1)
                twf = None
                tqa = None
                for ot in range(12):
                    kb.wait('Y', twf)
                    twl = load_wcols(wbufA, d["qaT"], NFT, ot * 128, 128)
                    bnk = 0
                    kb.bank_wait(bnk)
                    kb.wait('T', twl); kb.wait('T', txn1)
                    for i in range(NFT):
                        inst = T.matmul(kb.bank(bnk)[:, 0:512],
                                        wbufA[:, i * 128:(i + 1) * 128],
                                        XTc(xn1c, i, 0, 512, 512),
                                        start=(i == 0), stop=(i == NFT - 1))
                    tmm = kb.mark(inst, 'T')
                    twf = tmm
                    kb.wait('V', tmm)
                    tqa = kb.mark(V.tensor_copy(qac[:, ot * 512:(ot + 1) * 512],
                                                kb.bank(bnk)[:, 0:512]), 'V')
                    kb.bank_set_free(bnk, tqa)
                tnf = norm_stats(lambda i, j: qac[:, i * 512 + j * 128: i * 512 + (j + 1) * 128],
                                 list(range(12)), 4, 'rms', EPSL, src_tok=tqa)
                kb.wait('V', tnf)
                trq = kb.vsw(V.tensor_copy(rqa[:, qc * 4:(qc + 1) * 4], stat[:, 0:4]))
                kb.wait('Y', tqa)
                for ot in range(12):
                    tqaD = kb.mark(Y.dma_start(out=d["qa_spill"][ot * 128:(ot + 1) * 128,
                                                                 c0:c0 + 512],
                                               in_=qac[:, ot * 512:(ot + 1) * 512]), 'DY')
                tck = None
                for ot in range(5):
                    w = 128 if ot < 4 else 64
                    kb.wait('Y', twf)
                    twl = load_wcols(wbufA, d["kvaT"], NFT, ot * 128, w)
                    bnk = 1
                    kb.bank_wait(bnk)
                    kb.wait('T', twl)
                    for i in range(NFT):
                        inst = T.matmul(kb.bank(bnk)[0:w, 0:512],
                                        wbufA[:, i * w:(i + 1) * w],
                                        XTc(xn1c, i, 0, 512, 512),
                                        start=(i == 0), stop=(i == NFT - 1))
                    tmm = kb.mark(inst, 'T')
                    twf = tmm
                    kb.wait('V', tmm)
                    if ot < 4:
                        tck = kb.mark(V.tensor_copy(ckvt[:, ot * 512:(ot + 1) * 512],
                                                    kb.bank(bnk)[:, 0:512]), 'V')
                    else:
                        tck = kb.mark(V.tensor_copy(kpet[:], kb.bank(bnk)[0:64, 0:512]), 'V')
                    kb.bank_set_free(bnk, tck)
                tnf = norm_stats(lambda i, j: ckvt[:, i * 512 + j * 128: i * 512 + (j + 1) * 128],
                                 list(range(4)), 4, 'rms', EPSL, src_tok=tck)
                tnb = bcast_stats(stat[:, 0:4], 4, rb[:, 0:512], tnf)
                kb.wait('V', tnb)
                for ot in range(4):
                    V.tensor_tensor(ckvt[:, ot * 512:(ot + 1) * 512],
                                    ckvt[:, ot * 512:(ot + 1) * 512],
                                    rb[:, 0:512], op=ALU.mult)
                    tcn = kb.mark(V.tensor_scalar(ckvt[:, ot * 512:(ot + 1) * 512],
                                                  ckvt[:, ot * 512:(ot + 1) * 512],
                                                  gl1[:, 12 + ot:13 + ot], None,
                                                  op0=ALU.mult), 'V')
                kb.wait('Y', tcn)
                for ot in range(4):
                    kb.mark(Y.dma_start(out=d["ag1_in"][ot * 128:(ot + 1) * 128, c0:c0 + 512],
                                        in_=ckvt[:, ot * 512:(ot + 1) * 512]), 'DY')
                kb.wait('V', tcs)
                kb.vsw(V.tensor_copy(kper[0:32, :], kpet[32:64, :]))
                kb.vsw(V.tensor_scalar(kper[0:32, :], kper[0:32, :], -1.0, None, op0=ALU.mult))
                kb.vsw(V.tensor_copy(kper[32:64, :], kpet[0:32, :]))
                kb.vsw(V.tensor_tensor(kper[:], kper[:], sinb[:, c0:c0 + 512], op=ALU.mult))
                kb.vsw(V.tensor_tensor(kpet[:], kpet[:], cosb[:, c0:c0 + 512], op=ALU.mult))
                tkp = kb.vsw(V.tensor_tensor(kper[:], kper[:], kpet[:], op=ALU.add))
                kb.wait('Y', tkp)
                kb.mark(Y.dma_start(out=d["ag1_in"][512:576, c0:c0 + 512], in_=kper[:]), 'DY')
                for tt in range(4):
                    tm = xtm[:, 0:2048]
                    tb = 2
                    for i4 in range(4):
                        kb.bank_wait(tb)
                        for i in range(4):
                            ii = i4 * 4 + i
                            inst = T.transpose(kb.bank(tb)[:, i * 128:(i + 1) * 128],
                                               XTc(xn1c, ii, tt * 128, 128, 512), ident[:])
                        tTx = kb.mark(inst, 'T')
                        kb.wait('V', tTx)
                        tvv = kb.mark(V.tensor_copy(tm[:, i4 * 512:(i4 + 1) * 512],
                                                    kb.bank(tb)[:, 0:512]), 'V')
                        kb.bank_set_free(tb, tvv)
                    kb.wait('V', tvv)
                    kb.vsw(V.tensor_reduce(stat[:, 0:1], tm, axis=AX.X, op=ALU.max,
                                           apply_absolute_value=True))
                    kb.vsw(V.tensor_scalar(stat[:, 0:1], stat[:, 0:1], 1.0 / 127.0, 1e-8,
                                           op0=ALU.mult, op1=ALU.max))
                    kb.vsw(V.reciprocal(stat[:, 1:2], stat[:, 0:1]))
                    kb.vsw(V.tensor_scalar(tm, tm, stat[:, 1:2], None, op0=ALU.mult))
                    kb.vsw(V.tensor_copy(xint[:], tm))
                    kb.vsw(V.tensor_copy(tm, xint[:]))
                    tql = kb.vsw(V.tensor_scalar(tm, tm, stat[:, 0:1], None, op0=ALU.mult))
                    kb.wait('T', tql)
                    for i4 in range(4):
                        tb = 3
                        kb.bank_wait(tb)
                        for i in range(4):
                            inst = T.transpose(kb.bank(tb)[:, i * 128:(i + 1) * 128],
                                               tm[:, (i4 * 4 + i) * 128:(i4 * 4 + i + 1) * 128],
                                               ident[:])
                        tTx = kb.mark(inst, 'T')
                        kb.wait('V', tTx)
                        for i in range(4):
                            ii = i4 * 4 + i
                            tvq = kb.mark(V.tensor_copy(
                                XTc(xn1c, ii, tt * 128, 128, 512),
                                kb.bank(tb)[:, i * 128:(i + 1) * 128]), 'V')
                        kb.bank_set_free(tb, tvq)
                kb.wait('Y', tvq)
                for i in range(NFT):
                    kb.mark(Y.dma_start(out=d["xq_spill"][i * 128:(i + 1) * 128, c0:c0 + 512],
                                        in_=XTc(xn1c, i, 0, 512, 512)), 'DY')
            kb.wait('G', kb.tok('DY'))
            G.collective_compute("AllGather", ALU.bypass,
                                 replica_groups=[[0, 1, 2, 3], [4, 5, 6, 7]],
                                 ins=[d["ag1_in"][:, :]],
                                 outs=[d["ag1_out"][:, :, :]]).then_inc(kb.sems['CC'], 1)
            kb.cnt['CC'] += 1
        tag1 = kb.tok('CC')

        pB = ExitStack()
        with pB:
            wn = kb.sb("wn", [128, DFF], pB)
            wi2 = kb.sb("wi2", [128, DFF], pB, dtype=I32)
            wqe = kb.sb("wqe", [128, 512], pB)
            kb.barrier()
            tdq = None
            for (nj, wd, kdim, qwd, sd) in ((2 * NDT, "w1", DM, "qw1", "s1"),
                                            (NFT, "w2", DFF, "qw2", "s2")):
                nkt = kdim // 128
                for j in range(nj):
                    kb.wait('Y', kb.tok('T'))
                    twl = kb.mark(Y.dma_start(out=wn[:, 0:kdim],
                                              in_=d[wd][j * 128:(j + 1) * 128, :]), 'DY')
                    kb.wait('V', twl)
                    kb.vsw(V.tensor_reduce(stat[:, 0:1], wn[:, 0:kdim], axis=AX.X,
                                           op=ALU.max, apply_absolute_value=True))
                    kb.vsw(V.tensor_scalar(stat[:, 0:1], stat[:, 0:1], 1.0 / 127.0, 1e-8,
                                           op0=ALU.mult, op1=ALU.max))
                    kb.vsw(V.reciprocal(stat[:, 1:2], stat[:, 0:1]))
                    kb.vsw(V.tensor_scalar(wn[:, 0:kdim], wn[:, 0:kdim], stat[:, 1:2],
                                           None, op0=ALU.mult))
                    kb.vsw(V.tensor_copy(wi2[:, 0:kdim], wn[:, 0:kdim]))
                    tvw = kb.vsw(V.tensor_copy(wn[:, 0:kdim], wi2[:, 0:kdim]))
                    kb.wait('Y', tvw)
                    tsd = kb.mark(Y.dma_start(out=d[sd][j, :].rearrange("(p o) -> p o", o=1),
                                              in_=stat[:, 0:1]), 'DY')
                    kb.wait('T', tvw)
                    for k4 in range((nkt + 3) // 4):
                        tb = 4
                        kb.bank_wait(tb)
                        kn4 = min(4, nkt - k4 * 4)
                        for k in range(kn4):
                            inst = T.transpose(kb.bank(tb)[:, k * 128:(k + 1) * 128],
                                               wn[:, (k4 * 4 + k) * 128:(k4 * 4 + k + 1) * 128],
                                               ident[:])
                        tTx = kb.mark(inst, 'T')
                        kb.wait('V', tTx)
                        kb.wait('V', tdq)
                        tvq = kb.mark(V.tensor_copy(wqe[:, 0:kn4 * 128],
                                                    kb.bank(tb)[:, 0:kn4 * 128]), 'V')
                        kb.bank_set_free(tb, tvq)
                        kb.wait('Y', tvq)
                        for k in range(kn4):
                            tdq = kb.mark(Y.dma_start(
                                out=d[qwd][j, (k4 * 4 + k) * 128:(k4 * 4 + k + 1) * 128, :],
                                in_=wqe[:, k * 128:(k + 1) * 128]), 'DY')

        pC = ExitStack()
        with pC:
            kpf = kb.sb("kpf", [64, LSEQ], pC)
            kn = kb.sb("kn", [128, LSEQ], pC)
            vt = kb.sb("vt", [128, LSEQ], pC)
            s_sb = kb.sb("s_sb", [128, LSEQ], pC)
            qbs = kb.sb("qbs", [128, 12 * 192], pC)
            kvbk = kb.sb("kvbk", [128, 4 * 128], pC)
            kvbv = kb.sb("kvbv", [128, 4 * 128], pC)
            ckvb = kb.sb("ckvb", [128, 4 * 512], pC)
            qas = kb.sb("qas", [128, 12 * 128], pC)
            qn2 = kb.sb("qn2", [128, 128], pC)
            qp2 = kb.sb("qp2", [64, 128], pC)
            qpr = kb.sb("qpr", [64, 128], pC)
            psb2 = kb.sb("psb2", [128, 512], pC)
            ptb2 = kb.sb("ptb2", [128, 512], pC)
            mb2 = [kb.sb(f"mb2_{i}", [128, 512], pC) for i in range(2)]
            ab4 = [kb.sb(f"ab4_{i}", [128, TPC], pC) for i in range(2)]
            owb = kb.sb("owb", [128, 512], pC)
            mx8 = kb.sb("mx8", [128, 16], pC)
            z8 = kb.sb("z8", [128, 16], pC)
            kb.barrier()
            trq2 = bcast_stats(rqa[:, 0:8], 8, rqab[:, 0:TPC], kb.tok('V'))
            kb.wait('Y', tag1)
            tkpf = None
            for gb in range(NBLK):
                r = min(gb, 7 - gb); half = 0 if gb < 4 else 1
                tkpf = kb.mark(Y.dma_start(out=kpf[:, gb * 512:(gb + 1) * 512],
                                           in_=d["ag1_out"][r, 512:576,
                                                            half * 512:(half + 1) * 512]), 'DY')
            tev = None
            tva = None
            mfree = [None, None]
            tT_prev = None
            tpv_prev = None
            te_prev = [None]
            tqb_free = None
            for h in range(NH):
                kb.wait('Y', tqb_free)
                tqb = None
                for i in range(12):
                    tqb = kb.mark(Y.dma_start(out=qbs[:, i * 192:(i + 1) * 192],
                                              in_=d["qbT"][i * 128:(i + 1) * 128,
                                                           h * 192:(h + 1) * 192]), 'DY')
                kb.wait('V', tqb)
                for i in range(12):
                    tqb2 = kb.mark(V.tensor_scalar(qbs[:, i * 192:(i + 1) * 192],
                                                   qbs[:, i * 192:(i + 1) * 192],
                                                   gl1[:, i:i + 1], None, op0=ALU.mult), 'V')
                for ct in range(4):
                    Y.dma_start(out=kvbk[:, ct * 128:(ct + 1) * 128],
                                in_=d["kvbT"][ct * 128:(ct + 1) * 128,
                                              h * 256:h * 256 + 128]).then_inc(
                        kb.sems['DY'], 16)
                    kb.cnt['DY'] += 16
                    tkv = kb.mark(Y.dma_start(out=kvbv[:, ct * 128:(ct + 1) * 128],
                                              in_=d["kvbT"][ct * 128:(ct + 1) * 128,
                                                            h * 256 + 128:h * 256 + 256]), 'DY')
                tknv = None
                for cb in range(NBLK):
                    r = min(cb, 7 - cb); half = 0 if cb < 4 else 1
                    kb.wait('Y', tknv)
                    tcl = kb.mark(Y.dma_start(
                        out=ckvb[:].rearrange("p (c t) -> p c t", t=512),
                        in_=d["ag1_out"][r, 0:512, half * 512:(half + 1) * 512].rearrange(
                            "(c p) t -> p c t", p=128)), 'DY')
                    bnk = 0
                    kb.bank_wait(bnk)
                    kb.wait('T', tcl); kb.wait('T', tkv)
                    for ct in range(4):
                        inst = T.matmul(kb.bank(bnk)[:, 0:512],
                                        kvbk[:, ct * 128:(ct + 1) * 128],
                                        ckvb[:, ct * 512:(ct + 1) * 512],
                                        start=(ct == 0), stop=(ct == 3))
                    tmm = kb.mark(inst, 'T')
                    kb.wait('V', tmm)
                    tknv = kb.mark(V.tensor_copy(kn[:, cb * 512:(cb + 1) * 512],
                                                 kb.bank(bnk)[:, 0:512]), 'V')
                    kb.bank_set_free(bnk, tknv)
                    for tt in range(4):
                        bnk = 1
                        kb.bank_wait(bnk)
                        for ct in range(4):
                            inst = T.matmul(kb.bank(bnk)[:, 0:128],
                                            ckvb[:, ct * 512 + tt * 128:
                                                 ct * 512 + (tt + 1) * 128],
                                            kvbv[:, ct * 128:(ct + 1) * 128],
                                            start=(ct == 0), stop=(ct == 3))
                        tmm = kb.mark(inst, 'T')
                        kb.wait('V', tmm)
                        tknv = kb.mark(V.tensor_copy(
                            vt[:, (cb * 4 + tt) * 128:(cb * 4 + tt + 1) * 128],
                            kb.bank(bnk)[:, 0:128]), 'V')
                        kb.bank_set_free(bnk, tknv)
                for qt in range(NQT):
                    tqs2 = None
                    for i in range(12):
                        tqs2 = kb.mark(Y.dma_start(out=qas[:, i * 128:(i + 1) * 128],
                                                   in_=d["qa_spill"][i * 128:(i + 1) * 128,
                                                                     qt * 128:(qt + 1) * 128]),
                                       'DY')
                    bn_, bp_ = 2, 2
                    kb.bank_wait(bn_)
                    kb.wait('T', tqs2); kb.wait('T', tqb2)
                    for i in range(12):
                        inst = T.matmul(kb.bank(bn_)[:, 0:128],
                                        qbs[:, i * 192:i * 192 + 128],
                                        qas[:, i * 128:(i + 1) * 128],
                                        start=(i == 0), stop=(i == 11))
                    tq1 = kb.mark(inst, 'T')
                    for i in range(12):
                        inst = T.matmul(kb.bank(bp_)[0:64, 128:256],
                                        qbs[:, i * 192 + 128:(i + 1) * 192],
                                        qas[:, i * 128:(i + 1) * 128],
                                        start=(i == 0), stop=(i == 11))
                    tq2 = kb.mark(inst, 'T')
                    kb.wait('V', tq2); kb.wait('V', trq2)
                    kb.vsw(V.tensor_tensor(qn2[:], kb.bank(bn_)[:, 0:128],
                                           rqab[:, qt * 128:(qt + 1) * 128], op=ALU.mult))
                    tqn = kb.vsw(V.tensor_tensor(qp2[:], kb.bank(bp_)[0:64, 128:256],
                                                 rqab[0:64, qt * 128:(qt + 1) * 128],
                                                 op=ALU.mult))
                    kb.bank_set_free(bn_, tqn)
                    kb.vsw(V.tensor_copy(qpr[0:32, :], qp2[32:64, :]))
                    kb.vsw(V.tensor_scalar(qpr[0:32, :], qpr[0:32, :], -1.0, None,
                                           op0=ALU.mult))
                    kb.vsw(V.tensor_copy(qpr[32:64, :], qp2[0:32, :]))
                    kb.vsw(V.tensor_tensor(qpr[:], qpr[:],
                                           sinb[:, qt * 128:(qt + 1) * 128], op=ALU.mult))
                    kb.vsw(V.tensor_tensor(qp2[:], qp2[:],
                                           cosb[:, qt * 128:(qt + 1) * 128], op=ALU.mult))
                    tqp = kb.vsw(V.tensor_tensor(qp2[:], qp2[:], qpr[:], op=ALU.add))
                    for kbi in range(NBLK):
                        mbuf = kbi % 2
                        kb.wait('Y', mfree[mbuf])
                        tml = kb.mark(Y.dma_start(out=mb2[mbuf][:],
                                                  in_=d["l1mask"][qt, kbi, :, :]), 'DY')
                        sb_ = 3
                        kb.bank_wait(sb_)
                        kb.wait('T', tqp)
                        T.matmul(kb.bank(sb_)[:, 0:512], qn2[:],
                                 kn[:, kbi * 512:(kbi + 1) * 512], start=True, stop=False)
                        inst = T.matmul(kb.bank(sb_)[:, 0:512], qp2[:],
                                        kpf[:, kbi * 512:(kbi + 1) * 512],
                                        start=False, stop=True)
                        tmm = kb.mark(inst, 'T')
                        kb.wait('V', tmm); kb.wait('V', tml)
                        tse = kb.mark(V.tensor_tensor(s_sb[:, kbi * 512:(kbi + 1) * 512],
                                                      kb.bank(sb_)[:, 0:512],
                                                      mb2[mbuf][:], op=ALU.add), 'V')
                        kb.bank_set_free(sb_, tse)
                        mfree[mbuf] = tse
                        tse2 = kb.mark(V.reduce_max(mx8[:, kbi:kbi + 1],
                                                    s_sb[:, kbi * 512:(kbi + 1) * 512],
                                                    axis=AX.X), 'V')
                    kb.wait('V', tse2)
                    kb.vsw(V.reduce_max(stat[:, 0:1], mx8[:, 0:8], axis=AX.X))
                    tbn = kb.vsw(V.tensor_scalar(stat[:, 1:2], stat[:, 0:1], -SCALE1,
                                                 None, op0=ALU.mult))
                    teA = None
                    for kbi in range(NBLK):
                        kb.wait('S', tbn)
                        kb.wait('S', tT_prev)
                        teA = kb.mark(S.activation(psb2[:], s_sb[:, kbi * 512:(kbi + 1) * 512],
                                                   AF.Exp, bias=stat[:, 1:2], scale=SCALE1,
                                                   accum_out=z8[:, kbi:kbi + 1]), 'S')
                    kb.wait('V', teA)
                    kb.vsw(V.reduce_sum(stat[:, 2:3], z8[:, 0:8], axis=AX.X))
                    tvz = kb.tok('V')
                    kb.wait('S', tvz)
                    tlz = kb.mark(S.activation(stat[:, 3:4], stat[:, 2:3], AF.Ln), 'S')
                    kb.wait('V', tlz)
                    tb2 = kb.vsw(V.tensor_tensor(stat[:, 1:2], stat[:, 1:2], stat[:, 3:4],
                                                 op=ALU.subtract))
                    ob = 5
                    for kbi in range(NBLK):
                        kb.wait('S', tb2)
                        kb.wait('S', tT_prev)
                        teB = kb.mark(S.activation(psb2[:], s_sb[:, kbi * 512:(kbi + 1) * 512],
                                                   AF.Exp, bias=stat[:, 1:2], scale=SCALE1), 'S')
                        tb_ = 4
                        kb.bank_wait(tb_)
                        kb.wait('T', teB)
                        for j in range(4):
                            inst = T.transpose(kb.bank(tb_)[:, j * 128:(j + 1) * 128],
                                               psb2[:, j * 128:(j + 1) * 128], ident[:])
                        tT = kb.mark(inst, 'T')
                        tT_prev = tT
                        kb.wait('S', tT)
                        kb.wait('S', tpv_prev)
                        tpt = kb.mark(S.copy(ptb2[:], kb.bank(tb_)[:, 0:512]), 'S')
                        kb.bank_set_free(tb_, tpt)
                        if kbi == 0:
                            kb.bank_wait(ob)
                        kb.wait('T', tpt)
                        for j in range(4):
                            inst = T.matmul(kb.bank(ob)[:, 0:128],
                                            vt[:, (kbi * 4 + j) * 128:(kbi * 4 + j + 1) * 128],
                                            ptb2[:, j * 128:(j + 1) * 128],
                                            start=(kbi == 0 and j == 0),
                                            stop=(kbi == NBLK - 1 and j == 3))
                        tpv = kb.mark(inst, 'T')
                        tpv_prev = tpv
                    kb.wait('V', tpv)
                    tev = kb.mark(V.tensor_copy(ab4[h % 2][:, qt * 128:(qt + 1) * 128],
                                                kb.bank(ob)[:, 0:128]), 'V')
                    kb.bank_set_free(ob, tev)
                tqb_free = tpv
                if h % 2 == 1:
                    hb = h // 2
                    wo_free = None
                    for ot in range(NFT):
                        kb.wait('Y', wo_free)
                        towl = None
                        for qi in range(2):
                            gg = hb * 2 + qi
                            towl = kb.mark(Y.dma_start(
                                out=owb[:, qi * 128:(qi + 1) * 128],
                                in_=d["owT"][gg * 128:(gg + 1) * 128,
                                             ot * 128:(ot + 1) * 128]), 'DY')
                        for qc in range(2):
                            bnk = 6 + (qc % 2)
                            kb.bank_wait(bnk)
                            kb.wait('T', towl); kb.wait('T', tev)
                            for qi in range(2):
                                inst = T.matmul(kb.bank(bnk)[:, 0:512],
                                                owb[:, qi * 128:(qi + 1) * 128],
                                                ab4[qi][:, qc * 512:(qc + 1) * 512],
                                                start=(qi == 0), stop=(qi == 1))
                            tmm = kb.mark(inst, 'T')
                            wo_free = tmm
                            kb.wait('V', tmm)
                            tva = kb.mark(V.tensor_tensor(XTc(xsum, ot, qc * 512, 512),
                                                          XTc(xsum, ot, qc * 512, 512),
                                                          kb.bank(bnk)[:, 0:512],
                                                          op=ALU.add), 'V')
                            kb.bank_set_free(bnk, tva)
            tap("xsum_a1", xsum[:, 0:TPC], tva)
            kb.wait('Y', tva)
            for i in range(NFT):
                kb.mark(Y.dma_start(out=d["xsum_spill"][i * 128:(i + 1) * 128, :],
                                    in_=XT(xsum, i)), 'DY')

        pD = ExitStack()
        with pD:
            TC = 256
            htc = xsum
            gt = xsum[:, 11264:11264 + TC]
            ut = xsum[:, 11264 + TC:11264 + 2 * TC]
            ot_ = xsum[:, 11264 + 2 * TC:11264 + 3 * TC]
            xqc = xsum[:, 12288:12288 + NFT * TC]
            htm = kb.sb("htm", [128, 2 * DFF], pD)
            hint = kb.sb("hint", [128, DFF], pD, dtype=I32)
            xsc = kb.sb("xsc", [128, NFT * TC], pD)
            qwb = [kb.sb(f"qwb{i}", [128, NFT * 128], pD) for i in range(2)]
            s1b = kb.sb("s1b", [128, 4], pD)
            kb.barrier()
            qw_free = [None, None]
            tq2f = None
            tod = [None]
            tglast = [None, None]
            thj_prev = [None]
            for tc in range(4):
                c0 = tc * TC
                txl = None
                for i in range(NFT):
                    txl = kb.mark(Y.dma_start(out=XTc(xqc, i, 0, TC, TC),
                                              in_=d["xq_spill"][i * 128:(i + 1) * 128,
                                                                c0:c0 + TC]), 'DY')
                for j in range(NDT):
                    if DBG_NORM and 'PDBAR' in DBG_NORM:
                        kb.barrier()
                    for (jj, dst, col) in ((j, gt, 0), (j + NDT, ut, 1)):
                        buf = jj % 2
                        kb.wait('Y', qw_free[buf])
                        twl = kb.mark(Y.dma_start(
                            out=qwb[buf][:].rearrange("p (a b) -> p a b", b=128),
                            in_=d["qw1"][jj, :, :].rearrange("(a p) b -> p a b", p=128)), 'DY')
                        kb.wait('Y', tglast[col])
                        tsl = kb.mark(Y.dma_start(out=s1b[:, col:col + 1],
                                                  in_=d["s1"][jj, :].rearrange(
                                                      "(p o) -> p o", o=1)), 'DY')
                        bnk = 0 + col
                        kb.bank_wait(bnk)
                        kb.wait('T', twl); kb.wait('T', txl)
                        for i in range(NFT):
                            inst = T.matmul(kb.bank(bnk)[:, 0:TC],
                                            qwb[buf][:, i * 128:(i + 1) * 128],
                                            XTc(xqc, i, 0, TC, TC),
                                            start=(i == 0), stop=(i == NFT - 1))
                        tmm = kb.mark(inst, 'T')
                        qw_free[buf] = tmm
                        if col == 0:
                            kb.wait('S', tmm); kb.wait('S', tsl)
                            kb.wait('S', thj_prev[0])
                            tgv = kb.mark(S.activation(dst, kb.bank(bnk)[:, 0:TC],
                                                       AF.Silu, scale=s1b[:, 0:1]), 'S')
                        else:
                            kb.wait('V', tmm); kb.wait('V', tsl)
                            tgv = kb.mark(V.tensor_scalar(dst, kb.bank(bnk)[:, 0:TC],
                                                          s1b[:, 1:2], None, op0=ALU.mult),
                                          'V')
                        kb.bank_set_free(bnk, tgv)
                        tglast[col] = tgv
                        if col == 0:
                            tg_s = tgv
                    kb.wait('V', tgv)
                    kb.wait('V', tg_s)
                    thj = kb.mark(V.tensor_tensor(htc[:, j * TC:(j + 1) * TC], gt, ut,
                                                  op=ALU.mult), 'V')
                    thj_prev[0] = thj
                kb.wait('T', thj)
                for tt in range(2):
                    for j4 in range(11):
                        tb = 2
                        kb.bank_wait(tb)
                        for j in range(4):
                            jj = j4 * 4 + j
                            inst = T.transpose(kb.bank(tb)[:, j * 128:(j + 1) * 128],
                                               htc[:, jj * TC + tt * 128:
                                                   jj * TC + (tt + 1) * 128], ident[:])
                        tTx = kb.mark(inst, 'T')
                        kb.wait('V', tTx)
                        tvv = kb.mark(V.tensor_copy(htm[:, tt * DFF + j4 * 512:
                                                        tt * DFF + (j4 + 1) * 512],
                                                    kb.bank(tb)[:, 0:512]), 'V')
                        kb.bank_set_free(tb, tvv)
                kb.wait('V', tvv)
                for tt in range(2):
                    if DBG_NORM and 'PDBAR' in DBG_NORM:
                        kb.barrier()
                    tm = htm[:, tt * DFF:(tt + 1) * DFF]
                    kb.vsw(V.tensor_reduce(stat[:, 0:1], tm, axis=AX.X, op=ALU.max,
                                           apply_absolute_value=True))
                    kb.vsw(V.tensor_scalar(stat[:, 0:1], stat[:, 0:1], 1.0 / 127.0, 1e-8,
                                           op0=ALU.mult, op1=ALU.max))
                    kb.vsw(V.reciprocal(stat[:, 1:2], stat[:, 0:1]))
                    kb.vsw(V.tensor_scalar(tm, tm, stat[:, 1:2], None, op0=ALU.mult))
                    kb.vsw(V.tensor_copy(hint[:], tm))
                    kb.vsw(V.tensor_copy(tm, hint[:]))
                    tql = kb.vsw(V.tensor_scalar(tm, tm, stat[:, 0:1], None, op0=ALU.mult))
                    kb.wait('T', tql)
                    for j4 in range(11):
                        tb = 3
                        kb.bank_wait(tb)
                        for j in range(4):
                            inst = T.transpose(kb.bank(tb)[:, j * 128:(j + 1) * 128],
                                               tm[:, (j4 * 4 + j) * 128:(j4 * 4 + j + 1) * 128],
                                               ident[:])
                        tTx = kb.mark(inst, 'T')
                        kb.wait('V', tTx)
                        for j in range(4):
                            jj = j4 * 4 + j
                            tvq = kb.mark(V.tensor_copy(
                                htc[:, jj * TC + tt * 128: jj * TC + (tt + 1) * 128],
                                kb.bank(tb)[:, j * 128:(j + 1) * 128]), 'V')
                        kb.bank_set_free(tb, tvq)
                txs = None
                for i in range(NFT):
                    txs = kb.mark(Y.dma_start(out=XTc(xsc, i, 0, TC, TC),
                                              in_=d["xsum_spill"][i * 128:(i + 1) * 128,
                                                                  c0:c0 + TC]), 'DY')
                for ot in range(NFT):
                    if DBG_NORM and 'PDBAR' in DBG_NORM:
                        kb.barrier()
                    kb.wait('Y', tod[0])
                    tsl = kb.mark(Y.dma_start(out=s1b[:, 2:3],
                                              in_=d["s2"][ot, :].rearrange(
                                                  "(p o) -> p o", o=1)), 'DY')
                    bnk = 0
                    kb.bank_wait(bnk)
                    kb.wait('T', tvq)
                    for kg in range(3):
                        k0 = kg * 16
                        kn_ = min(16, NDT - k0)
                        buf2 = kg % 2
                        kb.wait('Y', qw_free[buf2])
                        tw2 = kb.mark(Y.dma_start(
                            out=qwb[buf2][:, 0:kn_ * 128].rearrange(
                                "p (a b) -> p a b", b=128),
                            in_=d["qw2"][ot, k0 * 128:(k0 + kn_) * 128, :].rearrange(
                                "(a p) b -> p a b", p=128)), 'DY')
                        kb.wait('T', tw2)
                        for kt in range(kn_):
                            inst = T.matmul(kb.bank(bnk)[:, 0:TC],
                                            qwb[buf2][:, kt * 128:(kt + 1) * 128],
                                            htc[:, (k0 + kt) * TC:(k0 + kt + 1) * TC],
                                            start=(k0 + kt == 0),
                                            stop=(k0 + kt == NDT - 1))
                        qw_free[buf2] = kb.mark(inst, 'T')
                    tmm = qw_free[0]
                    tq2f = tmm
                    kb.wait('V', tmm); kb.wait('V', tsl); kb.wait('V', txs)
                    kb.wait('V', tod[0])
                    kb.vsw(V.tensor_scalar(ot_, kb.bank(bnk)[:, 0:TC], s1b[:, 2:3],
                                           None, op0=ALU.mult))
                    tfo = kb.mark(V.tensor_tensor(ot_, ot_, XTc(xsc, ot, 0, TC, TC),
                                                  op=ALU.add), 'V')
                    kb.bank_set_free(bnk, tfo)
                    kb.wait('Y', tfo)
                    tod[0] = kb.mark(Y.dma_start(
                        out=d["outT"][ot * 128:(ot + 1) * 128, c0:c0 + TC],
                        in_=ot_), 'DY')


def _host_prep(inputs):
    f32 = np.float32
    x = np.asarray(inputs["L0_attn_hidden_states"], f32)

    shared = {}
    tr = lambda a: np.ascontiguousarray(np.asarray(a, f32).T)
    shared["wkT"] = tr(inputs["L0_attn_w_k"])
    shared["wvT"] = tr(inputs["L0_attn_w_v"])
    shared["wlmkT"] = tr(inputs["L0_attn_w_lmk"])
    shared["wqrT"] = tr(inputs["L0_attn_w_q_retr"])
    shared["wqT"] = tr(inputs["L0_attn_w_q"])
    shared["woT"] = tr(inputs["L0_attn_w_o"])
    for nm, key in (("g_pre", "L0_attn_g_pre"), ("g_chunk", "L0_attn_g_chunk_norm"),
                    ("g_pre_retr", "L0_attn_g_pre_retr"), ("g_q_retr", "L0_attn_g_q_retr"),
                    ("g_lmk", "L0_attn_g_lmk"), ("g_k", "L0_attn_g_k_norm"),
                    ("g_qn", "L0_attn_g_q_norm")):
        shared[nm] = np.ascontiguousarray(np.asarray(inputs[key], f32))
    shared["gateT"] = tr(inputs["L0_ffn_gate_w"])
    shared["RgT"] = tr(inputs["L0_ffn_Rg"])
    shared["RuT"] = tr(inputs["L0_ffn_Ru"])
    shared["RdT"] = tr(inputs["L0_ffn_Rd"])
    for nm, key in (("UgT", "L0_ffn_Ug"), ("UuT", "L0_ffn_Uu"), ("UdT", "L0_ffn_Ud"),
                    ("CgT", "L0_ffn_Cg"), ("CuT", "L0_ffn_Cu"), ("CdT", "L0_ffn_Cd")):
        shared[nm] = np.ascontiguousarray(np.asarray(inputs[key], f32).transpose(0, 2, 1))
    shared["sRgT"] = tr(inputs["L0_ffn_s_Rg"])
    shared["sRuT"] = tr(inputs["L0_ffn_s_Ru"])
    shared["sRdT"] = tr(inputs["L0_ffn_s_Rd"])
    shared["sUgT"] = tr(inputs["L0_ffn_s_Ug"])
    shared["sUuT"] = tr(inputs["L0_ffn_s_Uu"])
    shared["sUdT"] = tr(inputs["L0_ffn_s_Ud"])
    shared["sCgT"] = tr(inputs["L0_ffn_s_Cg"])
    shared["sCuT"] = tr(inputs["L0_ffn_s_Cu"])
    shared["sCdT"] = tr(inputs["L0_ffn_s_Cd"])
    shared["qaT"] = tr(inputs["L1_attn_q_a_w"])
    shared["qbT"] = tr(inputs["L1_attn_q_b_w"])
    shared["kvaT"] = tr(inputs["L1_attn_kv_a_w"])
    shared["kvbT"] = tr(inputs["L1_attn_kv_b_w"])
    shared["owT"] = tr(inputs["L1_attn_o_w"])
    shared["qa_ln"] = np.ascontiguousarray(np.asarray(inputs["L1_attn_q_a_ln_w"], f32))
    shared["kva_ln"] = np.ascontiguousarray(np.asarray(inputs["L1_attn_kv_a_ln_w"], f32))
    shared["w1"] = np.ascontiguousarray(np.asarray(inputs["L1_ffn_w_gate_up"], f32))
    shared["w2"] = np.ascontiguousarray(np.asarray(inputs["L1_ffn_w_down"], f32))
    shared["ident"] = np.eye(128, dtype=f32)

    inv = 1.0 / (10000.0 ** (np.arange(0, ROPE, 2, dtype=np.float64) / ROPE))
    t = np.arange(LSEQ, dtype=np.float64)
    fr = np.outer(t, inv)
    emb = np.concatenate([fr, fr], -1)
    cos = np.cos(emb).astype(f32)
    sin = np.sin(emb).astype(f32)

    in_maps = []
    for c in range(NCORES):
        seq = c // 4
        a = c % 4
        b = 7 - a
        rowsA = slice(512 * a, 512 * a + 512)
        rowsB = slice(512 * b, 512 * b + 512)
        xo = np.concatenate([x[seq][rowsA], x[seq][rowsB]], 0)
        m = dict(shared)
        m["x0T"] = np.ascontiguousarray(xo.T)
        m["cosT"] = np.ascontiguousarray(np.concatenate([cos[rowsA], cos[rowsB]], 0).T)
        m["sinT"] = np.ascontiguousarray(np.concatenate([sin[rowsA], sin[rowsB]], 0).T)
        gtiles = [4 * a + j for j in range(4)] + [4 * b + j for j in range(4)]
        l0m = np.zeros((NQT, 128, NCH), f32)
        l1m = np.zeros((NQT, NBLK, 128, 512), f32)
        for qt, g in enumerate(gtiles):
            pos = 128 * g + np.arange(128)
            ends = (np.arange(NCH) + 1) * SCH
            l0m[qt] = np.where(pos[:, None] >= ends[None, :], 0.0, NEG)
            for kbi in range(NBLK):
                kpos = 512 * kbi + np.arange(512)
                l1m[qt, kbi] = np.where(pos[:, None] >= kpos[None, :], 0.0, NEG)
        m["l0mask"] = l0m
        m["l1mask"] = l1m
        in_maps.append(m)
    return in_maps


def kernel(**inputs):
    in_maps = _host_prep(inputs)
    nc = build_program()
    res = run_bass_kernel_spmd(nc, in_maps, core_ids=list(range(NCORES)))
    out = np.zeros((BB, LSEQ, DM), np.float32)
    for c in range(NCORES):
        seq = c // 4
        a = c % 4
        b = 7 - a
        oT = res.results[c]["outT"]
        out[seq, 512 * a:512 * a + 512] = oT[:, :512].T
        out[seq, 512 * b:512 * b + 512] = oT[:, 512:].T
    return out
